# version 03
# baseline (speedup 1.0000x reference)
import numpy as np
from contextlib import ExitStack

import concourse.bass as bass
import concourse.mybir as mybir
from concourse.bass_utils import run_bass_kernel_spmd

F32 = mybir.dt.float32
BF16 = mybir.dt.bfloat16
AF = mybir.ActivationFunctionType
ALU = mybir.AluOpType
AX = mybir.AxisListType

D = 1024
CH = 8
TOK = 2048
SEQ = 4096
NH = 16
NPAIR = 8
NE = 8
FCH = 128

LOG2 = float(np.log(2.0))

DEBUG_TAPS = {}
EMIT_UPTO = 2
STOP_AFTER = ""
PARANOID = 0
NO_BOUNCE = False
LAST_RESULTS = None

R_A = 0
R_B = 32
R_V = 64
R_M = 96


class Sem:
    def __init__(self, ctx, nc, name):
        self.h = ctx.enter_context(nc.semaphore(name))
        self.n = 0

    def inc(self, instr, k=1):
        instr.then_inc(self.h, k)
        self.n += k
        return instr

    def wait(self, eng, n=None):
        eng.wait_ge(self.h, self.n if n is None else n)


class Rel:
    def __init__(self, sem, n):
        self.sem = sem
        self.n = n

    def wait(self, eng):
        eng.wait_ge(self.sem.h, self.n)


class P:
    def take_slot_sems(self, n):
        out = self.slot_sems[self._slot_next:self._slot_next + n]
        assert len(out) == n, "out of slot sems"
        self._slot_next += n
        return out

    def release_slot_sems(self, n):
        self._slot_next -= n

    def un(self, base):
        self._uc = getattr(self, "_uc", 0) + 1
        return f"{base}_{self._uc}"


def build_program(taps=None, emit_upto=2):
    nc = bass.Bass("TRN2", target_bir_lowering=False, debug=False)
    st = ExitStack()
    p = P()
    p.nc = nc
    p.st = st
    p.taps = dict(taps or {})

    dram = {}

    def din(name, shape, dtype=F32):
        dram[name] = nc.dram_tensor(name, shape, dtype, kind="ExternalInput").ap()

    def dint(name, shape, dtype=F32):
        dram[name] = nc.dram_tensor(name, shape, dtype, kind="Internal").ap()

    def dout(name, shape, dtype=F32):
        dram[name] = nc.dram_tensor(name, shape, dtype, kind="ExternalOutput").ap()

    din("xT", [128, CH * TOK])
    for nm in ("w0qT", "w0kT", "w0vT", "w0oT", "w1qT", "w1kT", "w1vT", "w1postT"):
        din(nm, [128, NPAIR * D])
    for nm in ("b0q", "b0k", "b0v", "b0o", "gam", "bet"):
        din(nm, [128, 8])
    din("mu", [1, 1])
    din("railsY", [128, 32 * 128])
    din("railsZ", [128, 128 * 128])
    din("cprojT", [128, FCH * D])
    din("gateT", [128, CH * NE])
    din("wgT", [NE, 128, CH * 512])
    din("wuT", [NE, 128, CH * 512])
    din("wdT", [NE, 128, 4 * D])
    din("mlo", [128, 1])
    din("mhi", [128, 1])
    din("eye", [128, 128])
    dint("x_dram", [128, CH * TOK])
    dint("cc_in0", [128, CH * TOK], BF16)
    dint("cc_out0", [256, CH * TOK], BF16)
    dint("cc_in1", [128, CH * TOK], BF16)
    dint("cc_out1", [256, CH * TOK], BF16)
    dint("cprojT_bf", [128, FCH * D], BF16)
    dout("outT", [128, CH * TOK])
    p.dram = dram

    p.tap_out = {}
    for tname, shape in p.taps.items():
        dout("tap_" + tname, list(shape), F32)
        p.tap_out[tname] = dram["tap_" + tname]

    p.dS = Sem(st, nc, "dS")
    p.dG = Sem(st, nc, "dG")
    p.sT = Sem(st, nc, "sT")
    p.sA = Sem(st, nc, "sA")
    p.sV = Sem(st, nc, "sV")
    p.sC = Sem(st, nc, "sC")
    p.dO = Sem(st, nc, "dO")
    p.slot_sems = [Sem(st, nc, f"sl{i}") for i in range(12)]
    p._slot_next = 0

    def wall(eng, *sems):
        for s_ in sems:
            s_.wait(eng)
    p.wall = wall

    sb = st.enter_context
    p.XN = sb(nc.sbuf_tensor("XN", [128, CH * SEQ], BF16))
    p.G = sb(nc.sbuf_tensor("G", [128, CH * TOK], BF16))
    p.TMP = sb(nc.sbuf_tensor("TMP", [128, TOK], F32))
    p.SR = sb(nc.sbuf_tensor("SR", [128, TOK], F32))
    p.c_ones_b = sb(nc.sbuf_tensor("c_ones_b", [128, 1], BF16))
    p.c_onesr_f = sb(nc.sbuf_tensor("c_onesr_f", [128, 128], F32))
    p.c_ln2 = sb(nc.sbuf_tensor("c_ln2", [128, 1], F32))
    p.c_mu2 = sb(nc.sbuf_tensor("c_mu2", [128, 1], F32))
    p.c_zero = sb(nc.sbuf_tensor("c_zero", [128, 1], F32))
    p.c_eps1 = sb(nc.sbuf_tensor("c_eps1", [128, 1], F32))
    p.c_eps2 = sb(nc.sbuf_tensor("c_eps2", [128, 1], F32))
    p.b0q = sb(nc.sbuf_tensor("b0q_s", [128, 8], F32))
    p.b0k = sb(nc.sbuf_tensor("b0k_s", [128, 8], F32))
    p.b0v = sb(nc.sbuf_tensor("b0v_s", [128, 8], F32))
    p.b0o = sb(nc.sbuf_tensor("b0o_s", [128, 8], F32))
    p.gam = sb(nc.sbuf_tensor("gam_s", [128, 8], F32))
    p.bet = sb(nc.sbuf_tensor("bet_s", [128, 8], F32))
    p.mlo = sb(nc.sbuf_tensor("mlo_s", [128, 1], F32))
    p.mhi = sb(nc.sbuf_tensor("mhi_s", [128, 1], F32))
    p.eyeb = sb(nc.sbuf_tensor("eyeb", [128, 128], BF16))
    p.eyef = sb(nc.sbuf_tensor("eyef", [128, 128], F32))

    prologue(p)
    try:
        for lid in range(min(2, emit_upto)):
            layer(p, lid)
    except StopEmission:
        pass
    finale(p)

    st.close()
    return nc


def tap(p, name, src_ap, sems):
    if name not in p.tap_out:
        return
    nc = p.nc
    eng = nc.gpsimd if src_ap.dtype != F32 else nc.sync
    p.wall(eng, *sems)
    p.dO.inc(eng.dma_start(out=p.tap_out[name][:], in_=src_ap), 16)


def prologue(p):
    nc = p.nc
    v, s, g, a, t = nc.vector, nc.sync, nc.gpsimd, nc.scalar, nc.tensor
    d = p.dram
    v.memset(p.c_ones_b[:], 1.0)
    v.memset(p.c_onesr_f[:], 1.0)
    v.memset(p.c_ln2[:], -LOG2)
    v.memset(p.c_zero[:], 0.0)
    v.memset(p.c_eps1[:], 1e-5)
    v.memset(p.c_eps2[:], 1e-6)
    p.sV.inc(v.engine_nop())
    for nm, t_sb in (("b0q", p.b0q), ("b0k", p.b0k), ("b0v", p.b0v), ("b0o", p.b0o),
                     ("gam", p.gam), ("bet", p.bet), ("mlo", p.mlo), ("mhi", p.mhi),
                     ("eye", p.eyef)):
        p.dS.inc(s.dma_start(out=t_sb[:], in_=d[nm][:]), 16)
    p.dG.inc(g.dma_start(out=p.eyeb[:], in_=d["eye"][:]), 16)
    mut = p.st.enter_context(nc.sbuf_tensor("mut", [1, 1], F32))
    with nc.psum_tensor("pmu", [128, 1], F32) as pmu:
        p.dS.inc(s.dma_start(out=mut[:], in_=d["mu"][:]), 16)
        p.wall(t, p.dS, p.sV)
        p.sT.inc(t.matmul(pmu[:], p.c_onesr_f[0:1, :], mut[:], start=True, stop=True))
        p.wall(a, p.sT, p.sV)
        p.sA.inc(a.activation(p.c_mu2[:], pmu[:], AF.Copy, bias=0.0, scale=-0.5))
        a.drain()


def dma_fence(p):
    nc = p.nc
    p.wall(nc.sync, p.sT, p.sA, p.sV)
    p.wall(nc.gpsimd, p.sT, p.sA, p.sV)


def bcast_row(p, psdst, row_ap, width):
    t = p.nc.tensor
    rb = row_ap.base_partition()
    for nb in range(width // 512):
        p.sT.inc(t.matmul(psdst[:, nb * 512:(nb + 1) * 512],
                          p.c_onesr_f[rb:rb + 1, :],
                          row_ap[0:1, nb * 512:(nb + 1) * 512],
                          start=True, stop=True, skip_group_check=True))


def ln_phase(p, layer_id, inner):
    nc = p.nc
    v, s, a, t = nc.vector, nc.sync, nc.scalar, nc.tensor
    nhalf = 2 if inner else 1
    src = p.dram["xT"] if layer_id == 0 else p.dram["x_dram"]

    p.wall(v, p.dS, p.sT)
    p.wall(s, p.dO)
    dma_fence(p)
    with ExitStack() as ph:
        SQ = ph.enter_context(nc.sbuf_tensor(p.un("lnsq"), [128, TOK], BF16))
        XB = None
        if not inner:
            XB = ph.enter_context(nc.sbuf_tensor(p.un("lnxb"), [128, 2 * TOK], F32))
        for half in range(nhalf):
            hs = half * TOK
            pstat_cm = nc.psum_tensor(p.un("pstat"), [64, TOK], F32)
            pstat = pstat_cm.__enter__()
            for c in range(CH):
                if not inner:
                    xsl = XB[:, (c % 2) * TOK:(c % 2 + 1) * TOK]
                    p.wall(s, p.sV)
                    p.dS.inc(s.dma_start(out=xsl, in_=src[:, c * TOK:(c + 1) * TOK]), 16)
                    p.wall(v, p.dS)
                    xb = p.XN[:, c * SEQ + TOK: c * SEQ + 2 * TOK]
                    p.sV.inc(v.tensor_copy(xb, xsl))
                else:
                    xb = p.XN[:, c * SEQ + hs: c * SEQ + hs + TOK]
                p.wall(v, p.sT)
                p.sV.inc(v.tensor_mul(SQ[:], xb, xb))
                p.wall(t, p.sV)
                for nb in range(4):
                    n0 = nb * 512
                    p.sT.inc(t.matmul(pstat[0:1, n0:n0 + 512], p.c_ones_b[:],
                                      xb[:, n0:n0 + 512], start=(c == 0),
                                      stop=(c == CH - 1), skip_group_check=True))
                    p.sT.inc(t.matmul(pstat[32:33, n0:n0 + 512], p.c_ones_b[:],
                                      SQ[:, n0:n0 + 512], start=(c == 0),
                                      stop=(c == CH - 1), skip_group_check=True))
            p.wall(v, p.sT)
            mrow = p.SR[R_M:R_M + 1, :]
            vrow = p.SR[R_V:R_V + 1, :]
            arow = p.SR[R_A:R_A + 1, :]
            brow = p.SR[R_B:R_B + 1, :]
            p.sV.inc(v.tensor_scalar_mul(mrow, pstat[0:1, :], 1.0 / D))
            p.sV.inc(v.tensor_mul(vrow, mrow, mrow))
            p.sV.inc(v.scalar_tensor_tensor(vrow, pstat[32:33, :], 1.0 / D, vrow,
                                            ALU.mult, ALU.subtract))
            epsc = p.c_eps2 if inner else p.c_eps1
            p.wall(a, p.sV)
            p.sA.inc(a.activation(brow, vrow, AF.Ln, bias=epsc[0:1, 0:1], scale=1.0))
            a.drain()
            p.sA.inc(a.activation(arow, brow, AF.Exp, bias=p.c_zero[0:1, 0:1], scale=-0.5))
            p.wall(v, p.sA)
            p.sV.inc(v.scalar_tensor_tensor(brow, pstat[0:1, :], -1.0 / D, arow,
                                            ALU.mult, ALU.mult))
            pstat_cm.__exit__(None, None, None)
            pbca_cm = nc.psum_tensor(p.un("pbca"), [128, TOK], F32)
            pbcb_cm = nc.psum_tensor(p.un("pbcb"), [128, TOK], F32)
            pbca = pbca_cm.__enter__()
            pbcb = pbcb_cm.__enter__()
            p.wall(t, p.sV)
            bcast_row(p, pbca, arow, TOK)
            bcast_row(p, pbcb, brow, TOK)
            for c in range(CH):
                if inner:
                    xb = p.XN[:, c * SEQ + hs: c * SEQ + hs + TOK]
                    xout = xb
                else:
                    xb = p.XN[:, c * SEQ + TOK: c * SEQ + 2 * TOK]
                    xout = p.XN[:, c * SEQ: c * SEQ + TOK]
                p.wall(v, p.sT)
                p.sV.inc(v.tensor_tensor(p.TMP[:], xb, pbca[:], ALU.mult))
                if inner:
                    p.sV.inc(v.tensor_tensor(p.TMP[:], p.TMP[:], pbcb[:], ALU.add))
                    p.sV.inc(v.tensor_scalar(xout, p.TMP[:], p.gam[:, c:c + 1],
                                             p.bet[:, c:c + 1], ALU.mult, ALU.add))
                else:
                    p.sV.inc(v.tensor_tensor(xout, p.TMP[:], pbcb[:], ALU.add))
            p.wall(t, p.sV)
            pbcb_cm.__exit__(None, None, None)
            pbca_cm.__exit__(None, None, None)


def exchange_export(p, layer_id):
    nc = p.nc
    s, g = nc.sync, nc.gpsimd
    cc_in = p.dram[f"cc_in{layer_id}"]
    cc_out = p.dram[f"cc_out{layer_id}"]
    p.wall(s, p.sV)
    for c in range(CH):
        p.dS.inc(s.dma_start(out=cc_in[:, c * TOK:(c + 1) * TOK],
                             in_=p.XN[:, c * SEQ: c * SEQ + TOK]), 16)
    p.wall(g, p.dS)
    p.sC.inc(g.collective_compute(
        "AllGather", ALU.bypass, ins=[cc_in[:]], outs=[cc_out[:]],
        replica_groups=[[0, 1], [2, 3], [4, 5], [6, 7]]))


def exchange_import(p, layer_id):
    nc = p.nc
    s, v = nc.sync, nc.vector
    cc_out = p.dram[f"cc_out{layer_id}"]
    p.wall(s, p.sC)
    dma_fence(p)
    with nc.sbuf_tensor(p.un("imA"), [128, TOK], BF16) as imA, \
         nc.sbuf_tensor(p.un("imB"), [128, TOK], BF16) as imB:
        for c in range(CH):
            p.wall(s, p.sV)
            p.dS.inc(s.dma_start(out=imA[:], in_=cc_out[0:128, c * TOK:(c + 1) * TOK]), 16)
            p.dS.inc(s.dma_start(out=imB[:], in_=cc_out[128:256, c * TOK:(c + 1) * TOK]), 16)
            p.wall(v, p.dS, p.sT)
            xo = p.XN[:, c * SEQ + TOK: c * SEQ + 2 * TOK]
            p.sV.inc(v.tensor_scalar(p.TMP[:], imA[:], p.mlo[:, 0:1], None, ALU.mult))
            p.sV.inc(v.scalar_tensor_tensor(xo, imB[:], p.mhi[:, 0:1], p.TMP[:],
                                            ALU.mult, ALU.add))



def ffn0(p):
    nc = p.nc
    v, s, g, a, t = nc.vector, nc.sync, nc.gpsimd, nc.scalar, nc.tensor
    d = p.dram
    NTG = 8
    TW = 256

    with ExitStack() as ph:
        RTY = ph.enter_context(nc.sbuf_tensor(p.un("RTY"), [128, 32 * 128], BF16))
        RTZ = ph.enter_context(nc.sbuf_tensor(p.un("RTZ"), [128, 128 * 128], BF16))
        YB = ph.enter_context(nc.sbuf_tensor(p.un("YB"), [128, 32 * TW], BF16))
        H4 = ph.enter_context(nc.sbuf_tensor(p.un("H4"), [128, 4 * TW], BF16))
        WA = ph.enter_context(nc.sbuf_tensor(p.un("WA"), [128, 2 * D], BF16))
        GTOK = ph.enter_context(nc.sbuf_tensor(p.un("GTOK"), [128, 16 * D], BF16))

        dma_fence(p)
        p.dG.inc(g.dma_start(out=RTY[:], in_=d["railsY"][:]), 16)
        for q in range(4):
            p.dG.inc(g.dma_start(out=RTZ[:, q * 4096:(q + 1) * 4096],
                                 in_=d["railsZ"][:, q * 4096:(q + 1) * 4096]), 16)
        rails_rel = Rel(p.dG, p.dG.n)

        wa_rel = [None, None]
        wa_bounce = [None, None]
        wa_load = [None, None]
        ldsem = p.take_slot_sems(2)
        bnsem = p.take_slot_sems(2)

        def load_w(i, tg):
            slot = i % 2
            dst = WA[:, slot * D:(slot + 1) * D]
            if NO_BOUNCE:
                if wa_rel[slot] is not None:
                    wa_rel[slot].wait(g)
                ldsem[slot].inc(g.dma_start(out=dst,
                                            in_=d["cprojT"][:, i * D:(i + 1) * D]), 16)
                wa_load[slot] = Rel(ldsem[slot], ldsem[slot].n)
                return
            if tg == 0:
                if wa_rel[slot] is not None:
                    wa_rel[slot].wait(g)
                if wa_bounce[slot] is not None:
                    wa_bounce[slot].wait(g)
                ldsem[slot].inc(g.dma_start(out=dst,
                                            in_=d["cprojT"][:, i * D:(i + 1) * D]), 16)
                wa_load[slot] = Rel(ldsem[slot], ldsem[slot].n)
                wa_load[slot].wait(s)
                bnsem[slot].inc(s.dma_start(out=d["cprojT_bf"][:, i * D:(i + 1) * D],
                                            in_=dst), 16)
                p.dS.n += 0
                wa_bounce[slot] = Rel(bnsem[slot], bnsem[slot].n)
            else:
                if wa_rel[slot] is not None:
                    wa_rel[slot].wait(s)
                ldsem[slot].inc(s.dma_start(out=dst,
                                            in_=d["cprojT_bf"][:, i * D:(i + 1) * D]), 16)
                wa_load[slot] = Rel(ldsem[slot], ldsem[slot].n)

        with nc.psum_tensor(p.un("py"), [128, 512], F32) as py, \
             nc.psum_tensor(p.un("pz"), [128, 1024], F32) as pz, \
             nc.psum_tensor(p.un("pot"), [128, 2048], F32) as pot:
            for tg in range(NTG):
                t0 = tg * TW
                if tg == 0:
                    rails_rel.wait(t)
                    p.wall(t, p.sV)
                if tg == 1:
                    p.wall(s, bnsem[0], bnsem[1])
                load_w(0, tg)
                load_w(1, tg)
                for yt in range(32):
                    j = yt % 8
                    half = yt % 2
                    p.wall(t, p.sV)
                    p.sT.inc(t.matmul(py[:, half * TW:(half + 1) * TW],
                                      RTY[:, yt * 128:(yt + 1) * 128],
                                      p.XN[:, j * SEQ + t0: j * SEQ + t0 + TW],
                                      start=True, stop=True, skip_group_check=True))
                    if half == 1:
                        p.wall(v, p.sT)
                        p.sV.inc(v.tensor_copy(YB[:, (yt - 1) * TW:(yt + 1) * TW], py[:]))
                for ib in range(32):
                    for k in range(4):
                        i = 4 * ib + k
                        if PARANOID & 4:
                            p.wall(t, p.dS, p.dG)
                        if PARANOID & 8:
                            p.wall(t, p.sA, p.sV)
                        if k == 0:
                            p.wall(t, p.sA)
                            if ib == 0:
                                p.wall(t, p.sV)
                        p.sT.inc(t.matmul(pz[:, k * TW:(k + 1) * TW],
                                          RTZ[:, i * 128:(i + 1) * 128],
                                          YB[:, (i // 4) * TW:(i // 4 + 1) * TW],
                                          start=True, stop=True, skip_group_check=True))
                    p.wall(a, p.sT)
                    if PARANOID & 16:
                        p.wall(a, p.dS, p.dG, p.sV)
                    p.sA.inc(a.activation(H4[:], pz[:], AF.Gelu,
                                          bias=p.c_zero[:, 0:1], scale=1.0))
                    gelu_cnt = p.sA.n
                    for k in range(4):
                        i = 4 * ib + k
                        slot = i % 2
                        if PARANOID & 1:
                            p.wall(t, p.dS, p.dG)
                        if PARANOID & 2:
                            p.wall(t, p.sA, p.sV)
                        wa_load[slot].wait(t)
                        t.wait_ge(p.sA.h, gelu_cnt)
                        for tc in range(2):
                            hsl = H4[:, k * TW + tc * 128: k * TW + (tc + 1) * 128]
                            for ob in range(2):
                                p.sT.inc(t.matmul(
                                    pot[:, tc * 1024 + ob * 512: tc * 1024 + (ob + 1) * 512],
                                    hsl,
                                    WA[:, slot * D + ob * 512: slot * D + (ob + 1) * 512],
                                    start=(i == 0), stop=(i == 127),
                                    skip_group_check=True))
                        wa_rel[slot] = Rel(p.sT, p.sT.n)
                        if i + 2 < 128:
                            load_w(i + 2, tg)
                p.wall(v, p.sT)
                for tc in range(2):
                    p.sV.inc(v.tensor_copy(
                        GTOK[:, (2 * tg + tc) * D:(2 * tg + tc + 1) * D],
                        pot[:, tc * 1024:(tc + 1) * 1024]))
                p.wall(t, p.sV)

        tap(p, "yb", YB[:], (p.sV,))
        tap(p, "gtok", GTOK[:], (p.sV,))
        p.release_slot_sems(4)
        with nc.psum_tensor(p.un("pgt"), [128, 2048], BF16) as pgt:
            for fc in range(CH):
                p.wall(t, p.sA, p.sV)
                for tc in range(16):
                    p.sT.inc(t.transpose(pgt[:, tc * 128:(tc + 1) * 128],
                                         GTOK[:, tc * D + fc * 128: tc * D + (fc + 1) * 128],
                                         p.eyeb[:]))
                p.wall(a, p.sT)
                p.sA.inc(a.activation(p.G[:, fc * TOK:(fc + 1) * TOK], pgt[:],
                                      AF.Silu, bias=p.c_zero[:, 0:1], scale=1.0))



def attention(p, layer_id):
    nc = p.nc
    v, s, g, a, t = nc.vector, nc.sync, nc.gpsimd, nc.scalar, nc.tensor
    d = p.dram
    l0 = (layer_id == 0)
    wq = d["w0qT"] if l0 else d["w1qT"]
    wk = d["w0kT"] if l0 else d["w1kT"]
    wv = d["w0vT"] if l0 else d["w1vT"]
    wo = d["w0oT"] if l0 else d["w1postT"]

    with ExitStack() as ph:
        A_sb = ph.enter_context(nc.sbuf_tensor(p.un("A_sb"), [128, NPAIR * TOK], BF16))
        with ExitStack() as core:
            K_sb = core.enter_context(nc.sbuf_tensor(p.un("K_sb"), [128, SEQ], BF16))
            VT_sb = core.enter_context(nc.sbuf_tensor(p.un("VT_sb"), [128, SEQ], BF16))
            Vsb = core.enter_context(nc.sbuf_tensor(p.un("Vsb"), [128, 32 * 130], BF16))
            Q_sb = core.enter_context(nc.sbuf_tensor(p.un("Q_sb"), [128, TOK], BF16))
            U = core.enter_context(nc.sbuf_tensor(p.un("U"), [128, 2 * TOK], BF16))
            PB = core.enter_context(nc.sbuf_tensor(p.un("PB"), [128, 2 * TOK], BF16))
            WQ = core.enter_context(nc.sbuf_tensor(p.un("WQs"), [128, 2 * D], BF16))
            WK = core.enter_context(nc.sbuf_tensor(p.un("WKs"), [128, 2 * D], BF16))
            WV = core.enter_context(nc.sbuf_tensor(p.un("WVs"), [128, 2 * D], BF16))
            psSc = core.enter_context(nc.psum_tensor(p.un("psSc"), [128, TOK], F32))
            psB = core.enter_context(nc.psum_tensor(p.un("psB"), [128, TOK], F32))

            wsl_rel = [None, None]
            wsl_load = [None, None]
            wssem = p.take_slot_sems(2)

            def load_pair_w(hp):
                slot = hp % 2
                if wsl_rel[slot] is not None:
                    wsl_rel[slot].wait(g)
                for wdram, wbuf in ((wq, WQ), (wk, WK), (wv, WV)):
                    wssem[slot].inc(g.dma_start(out=wbuf[:, slot * D:(slot + 1) * D],
                                                in_=wdram[:, hp * D:(hp + 1) * D]), 16)
                wsl_load[slot] = Rel(wssem[slot], wssem[slot].n)

            act_read = [None, None]
            u_free = [None, None]
            vt_free = [None, None]
            pb_free = [None, None]

            def proj(wbuf, slot, out_sb, bias_col, ntok):
                evac_rel = [None, None, None, None]
                for nb in range(ntok // 512):
                    q4 = nb % 4
                    if evac_rel[q4] is not None:
                        evac_rel[q4].wait(t)
                    else:
                        p.wall(t, p.sV)
                    for ic in range(CH):
                        p.sT.inc(t.matmul(
                            psSc[:, q4 * 512:(q4 + 1) * 512],
                            wbuf[:, slot * D + ic * 128: slot * D + (ic + 1) * 128],
                            p.XN[:, ic * SEQ + nb * 512: ic * SEQ + (nb + 1) * 512],
                            start=(ic == 0), stop=(ic == CH - 1),
                            skip_group_check=True))
                    v.wait_ge(p.sT.h, p.sT.n)
                    if bias_col is not None:
                        p.sV.inc(v.tensor_scalar(out_sb[:, nb * 512:(nb + 1) * 512],
                                                 psSc[:, q4 * 512:(q4 + 1) * 512],
                                                 bias_col, None, ALU.add))
                    else:
                        p.sV.inc(v.tensor_copy(out_sb[:, nb * 512:(nb + 1) * 512],
                                               psSc[:, q4 * 512:(q4 + 1) * 512]))
                    evac_rel[q4] = Rel(p.sV, p.sV.n)

            dma_fence(p)
            load_pair_w(0)
            for hp in range(NPAIR):
                slot = hp % 2
                wsl_load[slot].wait(t)
                if hp + 1 < NPAIR:
                    load_pair_w(hp + 1)
                p.wall(t, p.sA)
                proj(WK, slot, K_sb, p.b0k[:, hp:hp + 1] if l0 else None, SEQ)
                proj(WV, slot, VT_sb, p.b0v[:, hp:hp + 1] if l0 else None, SEQ)
                proj(WQ, slot, Q_sb, p.b0q[:, hp:hp + 1] if l0 else None, TOK)
                wsl_rel[slot] = Rel(p.sT, p.sT.n)
                psBb = psB[:].bitcast(BF16)
                p.wall(t, p.sV)
                for tc in range(32):
                    p.sT.inc(t.transpose(psBb[:, tc * 128:(tc + 1) * 128],
                                         VT_sb[:, tc * 128:(tc + 1) * 128], p.eyeb[:]))
                p.wall(v, p.sT)
                v.memset(Vsb[:], 1.0)
                src3 = psBb[:, 0:4096].rearrange("q (tc h c) -> q tc h c",
                                                 tc=32, h=2, c=64)
                dst3 = Vsb[:].rearrange("q (tc h c) -> q tc h c",
                                        tc=32, h=2, c=65)[:, :, :, 0:64]
                p.sV.inc(v.tensor_copy(dst3, src3))
                vsb_ready = p.sV.n
                for hh in range(2):
                    rb = 64 * hh
                    for kc in range(32):
                        ksl = K_sb[rb:rb + 64, kc * 128:(kc + 1) * 128]
                        for qh in range(2):
                            it = (kc * 2 + qh)
                            sb = it % 2
                            q0 = qh * 1024
                            if act_read[sb] is not None:
                                act_read[sb].wait(t)
                            if kc == 0 and qh == 0 and hh == 0:
                                t.wait_ge(p.sV.h, vsb_ready)
                            for nb in range(2):
                                p.sT.inc(t.matmul(
                                    psSc[:, sb * 1024 + nb * 512: sb * 1024 + (nb + 1) * 512],
                                    ksl,
                                    Q_sb[rb:rb + 64, q0 + nb * 512: q0 + (nb + 1) * 512],
                                    start=True, stop=True, skip_group_check=True))
                            sc_done = p.sT.n
                            a.wait_ge(p.sT.h, sc_done)
                            scsl = psSc[:, sb * 1024:(sb + 1) * 1024]
                            if l0:
                                if u_free[sb] is not None:
                                    u_free[sb].wait(a)
                                p.sA.inc(a.activation(U[:, sb * 1024:(sb + 1) * 1024],
                                                      scsl, AF.Exp,
                                                      bias=p.c_ln2[:, 0:1], scale=0.125))
                                if vt_free[sb] is not None:
                                    vt_free[sb].wait(a)
                                p.sA.inc(a.activation(p.TMP[:, sb * 1024:(sb + 1) * 1024],
                                                      scsl, AF.Tanh,
                                                      bias=p.c_mu2[:, 0:1], scale=0.0625))
                                act_read[sb] = Rel(p.sA, p.sA.n)
                                v.wait_ge(p.sA.h, p.sA.n)
                                if pb_free[sb] is not None:
                                    pb_free[sb].wait(v)
                                p.sV.inc(v.scalar_tensor_tensor(
                                    PB[:, sb * 1024:(sb + 1) * 1024],
                                    p.TMP[:, sb * 1024:(sb + 1) * 1024], 1.0,
                                    U[:, sb * 1024:(sb + 1) * 1024],
                                    ALU.add, ALU.mult))
                                u_free[sb] = Rel(p.sV, p.sV.n)
                                vt_free[sb] = Rel(p.sV, p.sV.n)
                                pready = Rel(p.sV, p.sV.n)
                            else:
                                if pb_free[sb] is not None:
                                    pb_free[sb].wait(a)
                                p.sA.inc(a.activation(PB[:, sb * 1024:(sb + 1) * 1024],
                                                      scsl, AF.Exp,
                                                      bias=p.c_zero[:, 0:1], scale=0.125))
                                act_read[sb] = Rel(p.sA, p.sA.n)
                                pready = Rel(p.sA, p.sA.n)
                            pready.wait(t)
                            vsl = Vsb[:, kc * 130 + hh * 65: kc * 130 + hh * 65 + 65]
                            for nb in range(2):
                                p.sT.inc(t.matmul(
                                    psB[0:65, q0 + nb * 512: q0 + (nb + 1) * 512], vsl,
                                    PB[:, sb * 1024 + nb * 512: sb * 1024 + (nb + 1) * 512],
                                    start=(kc == 0), stop=(kc == 31),
                                    skip_group_check=True))
                            pb_free[sb] = Rel(p.sT, p.sT.n)
                    p.wall(v, p.sT)
                    p.sV.inc(v.tensor_copy(A_sb[rb:rb + 64, hp * TOK:(hp + 1) * TOK],
                                           psB[0:64, :]))
                    rrow = p.SR[(R_A if hh == 0 else R_B):(R_A if hh == 0 else R_B) + 1, :]
                    p.sV.inc(v.reciprocal(rrow, psB[64:65, :]))
                    p.wall(t, p.sV)
                p.wall(t, p.sV, p.sA)
                bcast_row(p, psSc, p.SR[R_A:R_A + 1, :], TOK)
                p.wall(v, p.sT)
                p.sV.inc(v.tensor_tensor(A_sb[0:64, hp * TOK:(hp + 1) * TOK],
                                         A_sb[0:64, hp * TOK:(hp + 1) * TOK],
                                         psSc[0:64, :], ALU.mult))
                p.wall(t, p.sV)
                bcast_row(p, psSc, p.SR[R_B:R_B + 1, :], TOK)
                p.wall(v, p.sT)
                p.sV.inc(v.tensor_tensor(A_sb[64:128, hp * TOK:(hp + 1) * TOK],
                                         A_sb[64:128, hp * TOK:(hp + 1) * TOK],
                                         psSc[64:128, :], ALU.mult))
                p.wall(t, p.sV)
            p.release_slot_sems(2)
            if not l0:
                p.wall(a, p.sV)
                for hp in range(NPAIR):
                    p.sA.inc(a.activation(A_sb[:, hp * TOK:(hp + 1) * TOK],
                                          A_sb[:, hp * TOK:(hp + 1) * TOK],
                                          AF.Gelu, bias=p.c_zero[:, 0:1], scale=1.0))

        out_dram = p.dram["x_dram"] if l0 else p.dram["outT"]
        xin_dram = p.dram["xT"] if l0 else p.dram["x_dram"]
        with ExitStack() as ep:
            WO = ep.enter_context(nc.sbuf_tensor(p.un("WOs"), [128, 2 * D], BF16))
            XB = ep.enter_context(nc.sbuf_tensor(p.un("epxb"), [128, 2 * TOK], F32))
            T2 = ep.enter_context(nc.sbuf_tensor(p.un("ept2"), [128, TOK], F32))
            psW = ep.enter_context(nc.psum_tensor(p.un("psW"), [128, TOK], F32))
            wo_rel = [None, None]
            wo_load = [None, None]
            wosem = p.take_slot_sems(2)

            def load_wo(oc):
                slot = oc % 2
                if wo_rel[slot] is not None:
                    wo_rel[slot].wait(g)
                wosem[slot].inc(g.dma_start(out=WO[:, slot * D:(slot + 1) * D],
                                            in_=wo[:, oc * D:(oc + 1) * D]), 16)
                wo_load[slot] = Rel(wosem[slot], wosem[slot].n)

            dma_fence(p)
            load_wo(0)
            for oc in range(CH):
                slot = oc % 2
                wo_load[slot].wait(t)
                if oc + 1 < CH:
                    load_wo(oc + 1)
                xsl = XB[:, slot * TOK:(slot + 1) * TOK]
                p.wall(s, p.sV)
                p.dS.inc(s.dma_start(out=xsl, in_=xin_dram[:, oc * TOK:(oc + 1) * TOK]), 16)
                p.wall(t, p.sV, p.sA)
                for nb in range(4):
                    for ic in range(CH):
                        p.sT.inc(t.matmul(
                            psW[:, nb * 512:(nb + 1) * 512],
                            WO[:, slot * D + ic * 128: slot * D + (ic + 1) * 128],
                            A_sb[:, ic * TOK + nb * 512: ic * TOK + (nb + 1) * 512],
                            start=(ic == 0), stop=(ic == CH - 1),
                            skip_group_check=True))
                wo_rel[slot] = Rel(p.sT, p.sT.n)
                p.wall(v, p.sT, p.dS)
                if l0:
                    p.sV.inc(v.scalar_tensor_tensor(p.TMP[:], psW[:], p.b0o[:, oc:oc + 1],
                                                    p.G[:, oc * TOK:(oc + 1) * TOK],
                                                    ALU.add, ALU.mult))
                    p.sV.inc(v.tensor_add(T2[:], xsl, p.TMP[:]))
                else:
                    p.sV.inc(v.tensor_tensor(p.TMP[:], psW[:],
                                             p.XN[:, oc * SEQ: oc * SEQ + TOK], ALU.add))
                    p.wall(a, p.sV)
                    p.sA.inc(a.activation(T2[:], p.G[:, oc * TOK:(oc + 1) * TOK],
                                          AF.Silu, bias=p.c_zero[:, 0:1], scale=1.0))
                    p.wall(v, p.sA)
                    p.sV.inc(v.tensor_mul(p.TMP[:], p.TMP[:], T2[:]))
                    p.sV.inc(v.tensor_add(T2[:], xsl, p.TMP[:]))
                p.wall(s, p.sV)
                p.dO.inc(s.dma_start(out=out_dram[:, oc * TOK:(oc + 1) * TOK], in_=T2[:]), 16)
                p.wall(v, p.dO)
            p.release_slot_sems(2)



def gating_moe(p):
    nc = p.nc
    v, s, g, a, t = nc.vector, nc.sync, nc.gpsimd, nc.scalar, nc.tensor
    d = p.dram

    with ExitStack() as ph:
        GT = ph.enter_context(nc.sbuf_tensor(p.un("GT"), [128, CH * NE], BF16))
        LG = ph.enter_context(nc.sbuf_tensor(p.un("LG"), [8, TOK], F32))
        LGt = ph.enter_context(nc.sbuf_tensor(p.un("LGt"), [128, 128], F32))
        MX = ph.enter_context(nc.sbuf_tensor(p.un("MX"), [128, 16], F32))
        RW = ph.enter_context(nc.sbuf_tensor(p.un("RW"), [128, 16], F32))
        TS = ph.enter_context(nc.sbuf_tensor(p.un("TS"), [128, 128], F32))
        dma_fence(p)
        p.dG.inc(g.dma_start(out=GT[:], in_=d["gateT"][:]), 16)
        gt_rel = Rel(p.dG, p.dG.n)

        with nc.psum_tensor(p.un("psG"), [32, TOK], F32) as psG:
            gt_rel.wait(t)
            p.wall(t, p.sV)
            for nb in range(4):
                for ic in range(CH):
                    p.sT.inc(t.matmul(psG[0:8, nb * 512:(nb + 1) * 512],
                                      GT[:, ic * 8:(ic + 1) * 8],
                                      p.XN[:, ic * SEQ + nb * 512: ic * SEQ + (nb + 1) * 512],
                                      start=(ic == 0), stop=(ic == CH - 1),
                                      skip_group_check=True))
            p.wall(v, p.sT)
            p.sV.inc(v.tensor_copy(LG[:], psG[0:8, :]))
        with nc.psum_tensor(p.un("psLt"), [128, 128], F32) as psLt:
            p.wall(t, p.sV)
            for tc in range(16):
                p.sT.inc(t.transpose(psLt[:, tc * 8:(tc + 1) * 8],
                                     LG[0:8, tc * 128:(tc + 1) * 128],
                                     p.eyef[0:8, 0:8]))
            p.wall(v, p.sT)
            p.sV.inc(v.tensor_copy(LGt[:], psLt[:]))
        for tc in range(16):
            p.sV.inc(v.reduce_max(MX[:, tc:tc + 1], LGt[:, tc * 8:(tc + 1) * 8], axis=AX.X))
        v.drain()
        p.wall(a, p.sV)
        p.sA.inc(a.activation(RW[:], MX[:], AF.Sigmoid, bias=p.c_zero[:, 0:1], scale=1.0))
        p.wall(v, p.sA)
        v.drain()
        for tc in range(16):
            p.sV.inc(v.tensor_scalar(TS[:, tc * 8:(tc + 1) * 8], LGt[:, tc * 8:(tc + 1) * 8],
                                     MX[:, tc:tc + 1], RW[:, tc:tc + 1],
                                     ALU.is_equal, ALU.mult))

        p.wall(v, p.sA)
        v.memset(p.G[:], 0.0)
        p.sV.inc(v.engine_nop())
        XS = ph.enter_context(nc.sbuf_tensor(p.un("XS"), [128, CH * TOK], BF16))
        PRD = ph.enter_context(nc.sbuf_tensor(p.un("PRD"), [128, TOK], BF16))
        WG = ph.enter_context(nc.sbuf_tensor(p.un("WGs"), [128, 2 * 4096], BF16))
        WU = ph.enter_context(nc.sbuf_tensor(p.un("WUs"), [128, 2 * 4096], BF16))
        WD = ph.enter_context(nc.sbuf_tensor(p.un("WDs"), [128, 2 * 4096], BF16))
        we_rel = [None, None]
        we_load = [None, None]
        wesem = p.take_slot_sems(2)

        def load_we(e):
            slot = e % 2
            if we_rel[slot] is not None:
                we_rel[slot].wait(g)
            for wdram, wbuf in ((d["wgT"], WG), (d["wuT"], WU), (d["wdT"], WD)):
                wesem[slot].inc(g.dma_start(out=wbuf[:, slot * 4096:(slot + 1) * 4096],
                                            in_=wdram[e]), 16)
            we_load[slot] = Rel(wesem[slot], wesem[slot].n)

        dma_fence(p)
        load_we(0)
        for e in range(NE):
            slot = e % 2
            we_load[slot].wait(t)
            if e + 1 < NE:
                load_we(e + 1)
            with nc.psum_tensor(p.un("pRow"), [1, TOK], F32) as pRow:
                p.wall(t, p.sV)
                for tc in range(16):
                    p.sT.inc(t.transpose(pRow[0:1, tc * 128:(tc + 1) * 128],
                                         TS[:, tc * 8 + e: tc * 8 + e + 1], p.eyef[:]))
                p.wall(v, p.sT)
                p.sV.inc(v.tensor_copy(p.SR[R_A:R_A + 1, :], pRow[0:1, :]))
            with nc.psum_tensor(p.un("pB2"), [128, TOK], F32) as pB2:
                p.wall(t, p.sV)
                bcast_row(p, pB2, p.SR[R_A:R_A + 1, :], TOK)
                p.wall(v, p.sT)
                for c in range(CH):
                    p.sV.inc(v.tensor_tensor(XS[:, c * TOK:(c + 1) * TOK],
                                             p.XN[:, c * SEQ: c * SEQ + TOK],
                                             pB2[:], ALU.mult))
            for tt in range(4):
                t0 = tt * 512
                with nc.psum_tensor(p.un("pGm"), [128, 2048], F32) as pG, \
                     nc.psum_tensor(p.un("pUm"), [128, 2048], F32) as pU:
                    p.wall(t, p.sV, p.sA)
                    for oc in range(4):
                        for ic in range(CH):
                            wof = slot * 4096 + ic * 512 + oc * 128
                            p.sT.inc(t.matmul(pG[:, oc * 512:(oc + 1) * 512],
                                              WG[:, wof:wof + 128],
                                              XS[:, ic * TOK + t0: ic * TOK + t0 + 512],
                                              start=(ic == 0), stop=(ic == CH - 1),
                                              skip_group_check=True))
                            p.sT.inc(t.matmul(pU[:, oc * 512:(oc + 1) * 512],
                                              WU[:, wof:wof + 128],
                                              XS[:, ic * TOK + t0: ic * TOK + t0 + 512],
                                              start=(ic == 0), stop=(ic == CH - 1),
                                              skip_group_check=True))
                    p.wall(a, p.sT)
                    p.sA.inc(a.activation(p.TMP[:], pG[:], AF.Silu,
                                          bias=p.c_zero[:, 0:1], scale=1.0))
                    p.wall(v, p.sA, p.sT)
                    p.sV.inc(v.scalar_tensor_tensor(PRD[:], pU[:], 1.0, p.TMP[:],
                                                    ALU.bypass, ALU.mult))
                with nc.psum_tensor(p.un("pDm"), [128, 4096], F32) as pD:
                    p.wall(t, p.sV)
                    for oc in range(CH):
                        for ic in range(4):
                            wof = slot * 4096 + ic * D + oc * 128
                            p.sT.inc(t.matmul(pD[:, oc * 512:(oc + 1) * 512],
                                              WD[:, wof:wof + 128],
                                              PRD[:, ic * 512:(ic + 1) * 512],
                                              start=(ic == 0), stop=(ic == 3),
                                              skip_group_check=True))
                    p.wall(v, p.sT)
                    gview = p.G[:].rearrange("q (c tk) -> q c tk", c=CH)[:, :, t0:t0 + 512]
                    dview = pD[:].rearrange("q (c tk) -> q c tk", c=CH)
                    p.sV.inc(v.tensor_tensor(gview, gview, dview, ALU.add))
            we_rel[slot] = Rel(p.sT, p.sT.n)


class StopEmission(Exception):
    pass


def _ck(p, name):
    if STOP_AFTER and name == STOP_AFTER:
        raise StopEmission()


def layer(p, lid):
    ln_phase(p, lid, inner=False)
    tap(p, f"xn{lid}", p.XN[:, 0:TOK], (p.sV,))
    _ck(p, f"ln{lid}")
    exchange_export(p, lid)
    _ck(p, f"exp{lid}")
    if lid == 0:
        ffn0(p)
    else:
        gating_moe(p)
    tap(p, f"g{lid}", p.G[:, 0:TOK], (p.sV, p.sA))
    _ck(p, f"ffn{lid}")
    exchange_import(p, lid)
    _ck(p, f"imp{lid}")
    if lid == 0:
        ln_phase(p, lid, inner=True)
        tap(p, "xni0", p.XN[:, 0:TOK], (p.sV,))
        _ck(p, "lni0")
    attention(p, lid)
    if lid == 0:
        tap(p, "x1", p.dram["x_dram"][:, 0:TOK], (p.dO,))
    _ck(p, f"att{lid}")


def finale(p):
    nc = p.nc
    p.wall(nc.sync, p.dO)
    p.wall(nc.sync, p.dS)



def _lay_cm(w):
    r, n = w.shape
    ch = r // 128
    return np.ascontiguousarray(w.reshape(ch, 128, n).transpose(1, 0, 2).reshape(128, ch * n))


def _lay_pm(w_io):
    b = w_io.reshape(8, 128, 8, 128)
    return np.ascontiguousarray(b.transpose(1, 2, 0, 3).reshape(128, 8192))


def _lay_col(vec):
    return np.ascontiguousarray(vec.reshape(8, 128).T)


def _build_rails(rm, rup):
    ytiles = np.zeros((32, 128, 128), np.float32)
    for h in range(4):
        for j in range(8):
            tl = ytiles[h * 8 + j]
            for a_ in range(8):
                r = 8 * j + a_
                tl[16 * a_:16 * a_ + 16, 16 * a_:16 * a_ + 16] = rm[h, r]
    ztiles = np.zeros((128, 128, 128), np.float32)
    for ci in range(128):
        h = ci // 32
        rr = ci % 32
        tl = ztiles[ci]
        for l_ in range(2):
            r = 2 * rr + l_
            tl[16 * (r % 8):16 * (r % 8) + 16, 64 * l_:64 * l_ + 64] = rup[h, r].T
    layY = np.ascontiguousarray(ytiles.transpose(1, 0, 2).reshape(128, 32 * 128))
    layZ = np.ascontiguousarray(ztiles.transpose(1, 0, 2).reshape(128, 128 * 128))
    return layY, layZ


def kernel(**inputs):
    global LAST_RESULTS
    f = lambda k: np.asarray(inputs[k], dtype=np.float32)

    x = f("L0_attn_sequence")
    shared = {
        "w0qT": _lay_pm(np.ascontiguousarray(f("L0_attn_Wq").T)),
        "w0kT": _lay_pm(np.ascontiguousarray(f("L0_attn_Wk").T)),
        "w0vT": _lay_pm(np.ascontiguousarray(f("L0_attn_Wv").T)),
        "w0oT": _lay_pm(np.ascontiguousarray(f("L0_attn_Wo").T)),
        "b0q": _lay_col(f("L0_attn_bq")),
        "b0k": _lay_col(f("L0_attn_bk")),
        "b0v": _lay_col(f("L0_attn_bv")),
        "b0o": _lay_col(f("L0_attn_bo")),
        "gam": _lay_col(f("L0_attn_ln_gamma")),
        "bet": _lay_col(f("L0_attn_ln_beta")),
        "mu": f("L0_attn_mu").reshape(1, 1),
        "cprojT": _lay_cm(np.ascontiguousarray(f("L0_ffn_w_cproj").T)),
        "w1qT": _lay_pm(f("L1_attn_Wq")),
        "w1kT": _lay_pm(f("L1_attn_Wk")),
        "w1vT": _lay_pm(f("L1_attn_Wv")),
        "w1postT": _lay_pm(f("L1_attn_Wpost")),
        "gateT": _lay_cm(np.ascontiguousarray(f("L1_ffn_gate_w").T)),
        "wgT": np.stack([_lay_cm(np.ascontiguousarray(f("L1_ffn_w_gate")[e].T))
                         for e in range(NE)]),
        "wuT": np.stack([_lay_cm(np.ascontiguousarray(f("L1_ffn_w_up")[e].T))
                         for e in range(NE)]),
        "wdT": np.stack([_lay_cm(np.ascontiguousarray(f("L1_ffn_w_down")[e].T))
                         for e in range(NE)]),
        "eye": np.eye(128, dtype=np.float32),
    }
    shared["railsY"], shared["railsZ"] = _build_rails(
        f("L0_ffn_rail_maker"), f("L0_ffn_rail_up_proj"))

    nc = build_program(taps=DEBUG_TAPS, emit_upto=EMIT_UPTO)

    in_maps = []
    for c in range(8):
        b, par = c // 2, c % 2
        xs = np.ascontiguousarray(x[b, par * TOK:(par + 1) * TOK, :].T)
        m = dict(shared)
        m["xT"] = _lay_cm(xs)
        m["mlo"] = np.full((128, 1), float(par), np.float32)
        m["mhi"] = np.full((128, 1), float(1 - par), np.float32)
        in_maps.append(m)

    res = run_bass_kernel_spmd(nc, in_maps, core_ids=list(range(8)))
    LAST_RESULTS = res.results

    out = np.zeros((4, SEQ, D), np.float32)
    for c in range(8):
        b, par = c // 2, c % 2
        o = res.results[c]["outT"]
        o = o.reshape(128, CH, TOK).transpose(1, 0, 2).reshape(D, TOK).T
        out[b, par * TOK:(par + 1) * TOK, :] = o
    return out


# version 04
# speedup vs baseline: 1.2504x; 1.2504x over previous
import numpy as np
from contextlib import ExitStack

import concourse.bass as bass
import concourse.mybir as mybir
from concourse.bass_utils import run_bass_kernel_spmd

F32 = mybir.dt.float32
BF16 = mybir.dt.bfloat16
AF = mybir.ActivationFunctionType
ALU = mybir.AluOpType
AX = mybir.AxisListType

D = 1024
CH = 8
TOK = 2048
SEQ = 4096
NH = 16
NPAIR = 8
NE = 8
FCH = 128

LOG2 = float(np.log(2.0))

DEBUG_TAPS = {}
EMIT_UPTO = 2
STOP_AFTER = ""
PARANOID = 0
NO_BOUNCE = False
LAST_RESULTS = None

R_A = 0
R_B = 32
R_V = 64
R_M = 96


class Sem:
    def __init__(self, ctx, nc, name):
        self.h = ctx.enter_context(nc.semaphore(name))
        self.n = 0

    def inc(self, instr, k=1):
        instr.then_inc(self.h, k)
        self.n += k
        return instr

    def wait(self, eng, n=None):
        eng.wait_ge(self.h, self.n if n is None else n)


class Rel:
    def __init__(self, sem, n):
        self.sem = sem
        self.n = n

    def wait(self, eng):
        eng.wait_ge(self.sem.h, self.n)


class P:
    def take_slot_sems(self, n):
        out = self.slot_sems[self._slot_next:self._slot_next + n]
        assert len(out) == n, "out of slot sems"
        self._slot_next += n
        return out

    def release_slot_sems(self, n):
        self._slot_next -= n

    def un(self, base):
        self._uc = getattr(self, "_uc", 0) + 1
        return f"{base}_{self._uc}"


def build_program(taps=None, emit_upto=2):
    nc = bass.Bass("TRN2", target_bir_lowering=False, debug=False)
    st = ExitStack()
    p = P()
    p.nc = nc
    p.st = st
    p.taps = dict(taps or {})

    dram = {}

    def din(name, shape, dtype=F32):
        dram[name] = nc.dram_tensor(name, shape, dtype, kind="ExternalInput").ap()

    def dint(name, shape, dtype=F32):
        dram[name] = nc.dram_tensor(name, shape, dtype, kind="Internal").ap()

    def dout(name, shape, dtype=F32):
        dram[name] = nc.dram_tensor(name, shape, dtype, kind="ExternalOutput").ap()

    din("xT", [128, CH * TOK])
    for nm in ("w0qT", "w0kT", "w0vT", "w0oT", "w1qT", "w1kT", "w1vT", "w1postT"):
        din(nm, [128, NPAIR * D])
    for nm in ("b0q", "b0k", "b0v", "b0o", "gam", "bet"):
        din(nm, [128, 8])
    din("mu", [1, 1])
    din("railsY", [128, 32 * 128])
    din("railsZ", [128, 128 * 128])
    din("cprojT", [128, FCH * D])
    din("gateT", [128, CH * NE])
    din("wgT", [NE, 128, CH * 512])
    din("wuT", [NE, 128, CH * 512])
    din("wdT", [NE, 128, 4 * D])
    din("mlo", [128, 1])
    din("mhi", [128, 1])
    din("eye", [128, 128])
    dint("x_dram", [128, CH * TOK])
    dint("cc_in0", [128, CH * TOK], BF16)
    dint("cc_out0", [256, CH * TOK], BF16)
    dint("cc_in1", [128, CH * TOK], BF16)
    dint("cc_out1", [256, CH * TOK], BF16)
    dint("cprojT_bf", [128, FCH * D], BF16)
    dout("outT", [128, CH * TOK])
    p.dram = dram

    p.tap_out = {}
    for tname, shape in p.taps.items():
        dout("tap_" + tname, list(shape), F32)
        p.tap_out[tname] = dram["tap_" + tname]

    p.dS = Sem(st, nc, "dS")
    p.dG = Sem(st, nc, "dG")
    p.sT = Sem(st, nc, "sT")
    p.sA = Sem(st, nc, "sA")
    p.sV = Sem(st, nc, "sV")
    p.sC = Sem(st, nc, "sC")
    p.dO = Sem(st, nc, "dO")
    p.slot_sems = [Sem(st, nc, f"sl{i}") for i in range(12)]
    p._slot_next = 0

    def wall(eng, *sems):
        for s_ in sems:
            s_.wait(eng)
    p.wall = wall

    sb = st.enter_context
    p.XN = sb(nc.sbuf_tensor("XN", [128, CH * SEQ], BF16))
    p.G = sb(nc.sbuf_tensor("G", [128, CH * TOK], BF16))
    p.TMP = sb(nc.sbuf_tensor("TMP", [128, TOK], F32))
    p.SR = sb(nc.sbuf_tensor("SR", [128, TOK], F32))
    p.c_ones_b = sb(nc.sbuf_tensor("c_ones_b", [128, 1], BF16))
    p.c_onesr_f = sb(nc.sbuf_tensor("c_onesr_f", [128, 128], F32))
    p.c_ln2 = sb(nc.sbuf_tensor("c_ln2", [128, 1], F32))
    p.c_mu2 = sb(nc.sbuf_tensor("c_mu2", [128, 1], F32))
    p.c_zero = sb(nc.sbuf_tensor("c_zero", [128, 1], F32))
    p.c_eps1 = sb(nc.sbuf_tensor("c_eps1", [128, 1], F32))
    p.c_eps2 = sb(nc.sbuf_tensor("c_eps2", [128, 1], F32))
    p.b0q = sb(nc.sbuf_tensor("b0q_s", [128, 8], F32))
    p.b0k = sb(nc.sbuf_tensor("b0k_s", [128, 8], F32))
    p.b0v = sb(nc.sbuf_tensor("b0v_s", [128, 8], F32))
    p.b0o = sb(nc.sbuf_tensor("b0o_s", [128, 8], F32))
    p.gam = sb(nc.sbuf_tensor("gam_s", [128, 8], F32))
    p.bet = sb(nc.sbuf_tensor("bet_s", [128, 8], F32))
    p.mlo = sb(nc.sbuf_tensor("mlo_s", [128, 1], F32))
    p.mhi = sb(nc.sbuf_tensor("mhi_s", [128, 1], F32))
    p.eyeb = sb(nc.sbuf_tensor("eyeb", [128, 128], BF16))
    p.eyef = sb(nc.sbuf_tensor("eyef", [128, 128], F32))

    prologue(p)
    try:
        for lid in range(min(2, emit_upto)):
            layer(p, lid)
    except StopEmission:
        pass
    finale(p)

    st.close()
    return nc


def tap(p, name, src_ap, sems):
    if name not in p.tap_out:
        return
    nc = p.nc
    eng = nc.gpsimd if src_ap.dtype != F32 else nc.sync
    p.wall(eng, *sems)
    p.dO.inc(eng.dma_start(out=p.tap_out[name][:], in_=src_ap), 16)


def prologue(p):
    nc = p.nc
    v, s, g, a, t = nc.vector, nc.sync, nc.gpsimd, nc.scalar, nc.tensor
    d = p.dram
    v.memset(p.c_ones_b[:], 1.0)
    v.memset(p.c_onesr_f[:], 1.0)
    v.memset(p.c_ln2[:], -LOG2)
    v.memset(p.c_zero[:], 0.0)
    v.memset(p.c_eps1[:], 1e-5)
    v.memset(p.c_eps2[:], 1e-6)
    p.sV.inc(v.engine_nop())
    for nm, t_sb in (("b0q", p.b0q), ("b0k", p.b0k), ("b0v", p.b0v), ("b0o", p.b0o),
                     ("gam", p.gam), ("bet", p.bet), ("mlo", p.mlo), ("mhi", p.mhi),
                     ("eye", p.eyef)):
        p.dS.inc(s.dma_start(out=t_sb[:], in_=d[nm][:]), 16)
    p.dG.inc(g.dma_start(out=p.eyeb[:], in_=d["eye"][:]), 16)
    mut = p.st.enter_context(nc.sbuf_tensor("mut", [1, 1], F32))
    with nc.psum_tensor("pmu", [128, 1], F32) as pmu:
        p.dS.inc(s.dma_start(out=mut[:], in_=d["mu"][:]), 16)
        p.wall(t, p.dS, p.sV)
        p.sT.inc(t.matmul(pmu[:], p.c_onesr_f[0:1, :], mut[:], start=True, stop=True))
        p.wall(a, p.sT, p.sV)
        p.sA.inc(a.activation(p.c_mu2[:], pmu[:], AF.Copy, bias=0.0, scale=-0.5))
        a.drain()


def dma_fence(p):
    nc = p.nc
    p.wall(nc.sync, p.sT, p.sA, p.sV)
    p.wall(nc.gpsimd, p.sT, p.sA, p.sV)


def bcast_row(p, psdst, row_ap, width):
    t = p.nc.tensor
    rb = row_ap.base_partition()
    for nb in range(width // 512):
        p.sT.inc(t.matmul(psdst[:, nb * 512:(nb + 1) * 512],
                          p.c_onesr_f[rb:rb + 1, :],
                          row_ap[0:1, nb * 512:(nb + 1) * 512],
                          start=True, stop=True, skip_group_check=True))


def ln_phase(p, layer_id, inner):
    nc = p.nc
    v, s, a, t = nc.vector, nc.sync, nc.scalar, nc.tensor
    nhalf = 2 if inner else 1
    src = p.dram["xT"] if layer_id == 0 else p.dram["x_dram"]

    p.wall(v, p.dS, p.sT)
    p.wall(s, p.dO)
    dma_fence(p)
    with ExitStack() as ph:
        SQ = ph.enter_context(nc.sbuf_tensor(p.un("lnsq"), [128, TOK], BF16))
        XB = None
        if not inner:
            XB = ph.enter_context(nc.sbuf_tensor(p.un("lnxb"), [128, 2 * TOK], F32))
        for half in range(nhalf):
            hs = half * TOK
            pstat_cm = nc.psum_tensor(p.un("pstat"), [64, TOK], F32)
            pstat = pstat_cm.__enter__()
            for c in range(CH):
                if not inner:
                    xsl = XB[:, (c % 2) * TOK:(c % 2 + 1) * TOK]
                    p.wall(s, p.sV)
                    p.dS.inc(s.dma_start(out=xsl, in_=src[:, c * TOK:(c + 1) * TOK]), 16)
                    p.wall(v, p.dS)
                    xb = p.XN[:, c * SEQ + TOK: c * SEQ + 2 * TOK]
                    p.sV.inc(v.tensor_copy(xb, xsl))
                else:
                    xb = p.XN[:, c * SEQ + hs: c * SEQ + hs + TOK]
                p.wall(v, p.sT)
                p.sV.inc(v.tensor_mul(SQ[:], xb, xb))
                p.wall(t, p.sV)
                for nb in range(4):
                    n0 = nb * 512
                    p.sT.inc(t.matmul(pstat[0:1, n0:n0 + 512], p.c_ones_b[:],
                                      xb[:, n0:n0 + 512], start=(c == 0),
                                      stop=(c == CH - 1), skip_group_check=True))
                    p.sT.inc(t.matmul(pstat[32:33, n0:n0 + 512], p.c_ones_b[:],
                                      SQ[:, n0:n0 + 512], start=(c == 0),
                                      stop=(c == CH - 1), skip_group_check=True))
            p.wall(v, p.sT)
            mrow = p.SR[R_M:R_M + 1, :]
            vrow = p.SR[R_V:R_V + 1, :]
            arow = p.SR[R_A:R_A + 1, :]
            brow = p.SR[R_B:R_B + 1, :]
            p.sV.inc(v.tensor_scalar_mul(mrow, pstat[0:1, :], 1.0 / D))
            p.sV.inc(v.tensor_mul(vrow, mrow, mrow))
            p.sV.inc(v.scalar_tensor_tensor(vrow, pstat[32:33, :], 1.0 / D, vrow,
                                            ALU.mult, ALU.subtract))
            epsc = p.c_eps2 if inner else p.c_eps1
            p.wall(a, p.sV)
            p.sA.inc(a.activation(brow, vrow, AF.Ln, bias=epsc[0:1, 0:1], scale=1.0))
            a.drain()
            p.sA.inc(a.activation(arow, brow, AF.Exp, bias=p.c_zero[0:1, 0:1], scale=-0.5))
            p.wall(v, p.sA)
            p.sV.inc(v.scalar_tensor_tensor(brow, pstat[0:1, :], -1.0 / D, arow,
                                            ALU.mult, ALU.mult))
            pstat_cm.__exit__(None, None, None)
            pbca_cm = nc.psum_tensor(p.un("pbca"), [128, TOK], F32)
            pbcb_cm = nc.psum_tensor(p.un("pbcb"), [128, TOK], F32)
            pbca = pbca_cm.__enter__()
            pbcb = pbcb_cm.__enter__()
            p.wall(t, p.sV)
            bcast_row(p, pbca, arow, TOK)
            bcast_row(p, pbcb, brow, TOK)
            for c in range(CH):
                if inner:
                    xb = p.XN[:, c * SEQ + hs: c * SEQ + hs + TOK]
                    xout = xb
                else:
                    xb = p.XN[:, c * SEQ + TOK: c * SEQ + 2 * TOK]
                    xout = p.XN[:, c * SEQ: c * SEQ + TOK]
                p.wall(v, p.sT)
                p.sV.inc(v.tensor_tensor(p.TMP[:], xb, pbca[:], ALU.mult))
                if inner:
                    p.sV.inc(v.tensor_tensor(p.TMP[:], p.TMP[:], pbcb[:], ALU.add))
                    p.sV.inc(v.tensor_scalar(xout, p.TMP[:], p.gam[:, c:c + 1],
                                             p.bet[:, c:c + 1], ALU.mult, ALU.add))
                else:
                    p.sV.inc(v.tensor_tensor(xout, p.TMP[:], pbcb[:], ALU.add))
            p.wall(t, p.sV)
            pbcb_cm.__exit__(None, None, None)
            pbca_cm.__exit__(None, None, None)


def exchange_export(p, layer_id):
    nc = p.nc
    s, g = nc.sync, nc.gpsimd
    cc_in = p.dram[f"cc_in{layer_id}"]
    cc_out = p.dram[f"cc_out{layer_id}"]
    p.wall(s, p.sV)
    for c in range(CH):
        p.dS.inc(s.dma_start(out=cc_in[:, c * TOK:(c + 1) * TOK],
                             in_=p.XN[:, c * SEQ: c * SEQ + TOK]), 16)
    p.wall(g, p.dS)
    p.sC.inc(g.collective_compute(
        "AllGather", ALU.bypass, ins=[cc_in[:]], outs=[cc_out[:]],
        replica_groups=[[0, 1], [2, 3], [4, 5], [6, 7]]))


def exchange_import(p, layer_id):
    nc = p.nc
    s, v = nc.sync, nc.vector
    cc_out = p.dram[f"cc_out{layer_id}"]
    p.wall(s, p.sC)
    dma_fence(p)
    with nc.sbuf_tensor(p.un("imA"), [128, TOK], BF16) as imA, \
         nc.sbuf_tensor(p.un("imB"), [128, TOK], BF16) as imB:
        for c in range(CH):
            p.wall(s, p.sV)
            p.dS.inc(s.dma_start(out=imA[:], in_=cc_out[0:128, c * TOK:(c + 1) * TOK]), 16)
            p.dS.inc(s.dma_start(out=imB[:], in_=cc_out[128:256, c * TOK:(c + 1) * TOK]), 16)
            p.wall(v, p.dS, p.sT)
            xo = p.XN[:, c * SEQ + TOK: c * SEQ + 2 * TOK]
            p.sV.inc(v.tensor_scalar(p.TMP[:], imA[:], p.mlo[:, 0:1], None, ALU.mult))
            p.sV.inc(v.scalar_tensor_tensor(xo, imB[:], p.mhi[:, 0:1], p.TMP[:],
                                            ALU.mult, ALU.add))



def ffn0(p):
    nc = p.nc
    v, s, g, a, t = nc.vector, nc.sync, nc.gpsimd, nc.scalar, nc.tensor
    d = p.dram
    NTG = 8
    TW = 256

    with ExitStack() as ph:
        RTY = ph.enter_context(nc.sbuf_tensor(p.un("RTY"), [128, 32 * 128], BF16))
        RTZ = ph.enter_context(nc.sbuf_tensor(p.un("RTZ"), [128, 128 * 128], BF16))
        YB = ph.enter_context(nc.sbuf_tensor(p.un("YB"), [128, 32 * TW], BF16))
        H4 = ph.enter_context(nc.sbuf_tensor(p.un("H4"), [128, 4 * TW], BF16))
        WA = ph.enter_context(nc.sbuf_tensor(p.un("WA"), [128, 4 * D], BF16))
        def GTOK(tc):
            return p.XN[:, (tc // 2) * SEQ + TOK + (tc % 2) * D:
                        (tc // 2) * SEQ + TOK + (tc % 2 + 1) * D]

        dma_fence(p)
        p.dG.inc(g.dma_start(out=RTY[:], in_=d["railsY"][:]), 16)
        for q in range(4):
            p.dG.inc(g.dma_start(out=RTZ[:, q * 4096:(q + 1) * 4096],
                                 in_=d["railsZ"][:, q * 4096:(q + 1) * 4096]), 16)
        rails_rel = Rel(p.dG, p.dG.n)

        wa_rel = [None, None]
        wa_bounce = [None, None]
        wa_load = [None, None]
        ldsem = p.take_slot_sems(2)
        bnsem = p.take_slot_sems(2)

        def load_w2(j, tg):
            slot = j % 2
            dst = WA[:, slot * 2 * D:(slot + 1) * 2 * D]
            if NO_BOUNCE or tg == 0:
                if wa_rel[slot] is not None:
                    wa_rel[slot].wait(g)
                if wa_bounce[slot] is not None:
                    wa_bounce[slot].wait(g)
                ldsem[slot].inc(g.dma_start(out=dst,
                                            in_=d["cprojT"][:, 2 * j * D:(2 * j + 2) * D]), 16)
                wa_load[slot] = Rel(ldsem[slot], ldsem[slot].n)
                if not NO_BOUNCE:
                    wa_load[slot].wait(s)
                    bnsem[slot].inc(s.dma_start(
                        out=d["cprojT_bf"][:, 2 * j * D:(2 * j + 2) * D], in_=dst), 16)
                    wa_bounce[slot] = Rel(bnsem[slot], bnsem[slot].n)
            else:
                eng = s if (j % 2 == 0) else g
                if wa_rel[slot] is not None:
                    wa_rel[slot].wait(eng)
                ldsem[slot].inc(eng.dma_start(
                    out=dst, in_=d["cprojT_bf"][:, 2 * j * D:(2 * j + 2) * D]), 16)
                wa_load[slot] = Rel(ldsem[slot], ldsem[slot].n)

        with nc.psum_tensor(p.un("py"), [128, 512], F32) as py, \
             nc.psum_tensor(p.un("pz"), [128, 1024], F32) as pz, \
             nc.psum_tensor(p.un("pot"), [128, 2048], F32) as pot:
            for tg in range(NTG):
                t0 = tg * TW
                if tg == 0:
                    rails_rel.wait(t)
                    p.wall(t, p.sV)
                if tg == 1:
                    p.wall(s, bnsem[0], bnsem[1])
                    p.wall(g, bnsem[0], bnsem[1])
                load_w2(0, tg)
                load_w2(1, tg)
                for yt in range(32):
                    j = yt % 8
                    half = yt % 2
                    p.wall(t, p.sV)
                    p.sT.inc(t.matmul(py[:, half * TW:(half + 1) * TW],
                                      RTY[:, yt * 128:(yt + 1) * 128],
                                      p.XN[:, j * SEQ + t0: j * SEQ + t0 + TW],
                                      start=True, stop=True, skip_group_check=True))
                    if half == 1:
                        p.wall(v, p.sT)
                        p.sV.inc(v.tensor_copy(YB[:, (yt - 1) * TW:(yt + 1) * TW], py[:]))
                for ib in range(32):
                    for k in range(4):
                        i = 4 * ib + k
                        if PARANOID & 4:
                            p.wall(t, p.dS, p.dG)
                        if PARANOID & 8:
                            p.wall(t, p.sA, p.sV)
                        if k == 0:
                            p.wall(t, p.sA)
                            if ib == 0:
                                p.wall(t, p.sV)
                        p.sT.inc(t.matmul(pz[:, k * TW:(k + 1) * TW],
                                          RTZ[:, i * 128:(i + 1) * 128],
                                          YB[:, (i // 4) * TW:(i // 4 + 1) * TW],
                                          start=True, stop=True, skip_group_check=True))
                    p.wall(a, p.sT)
                    if PARANOID & 16:
                        p.wall(a, p.dS, p.dG, p.sV)
                    p.sA.inc(a.activation(H4[:], pz[:], AF.Gelu,
                                          bias=p.c_zero[:, 0:1], scale=1.0))
                    gelu_cnt = p.sA.n
                    for k in range(4):
                        i = 4 * ib + k
                        j = i // 2
                        slot = j % 2
                        woff = slot * 2 * D + (i % 2) * D
                        wa_load[slot].wait(t)
                        t.wait_ge(p.sA.h, gelu_cnt)
                        for tc in range(2):
                            hsl = H4[:, k * TW + tc * 128: k * TW + (tc + 1) * 128]
                            for ob in range(2):
                                p.sT.inc(t.matmul(
                                    pot[:, tc * 1024 + ob * 512: tc * 1024 + (ob + 1) * 512],
                                    hsl,
                                    WA[:, woff + ob * 512: woff + (ob + 1) * 512],
                                    start=(i == 0), stop=(i == 127),
                                    skip_group_check=True))
                        if i % 2 == 1:
                            wa_rel[slot] = Rel(p.sT, p.sT.n)
                            if j + 2 < 64:
                                load_w2(j + 2, tg)
                p.wall(v, p.sT)
                for tc in range(2):
                    p.sV.inc(v.tensor_copy(GTOK(2 * tg + tc),
                                           pot[:, tc * 1024:(tc + 1) * 1024]))
                p.wall(t, p.sV)

        tap(p, "yb", YB[:], (p.sV,))
        p.release_slot_sems(4)
        with nc.psum_tensor(p.un("pgt"), [128, 2048], BF16) as pgt:
            for fc in range(CH):
                p.wall(t, p.sA, p.sV)
                for tc in range(16):
                    p.sT.inc(t.transpose(pgt[:, tc * 128:(tc + 1) * 128],
                                         GTOK(tc)[:, fc * 128:(fc + 1) * 128],
                                         p.eyeb[:]))
                p.wall(a, p.sT)
                p.sA.inc(a.activation(p.G[:, fc * TOK:(fc + 1) * TOK], pgt[:],
                                      AF.Silu, bias=p.c_zero[:, 0:1], scale=1.0))



def attention(p, layer_id):
    nc = p.nc
    v, s, g, a, t = nc.vector, nc.sync, nc.gpsimd, nc.scalar, nc.tensor
    d = p.dram
    l0 = (layer_id == 0)
    wq = d["w0qT"] if l0 else d["w1qT"]
    wk = d["w0kT"] if l0 else d["w1kT"]
    wv = d["w0vT"] if l0 else d["w1vT"]
    wo = d["w0oT"] if l0 else d["w1postT"]

    with ExitStack() as ph:
        A_sb = ph.enter_context(nc.sbuf_tensor(p.un("A_sb"), [128, NPAIR * TOK], BF16))
        with ExitStack() as core:
            K_sb = core.enter_context(nc.sbuf_tensor(p.un("K_sb"), [128, SEQ], BF16))
            VT_sb = core.enter_context(nc.sbuf_tensor(p.un("VT_sb"), [128, SEQ], BF16))
            Vsb = core.enter_context(nc.sbuf_tensor(p.un("Vsb"), [128, 32 * 130], BF16))
            Q_sb = core.enter_context(nc.sbuf_tensor(p.un("Q_sb"), [128, TOK], BF16))
            U = core.enter_context(nc.sbuf_tensor(p.un("U"), [128, 2 * TOK], BF16))
            PB = core.enter_context(nc.sbuf_tensor(p.un("PB"), [128, 2 * TOK], BF16))
            WQ = core.enter_context(nc.sbuf_tensor(p.un("WQs"), [128, 2 * D], BF16))
            WK = core.enter_context(nc.sbuf_tensor(p.un("WKs"), [128, 2 * D], BF16))
            WV = core.enter_context(nc.sbuf_tensor(p.un("WVs"), [128, 2 * D], BF16))
            psSc = core.enter_context(nc.psum_tensor(p.un("psSc"), [128, TOK], F32))
            psB = core.enter_context(nc.psum_tensor(p.un("psB"), [128, TOK], F32))

            wsl_rel = [None, None]
            wsl_load = [None, None]
            wssem = p.take_slot_sems(2)

            def load_pair_w(hp):
                slot = hp % 2
                if wsl_rel[slot] is not None:
                    wsl_rel[slot].wait(g)
                for wdram, wbuf in ((wq, WQ), (wk, WK), (wv, WV)):
                    wssem[slot].inc(g.dma_start(out=wbuf[:, slot * D:(slot + 1) * D],
                                                in_=wdram[:, hp * D:(hp + 1) * D]), 16)
                wsl_load[slot] = Rel(wssem[slot], wssem[slot].n)

            act_read = [None, None]
            u_free = [None, None]
            vt_free = [None, None]
            pb_free = [None, None]

            def proj(wbuf, slot, out_sb, bias_col, ntok):
                evac_rel = [None, None, None, None]
                for nb in range(ntok // 512):
                    q4 = nb % 4
                    if evac_rel[q4] is not None:
                        evac_rel[q4].wait(t)
                    else:
                        p.wall(t, p.sV)
                    for ic in range(CH):
                        p.sT.inc(t.matmul(
                            psSc[:, q4 * 512:(q4 + 1) * 512],
                            wbuf[:, slot * D + ic * 128: slot * D + (ic + 1) * 128],
                            p.XN[:, ic * SEQ + nb * 512: ic * SEQ + (nb + 1) * 512],
                            start=(ic == 0), stop=(ic == CH - 1),
                            skip_group_check=True))
                    v.wait_ge(p.sT.h, p.sT.n)
                    if bias_col is not None:
                        p.sV.inc(v.tensor_scalar(out_sb[:, nb * 512:(nb + 1) * 512],
                                                 psSc[:, q4 * 512:(q4 + 1) * 512],
                                                 bias_col, None, ALU.add))
                    else:
                        p.sV.inc(v.tensor_copy(out_sb[:, nb * 512:(nb + 1) * 512],
                                               psSc[:, q4 * 512:(q4 + 1) * 512]))
                    evac_rel[q4] = Rel(p.sV, p.sV.n)

            dma_fence(p)
            load_pair_w(0)
            for hp in range(NPAIR):
                slot = hp % 2
                wsl_load[slot].wait(t)
                if hp + 1 < NPAIR:
                    load_pair_w(hp + 1)
                p.wall(t, p.sA)
                proj(WK, slot, K_sb, p.b0k[:, hp:hp + 1] if l0 else None, SEQ)
                proj(WV, slot, VT_sb, p.b0v[:, hp:hp + 1] if l0 else None, SEQ)
                proj(WQ, slot, Q_sb, p.b0q[:, hp:hp + 1] if l0 else None, TOK)
                wsl_rel[slot] = Rel(p.sT, p.sT.n)
                psBb = psB[:].bitcast(BF16)
                p.wall(t, p.sV)
                for tc in range(32):
                    p.sT.inc(t.transpose(psBb[:, tc * 128:(tc + 1) * 128],
                                         VT_sb[:, tc * 128:(tc + 1) * 128], p.eyeb[:]))
                p.wall(v, p.sT)
                v.memset(Vsb[:], 1.0)
                src3 = psBb[:, 0:4096].rearrange("q (tc h c) -> q tc h c",
                                                 tc=32, h=2, c=64)
                dst3 = Vsb[:].rearrange("q (tc h c) -> q tc h c",
                                        tc=32, h=2, c=65)[:, :, :, 0:64]
                p.sV.inc(v.tensor_copy(dst3, src3))
                vsb_ready = p.sV.n
                for hh in range(2):
                    rb = 64 * hh
                    pend = None
                    for kc in range(32):
                        ksl = K_sb[rb:rb + 64, kc * 128:(kc + 1) * 128]
                        for qh in range(2):
                            it = (kc * 2 + qh)
                            sb = it % 2
                            q0 = qh * 1024
                            if act_read[sb] is not None:
                                act_read[sb].wait(t)
                            if kc == 0 and qh == 0 and hh == 0:
                                t.wait_ge(p.sV.h, vsb_ready)
                            for nb in range(2):
                                p.sT.inc(t.matmul(
                                    psSc[:, sb * 1024 + nb * 512: sb * 1024 + (nb + 1) * 512],
                                    ksl,
                                    Q_sb[rb:rb + 64, q0 + nb * 512: q0 + (nb + 1) * 512],
                                    start=True, stop=True, skip_group_check=True))
                            sc_done = p.sT.n
                            a.wait_ge(p.sT.h, sc_done)
                            scsl = psSc[:, sb * 1024:(sb + 1) * 1024]
                            if l0:
                                if u_free[sb] is not None:
                                    u_free[sb].wait(a)
                                p.sA.inc(a.activation(U[:, sb * 1024:(sb + 1) * 1024],
                                                      scsl, AF.Exp,
                                                      bias=p.c_ln2[:, 0:1], scale=0.125))
                                if vt_free[sb] is not None:
                                    vt_free[sb].wait(a)
                                p.sA.inc(a.activation(p.TMP[:, sb * 1024:(sb + 1) * 1024],
                                                      scsl, AF.Tanh,
                                                      bias=p.c_mu2[:, 0:1], scale=0.0625))
                                act_read[sb] = Rel(p.sA, p.sA.n)
                                v.wait_ge(p.sA.h, p.sA.n)
                                if pb_free[sb] is not None:
                                    pb_free[sb].wait(v)
                                p.sV.inc(v.scalar_tensor_tensor(
                                    PB[:, sb * 1024:(sb + 1) * 1024],
                                    p.TMP[:, sb * 1024:(sb + 1) * 1024], 1.0,
                                    U[:, sb * 1024:(sb + 1) * 1024],
                                    ALU.add, ALU.mult))
                                u_free[sb] = Rel(p.sV, p.sV.n)
                                vt_free[sb] = Rel(p.sV, p.sV.n)
                                pready = Rel(p.sV, p.sV.n)
                            else:
                                if pb_free[sb] is not None:
                                    pb_free[sb].wait(a)
                                p.sA.inc(a.activation(PB[:, sb * 1024:(sb + 1) * 1024],
                                                      scsl, AF.Exp,
                                                      bias=p.c_zero[:, 0:1], scale=0.125))
                                act_read[sb] = Rel(p.sA, p.sA.n)
                                pready = Rel(p.sA, p.sA.n)
                            if pend is not None:
                                _vsl, _sb, _q0, _st, _sp, _pr = pend
                                _pr.wait(t)
                                for nb in range(2):
                                    p.sT.inc(t.matmul(
                                        psB[0:65, _q0 + nb * 512: _q0 + (nb + 1) * 512],
                                        _vsl,
                                        PB[:, _sb * 1024 + nb * 512: _sb * 1024 + (nb + 1) * 512],
                                        start=_st, stop=_sp, skip_group_check=True))
                                pb_free[_sb] = Rel(p.sT, p.sT.n)
                            vsl = Vsb[:, kc * 130 + hh * 65: kc * 130 + hh * 65 + 65]
                            pend = (vsl, sb, q0, kc == 0, kc == 31, pready)
                    _vsl, _sb, _q0, _st, _sp, _pr = pend
                    _pr.wait(t)
                    for nb in range(2):
                        p.sT.inc(t.matmul(
                            psB[0:65, _q0 + nb * 512: _q0 + (nb + 1) * 512], _vsl,
                            PB[:, _sb * 1024 + nb * 512: _sb * 1024 + (nb + 1) * 512],
                            start=_st, stop=_sp, skip_group_check=True))
                    pb_free[_sb] = Rel(p.sT, p.sT.n)
                    p.wall(v, p.sT)
                    p.sV.inc(v.tensor_copy(A_sb[rb:rb + 64, hp * TOK:(hp + 1) * TOK],
                                           psB[0:64, :]))
                    rrow = p.SR[(R_A if hh == 0 else R_B):(R_A if hh == 0 else R_B) + 1, :]
                    p.sV.inc(v.reciprocal(rrow, psB[64:65, :]))
                    p.wall(t, p.sV)
                p.wall(t, p.sV, p.sA)
                bcast_row(p, psSc, p.SR[R_A:R_A + 1, :], TOK)
                p.wall(v, p.sT)
                p.sV.inc(v.tensor_tensor(A_sb[0:64, hp * TOK:(hp + 1) * TOK],
                                         A_sb[0:64, hp * TOK:(hp + 1) * TOK],
                                         psSc[0:64, :], ALU.mult))
                p.wall(t, p.sV)
                bcast_row(p, psSc, p.SR[R_B:R_B + 1, :], TOK)
                p.wall(v, p.sT)
                p.sV.inc(v.tensor_tensor(A_sb[64:128, hp * TOK:(hp + 1) * TOK],
                                         A_sb[64:128, hp * TOK:(hp + 1) * TOK],
                                         psSc[64:128, :], ALU.mult))
                p.wall(t, p.sV)
            p.release_slot_sems(2)
            if not l0:
                p.wall(a, p.sV)
                for hp in range(NPAIR):
                    p.sA.inc(a.activation(A_sb[:, hp * TOK:(hp + 1) * TOK],
                                          A_sb[:, hp * TOK:(hp + 1) * TOK],
                                          AF.Gelu, bias=p.c_zero[:, 0:1], scale=1.0))

        out_dram = p.dram["x_dram"] if l0 else p.dram["outT"]
        xin_dram = p.dram["xT"] if l0 else p.dram["x_dram"]
        with ExitStack() as ep:
            WO = ep.enter_context(nc.sbuf_tensor(p.un("WOs"), [128, 2 * D], BF16))
            XB = ep.enter_context(nc.sbuf_tensor(p.un("epxb"), [128, 2 * TOK], F32))
            T2 = ep.enter_context(nc.sbuf_tensor(p.un("ept2"), [128, TOK], F32))
            psW = ep.enter_context(nc.psum_tensor(p.un("psW"), [128, TOK], F32))
            wo_rel = [None, None]
            wo_load = [None, None]
            wosem = p.take_slot_sems(2)

            def load_wo(oc):
                slot = oc % 2
                if wo_rel[slot] is not None:
                    wo_rel[slot].wait(g)
                wosem[slot].inc(g.dma_start(out=WO[:, slot * D:(slot + 1) * D],
                                            in_=wo[:, oc * D:(oc + 1) * D]), 16)
                wo_load[slot] = Rel(wosem[slot], wosem[slot].n)

            dma_fence(p)
            load_wo(0)
            for oc in range(CH):
                slot = oc % 2
                wo_load[slot].wait(t)
                if oc + 1 < CH:
                    load_wo(oc + 1)
                xsl = XB[:, slot * TOK:(slot + 1) * TOK]
                p.wall(s, p.sV)
                p.dS.inc(s.dma_start(out=xsl, in_=xin_dram[:, oc * TOK:(oc + 1) * TOK]), 16)
                p.wall(t, p.sV, p.sA)
                for nb in range(4):
                    for ic in range(CH):
                        p.sT.inc(t.matmul(
                            psW[:, nb * 512:(nb + 1) * 512],
                            WO[:, slot * D + ic * 128: slot * D + (ic + 1) * 128],
                            A_sb[:, ic * TOK + nb * 512: ic * TOK + (nb + 1) * 512],
                            start=(ic == 0), stop=(ic == CH - 1),
                            skip_group_check=True))
                wo_rel[slot] = Rel(p.sT, p.sT.n)
                p.wall(v, p.sT, p.dS)
                if l0:
                    p.sV.inc(v.scalar_tensor_tensor(p.TMP[:], psW[:], p.b0o[:, oc:oc + 1],
                                                    p.G[:, oc * TOK:(oc + 1) * TOK],
                                                    ALU.add, ALU.mult))
                    p.sV.inc(v.tensor_add(T2[:], xsl, p.TMP[:]))
                else:
                    p.sV.inc(v.tensor_tensor(p.TMP[:], psW[:],
                                             p.XN[:, oc * SEQ: oc * SEQ + TOK], ALU.add))
                    p.wall(a, p.sV)
                    p.sA.inc(a.activation(T2[:], p.G[:, oc * TOK:(oc + 1) * TOK],
                                          AF.Silu, bias=p.c_zero[:, 0:1], scale=1.0))
                    p.wall(v, p.sA)
                    p.sV.inc(v.tensor_mul(p.TMP[:], p.TMP[:], T2[:]))
                    p.sV.inc(v.tensor_add(T2[:], xsl, p.TMP[:]))
                p.wall(s, p.sV)
                p.dO.inc(s.dma_start(out=out_dram[:, oc * TOK:(oc + 1) * TOK], in_=T2[:]), 16)
                p.wall(v, p.dO)
            p.release_slot_sems(2)



def gating_moe(p):
    nc = p.nc
    v, s, g, a, t = nc.vector, nc.sync, nc.gpsimd, nc.scalar, nc.tensor
    d = p.dram

    with ExitStack() as ph:
        GT = ph.enter_context(nc.sbuf_tensor(p.un("GT"), [128, CH * NE], BF16))
        LG = ph.enter_context(nc.sbuf_tensor(p.un("LG"), [8, TOK], F32))
        LGt = ph.enter_context(nc.sbuf_tensor(p.un("LGt"), [128, 128], F32))
        MX = ph.enter_context(nc.sbuf_tensor(p.un("MX"), [128, 16], F32))
        RW = ph.enter_context(nc.sbuf_tensor(p.un("RW"), [128, 16], F32))
        TS = ph.enter_context(nc.sbuf_tensor(p.un("TS"), [128, 128], F32))
        dma_fence(p)
        p.dG.inc(g.dma_start(out=GT[:], in_=d["gateT"][:]), 16)
        gt_rel = Rel(p.dG, p.dG.n)

        with nc.psum_tensor(p.un("psG"), [32, TOK], F32) as psG:
            gt_rel.wait(t)
            p.wall(t, p.sV)
            for nb in range(4):
                for ic in range(CH):
                    p.sT.inc(t.matmul(psG[0:8, nb * 512:(nb + 1) * 512],
                                      GT[:, ic * 8:(ic + 1) * 8],
                                      p.XN[:, ic * SEQ + nb * 512: ic * SEQ + (nb + 1) * 512],
                                      start=(ic == 0), stop=(ic == CH - 1),
                                      skip_group_check=True))
            p.wall(v, p.sT)
            p.sV.inc(v.tensor_copy(LG[:], psG[0:8, :]))
        with nc.psum_tensor(p.un("psLt"), [128, 128], F32) as psLt:
            p.wall(t, p.sV)
            for tc in range(16):
                p.sT.inc(t.transpose(psLt[:, tc * 8:(tc + 1) * 8],
                                     LG[0:8, tc * 128:(tc + 1) * 128],
                                     p.eyef[0:8, 0:8]))
            p.wall(v, p.sT)
            p.sV.inc(v.tensor_copy(LGt[:], psLt[:]))
        for tc in range(16):
            p.sV.inc(v.reduce_max(MX[:, tc:tc + 1], LGt[:, tc * 8:(tc + 1) * 8], axis=AX.X))
        v.drain()
        p.wall(a, p.sV)
        p.sA.inc(a.activation(RW[:], MX[:], AF.Sigmoid, bias=p.c_zero[:, 0:1], scale=1.0))
        p.wall(v, p.sA)
        v.drain()
        for tc in range(16):
            p.sV.inc(v.tensor_scalar(TS[:, tc * 8:(tc + 1) * 8], LGt[:, tc * 8:(tc + 1) * 8],
                                     MX[:, tc:tc + 1], RW[:, tc:tc + 1],
                                     ALU.is_equal, ALU.mult))

        p.wall(v, p.sA)
        v.memset(p.G[:], 0.0)
        p.sV.inc(v.engine_nop())
        XS = ph.enter_context(nc.sbuf_tensor(p.un("XS"), [128, CH * TOK], BF16))
        PRD = ph.enter_context(nc.sbuf_tensor(p.un("PRD"), [128, TOK], BF16))
        WG = ph.enter_context(nc.sbuf_tensor(p.un("WGs"), [128, 2 * 4096], BF16))
        WU = ph.enter_context(nc.sbuf_tensor(p.un("WUs"), [128, 2 * 4096], BF16))
        WD = ph.enter_context(nc.sbuf_tensor(p.un("WDs"), [128, 2 * 4096], BF16))
        we_rel = [None, None]
        we_load = [None, None]
        wesem = p.take_slot_sems(2)

        def load_we(e):
            slot = e % 2
            if we_rel[slot] is not None:
                we_rel[slot].wait(g)
            for wdram, wbuf in ((d["wgT"], WG), (d["wuT"], WU), (d["wdT"], WD)):
                wesem[slot].inc(g.dma_start(out=wbuf[:, slot * 4096:(slot + 1) * 4096],
                                            in_=wdram[e]), 16)
            we_load[slot] = Rel(wesem[slot], wesem[slot].n)

        dma_fence(p)
        load_we(0)
        for e in range(NE):
            slot = e % 2
            we_load[slot].wait(t)
            if e + 1 < NE:
                load_we(e + 1)
            with nc.psum_tensor(p.un("pRow"), [1, TOK], F32) as pRow:
                p.wall(t, p.sV)
                for tc in range(16):
                    p.sT.inc(t.transpose(pRow[0:1, tc * 128:(tc + 1) * 128],
                                         TS[:, tc * 8 + e: tc * 8 + e + 1], p.eyef[:]))
                p.wall(v, p.sT)
                p.sV.inc(v.tensor_copy(p.SR[R_A:R_A + 1, :], pRow[0:1, :]))
            with nc.psum_tensor(p.un("pB2"), [128, TOK], F32) as pB2:
                p.wall(t, p.sV)
                bcast_row(p, pB2, p.SR[R_A:R_A + 1, :], TOK)
                p.wall(v, p.sT)
                for c in range(CH):
                    p.sV.inc(v.tensor_tensor(XS[:, c * TOK:(c + 1) * TOK],
                                             p.XN[:, c * SEQ: c * SEQ + TOK],
                                             pB2[:], ALU.mult))
            for tt in range(4):
                t0 = tt * 512
                with nc.psum_tensor(p.un("pGm"), [128, 2048], F32) as pG, \
                     nc.psum_tensor(p.un("pUm"), [128, 2048], F32) as pU:
                    p.wall(t, p.sV, p.sA)
                    for oc in range(4):
                        for ic in range(CH):
                            wof = slot * 4096 + ic * 512 + oc * 128
                            p.sT.inc(t.matmul(pG[:, oc * 512:(oc + 1) * 512],
                                              WG[:, wof:wof + 128],
                                              XS[:, ic * TOK + t0: ic * TOK + t0 + 512],
                                              start=(ic == 0), stop=(ic == CH - 1),
                                              skip_group_check=True))
                            p.sT.inc(t.matmul(pU[:, oc * 512:(oc + 1) * 512],
                                              WU[:, wof:wof + 128],
                                              XS[:, ic * TOK + t0: ic * TOK + t0 + 512],
                                              start=(ic == 0), stop=(ic == CH - 1),
                                              skip_group_check=True))
                    p.wall(a, p.sT)
                    p.sA.inc(a.activation(p.TMP[:], pG[:], AF.Silu,
                                          bias=p.c_zero[:, 0:1], scale=1.0))
                    p.wall(v, p.sA, p.sT)
                    p.sV.inc(v.scalar_tensor_tensor(PRD[:], pU[:], 1.0, p.TMP[:],
                                                    ALU.bypass, ALU.mult))
                with nc.psum_tensor(p.un("pDm"), [128, 4096], F32) as pD:
                    p.wall(t, p.sV)
                    for oc in range(CH):
                        for ic in range(4):
                            wof = slot * 4096 + ic * D + oc * 128
                            p.sT.inc(t.matmul(pD[:, oc * 512:(oc + 1) * 512],
                                              WD[:, wof:wof + 128],
                                              PRD[:, ic * 512:(ic + 1) * 512],
                                              start=(ic == 0), stop=(ic == 3),
                                              skip_group_check=True))
                    p.wall(v, p.sT)
                    gview = p.G[:].rearrange("q (c tk) -> q c tk", c=CH)[:, :, t0:t0 + 512]
                    dview = pD[:].rearrange("q (c tk) -> q c tk", c=CH)
                    p.sV.inc(v.tensor_tensor(gview, gview, dview, ALU.add))
            we_rel[slot] = Rel(p.sT, p.sT.n)


class StopEmission(Exception):
    pass


def _ck(p, name):
    if STOP_AFTER and name == STOP_AFTER:
        raise StopEmission()


def layer(p, lid):
    ln_phase(p, lid, inner=False)
    tap(p, f"xn{lid}", p.XN[:, 0:TOK], (p.sV,))
    _ck(p, f"ln{lid}")
    exchange_export(p, lid)
    _ck(p, f"exp{lid}")
    if lid == 0:
        ffn0(p)
    else:
        gating_moe(p)
    tap(p, f"g{lid}", p.G[:, 0:TOK], (p.sV, p.sA))
    _ck(p, f"ffn{lid}")
    exchange_import(p, lid)
    _ck(p, f"imp{lid}")
    if lid == 0:
        ln_phase(p, lid, inner=True)
        tap(p, "xni0", p.XN[:, 0:TOK], (p.sV,))
        _ck(p, "lni0")
    attention(p, lid)
    if lid == 0:
        tap(p, "x1", p.dram["x_dram"][:, 0:TOK], (p.dO,))
    _ck(p, f"att{lid}")


def finale(p):
    nc = p.nc
    p.wall(nc.sync, p.dO)
    p.wall(nc.sync, p.dS)



def _lay_cm(w):
    r, n = w.shape
    ch = r // 128
    return np.ascontiguousarray(w.reshape(ch, 128, n).transpose(1, 0, 2).reshape(128, ch * n))


def _lay_pm(w_io):
    b = w_io.reshape(8, 128, 8, 128)
    return np.ascontiguousarray(b.transpose(1, 2, 0, 3).reshape(128, 8192))


def _lay_col(vec):
    return np.ascontiguousarray(vec.reshape(8, 128).T)


def _build_rails(rm, rup):
    ytiles = np.zeros((32, 128, 128), np.float32)
    for h in range(4):
        for j in range(8):
            tl = ytiles[h * 8 + j]
            for a_ in range(8):
                r = 8 * j + a_
                tl[16 * a_:16 * a_ + 16, 16 * a_:16 * a_ + 16] = rm[h, r]
    ztiles = np.zeros((128, 128, 128), np.float32)
    for ci in range(128):
        h = ci // 32
        rr = ci % 32
        tl = ztiles[ci]
        for l_ in range(2):
            r = 2 * rr + l_
            tl[16 * (r % 8):16 * (r % 8) + 16, 64 * l_:64 * l_ + 64] = rup[h, r].T
    layY = np.ascontiguousarray(ytiles.transpose(1, 0, 2).reshape(128, 32 * 128))
    layZ = np.ascontiguousarray(ztiles.transpose(1, 0, 2).reshape(128, 128 * 128))
    return layY, layZ


def kernel(**inputs):
    global LAST_RESULTS
    f = lambda k: np.asarray(inputs[k], dtype=np.float32)

    x = f("L0_attn_sequence")
    shared = {
        "w0qT": _lay_pm(np.ascontiguousarray(f("L0_attn_Wq").T)),
        "w0kT": _lay_pm(np.ascontiguousarray(f("L0_attn_Wk").T)),
        "w0vT": _lay_pm(np.ascontiguousarray(f("L0_attn_Wv").T)),
        "w0oT": _lay_pm(np.ascontiguousarray(f("L0_attn_Wo").T)),
        "b0q": _lay_col(f("L0_attn_bq")),
        "b0k": _lay_col(f("L0_attn_bk")),
        "b0v": _lay_col(f("L0_attn_bv")),
        "b0o": _lay_col(f("L0_attn_bo")),
        "gam": _lay_col(f("L0_attn_ln_gamma")),
        "bet": _lay_col(f("L0_attn_ln_beta")),
        "mu": f("L0_attn_mu").reshape(1, 1),
        "cprojT": _lay_cm(np.ascontiguousarray(f("L0_ffn_w_cproj").T)),
        "w1qT": _lay_pm(f("L1_attn_Wq")),
        "w1kT": _lay_pm(f("L1_attn_Wk")),
        "w1vT": _lay_pm(f("L1_attn_Wv")),
        "w1postT": _lay_pm(f("L1_attn_Wpost")),
        "gateT": _lay_cm(np.ascontiguousarray(f("L1_ffn_gate_w").T)),
        "wgT": np.stack([_lay_cm(np.ascontiguousarray(f("L1_ffn_w_gate")[e].T))
                         for e in range(NE)]),
        "wuT": np.stack([_lay_cm(np.ascontiguousarray(f("L1_ffn_w_up")[e].T))
                         for e in range(NE)]),
        "wdT": np.stack([_lay_cm(np.ascontiguousarray(f("L1_ffn_w_down")[e].T))
                         for e in range(NE)]),
        "eye": np.eye(128, dtype=np.float32),
    }
    shared["railsY"], shared["railsZ"] = _build_rails(
        f("L0_ffn_rail_maker"), f("L0_ffn_rail_up_proj"))

    nc = build_program(taps=DEBUG_TAPS, emit_upto=EMIT_UPTO)

    in_maps = []
    for c in range(8):
        b, par = c // 2, c % 2
        xs = np.ascontiguousarray(x[b, par * TOK:(par + 1) * TOK, :].T)
        m = dict(shared)
        m["xT"] = _lay_cm(xs)
        m["mlo"] = np.full((128, 1), float(par), np.float32)
        m["mhi"] = np.full((128, 1), float(1 - par), np.float32)
        in_maps.append(m)

    res = run_bass_kernel_spmd(nc, in_maps, core_ids=list(range(8)))
    LAST_RESULTS = res.results

    out = np.zeros((4, SEQ, D), np.float32)
    for c in range(8):
        b, par = c // 2, c % 2
        o = res.results[c]["outT"]
        o = o.reshape(128, CH, TOK).transpose(1, 0, 2).reshape(D, TOK).T
        out[b, par * TOK:(par + 1) * TOK, :] = o
    return out


# version 05
# speedup vs baseline: 1.5816x; 1.2649x over previous
import numpy as np
from contextlib import ExitStack

import concourse.bass as bass
import concourse.mybir as mybir
from concourse.bass_utils import run_bass_kernel_spmd

F32 = mybir.dt.float32
BF16 = mybir.dt.bfloat16
AF = mybir.ActivationFunctionType
ALU = mybir.AluOpType
AX = mybir.AxisListType

D = 1024
CH = 8
TOK = 2048
SEQ = 4096
NH = 16
NPAIR = 8
NE = 8
FCH = 128

LOG2 = float(np.log(2.0))

DEBUG_TAPS = {}
EMIT_UPTO = 2
STOP_AFTER = ""
PARANOID = 0
NO_BOUNCE = False
LAST_RESULTS = None

R_A = 0
R_B = 32
R_V = 64
R_M = 96


class Sem:
    def __init__(self, ctx, nc, name):
        self.h = ctx.enter_context(nc.semaphore(name))
        self.n = 0

    def inc(self, instr, k=1):
        instr.then_inc(self.h, k)
        self.n += k
        return instr

    def wait(self, eng, n=None):
        eng.wait_ge(self.h, self.n if n is None else n)


class Rel:
    def __init__(self, sem, n):
        self.sem = sem
        self.n = n

    def wait(self, eng):
        eng.wait_ge(self.sem.h, self.n)


class P:
    def take_slot_sems(self, n):
        out = self.slot_sems[self._slot_next:self._slot_next + n]
        assert len(out) == n, "out of slot sems"
        self._slot_next += n
        return out

    def release_slot_sems(self, n):
        self._slot_next -= n

    def un(self, base):
        self._uc = getattr(self, "_uc", 0) + 1
        return f"{base}_{self._uc}"


def build_program(taps=None, emit_upto=2):
    nc = bass.Bass("TRN2", target_bir_lowering=False, debug=False)
    st = ExitStack()
    p = P()
    p.nc = nc
    p.st = st
    p.taps = dict(taps or {})

    dram = {}

    def din(name, shape, dtype=F32):
        dram[name] = nc.dram_tensor(name, shape, dtype, kind="ExternalInput").ap()

    def dint(name, shape, dtype=F32):
        dram[name] = nc.dram_tensor(name, shape, dtype, kind="Internal").ap()

    def dout(name, shape, dtype=F32):
        dram[name] = nc.dram_tensor(name, shape, dtype, kind="ExternalOutput").ap()

    din("xT", [128, CH * TOK])
    for nm in ("w0qT", "w0kT", "w0vT", "w0oT", "w1qT", "w1kT", "w1vT", "w1postT"):
        din(nm, [128, NPAIR * D])
    for nm in ("b0q", "b0k", "b0v", "b0o", "gam", "bet"):
        din(nm, [128, 8])
    din("mu", [1, 1])
    din("railsY", [128, 32 * 128])
    din("railsZ", [128, 128 * 128])
    din("cprojT", [128, FCH * D])
    din("gateT", [128, CH * NE])
    din("wgT", [NE, 128, CH * 512])
    din("wuT", [NE, 128, CH * 512])
    din("wdT", [NE, 128, 4 * D])
    din("mlo", [128, 1])
    din("mhi", [128, 1])
    din("eye", [128, 128])
    dint("x_dram", [128, CH * TOK])
    dint("cc_in0", [128, CH * TOK], BF16)
    dint("cc_out0", [256, CH * TOK], BF16)
    dint("cc_in1", [128, CH * TOK], BF16)
    dint("cc_out1", [256, CH * TOK], BF16)
    dint("cprojT_bf", [128, FCH * D], BF16)
    dout("outT", [128, CH * TOK])
    p.dram = dram

    p.tap_out = {}
    for tname, shape in p.taps.items():
        dout("tap_" + tname, list(shape), F32)
        p.tap_out[tname] = dram["tap_" + tname]

    p.dS = Sem(st, nc, "dS")
    p.dG = Sem(st, nc, "dG")
    p.sT = Sem(st, nc, "sT")
    p.sA = Sem(st, nc, "sA")
    p.sV = Sem(st, nc, "sV")
    p.sC = Sem(st, nc, "sC")
    p.dO = Sem(st, nc, "dO")
    p.slot_sems = [Sem(st, nc, f"sl{i}") for i in range(12)]
    p._slot_next = 0

    def wall(eng, *sems):
        for s_ in sems:
            s_.wait(eng)
    p.wall = wall

    sb = st.enter_context
    p.XN = sb(nc.sbuf_tensor("XN", [128, CH * SEQ], BF16))
    p.G = sb(nc.sbuf_tensor("G", [128, CH * TOK], BF16))
    p.TMP = sb(nc.sbuf_tensor("TMP", [128, TOK], F32))
    p.SR = sb(nc.sbuf_tensor("SR", [128, TOK], F32))
    p.c_ones_b = sb(nc.sbuf_tensor("c_ones_b", [128, 1], BF16))
    p.c_onesr_f = sb(nc.sbuf_tensor("c_onesr_f", [128, 128], F32))
    p.c_ln2 = sb(nc.sbuf_tensor("c_ln2", [128, 1], F32))
    p.c_mu2 = sb(nc.sbuf_tensor("c_mu2", [128, 1], F32))
    p.c_zero = sb(nc.sbuf_tensor("c_zero", [128, 1], F32))
    p.c_eps1 = sb(nc.sbuf_tensor("c_eps1", [128, 1], F32))
    p.c_eps2 = sb(nc.sbuf_tensor("c_eps2", [128, 1], F32))
    p.b0q = sb(nc.sbuf_tensor("b0q_s", [128, 8], F32))
    p.b0k = sb(nc.sbuf_tensor("b0k_s", [128, 8], F32))
    p.b0v = sb(nc.sbuf_tensor("b0v_s", [128, 8], F32))
    p.b0o = sb(nc.sbuf_tensor("b0o_s", [128, 8], F32))
    p.gam = sb(nc.sbuf_tensor("gam_s", [128, 8], F32))
    p.bet = sb(nc.sbuf_tensor("bet_s", [128, 8], F32))
    p.mlo = sb(nc.sbuf_tensor("mlo_s", [128, 1], F32))
    p.mhi = sb(nc.sbuf_tensor("mhi_s", [128, 1], F32))
    p.eyeb = sb(nc.sbuf_tensor("eyeb", [128, 128], BF16))
    p.eyef = sb(nc.sbuf_tensor("eyef", [128, 128], F32))

    prologue(p)
    try:
        for lid in range(min(2, emit_upto)):
            layer(p, lid)
    except StopEmission:
        pass
    finale(p)

    st.close()
    return nc


def tap(p, name, src_ap, sems):
    if name not in p.tap_out:
        return
    nc = p.nc
    eng = nc.gpsimd if src_ap.dtype != F32 else nc.sync
    p.wall(eng, *sems)
    p.dO.inc(eng.dma_start(out=p.tap_out[name][:], in_=src_ap), 16)


def prologue(p):
    nc = p.nc
    v, s, g, a, t = nc.vector, nc.sync, nc.gpsimd, nc.scalar, nc.tensor
    d = p.dram
    v.memset(p.c_ones_b[:], 1.0)
    v.memset(p.c_onesr_f[:], 1.0)
    v.memset(p.c_ln2[:], -LOG2)
    v.memset(p.c_zero[:], 0.0)
    v.memset(p.c_eps1[:], 1e-5)
    v.memset(p.c_eps2[:], 1e-6)
    p.sV.inc(v.engine_nop())
    for nm, t_sb in (("b0q", p.b0q), ("b0k", p.b0k), ("b0v", p.b0v), ("b0o", p.b0o),
                     ("gam", p.gam), ("bet", p.bet), ("mlo", p.mlo), ("mhi", p.mhi),
                     ("eye", p.eyef)):
        p.dS.inc(s.dma_start(out=t_sb[:], in_=d[nm][:]), 16)
    p.dG.inc(g.dma_start(out=p.eyeb[:], in_=d["eye"][:]), 16)
    mut = p.st.enter_context(nc.sbuf_tensor("mut", [1, 1], F32))
    with nc.psum_tensor("pmu", [128, 1], F32) as pmu:
        p.dS.inc(s.dma_start(out=mut[:], in_=d["mu"][:]), 16)
        p.wall(t, p.dS, p.sV)
        p.sT.inc(t.matmul(pmu[:], p.c_onesr_f[0:1, :], mut[:], start=True, stop=True))
        p.wall(a, p.sT, p.sV)
        p.sA.inc(a.activation(p.c_mu2[:], pmu[:], AF.Copy, bias=0.0, scale=-0.5))
        a.drain()


def dma_fence(p):
    nc = p.nc
    p.wall(nc.sync, p.sT, p.sA, p.sV)
    p.wall(nc.gpsimd, p.sT, p.sA, p.sV)


def bcast_row(p, psdst, row_ap, width):
    t = p.nc.tensor
    rb = row_ap.base_partition()
    for nb in range(width // 512):
        p.sT.inc(t.matmul(psdst[:, nb * 512:(nb + 1) * 512],
                          p.c_onesr_f[rb:rb + 1, :],
                          row_ap[0:1, nb * 512:(nb + 1) * 512],
                          start=True, stop=True, skip_group_check=True))


def ln_phase(p, layer_id, inner):
    nc = p.nc
    v, s, a, t = nc.vector, nc.sync, nc.scalar, nc.tensor
    nhalf = 2 if inner else 1
    src = p.dram["xT"] if layer_id == 0 else p.dram["x_dram"]

    p.wall(v, p.dS, p.sT)
    p.wall(s, p.dO)
    dma_fence(p)
    with ExitStack() as ph:
        SQ = ph.enter_context(nc.sbuf_tensor(p.un("lnsq"), [128, TOK], BF16))
        XB = None
        if not inner:
            XB = ph.enter_context(nc.sbuf_tensor(p.un("lnxb"), [128, 2 * TOK], F32))
        for half in range(nhalf):
            hs = half * TOK
            pstat_cm = nc.psum_tensor(p.un("pstat"), [64, TOK], F32)
            pstat = pstat_cm.__enter__()
            for c in range(CH):
                if not inner:
                    xsl = XB[:, (c % 2) * TOK:(c % 2 + 1) * TOK]
                    p.wall(s, p.sV)
                    p.dS.inc(s.dma_start(out=xsl, in_=src[:, c * TOK:(c + 1) * TOK]), 16)
                    p.wall(v, p.dS)
                    xb = p.XN[:, c * SEQ + TOK: c * SEQ + 2 * TOK]
                    p.sV.inc(v.tensor_copy(xb, xsl))
                else:
                    xb = p.XN[:, c * SEQ + hs: c * SEQ + hs + TOK]
                p.wall(v, p.sT)
                p.sV.inc(v.tensor_mul(SQ[:], xb, xb))
                p.wall(t, p.sV)
                for nb in range(4):
                    n0 = nb * 512
                    p.sT.inc(t.matmul(pstat[0:1, n0:n0 + 512], p.c_ones_b[:],
                                      xb[:, n0:n0 + 512], start=(c == 0),
                                      stop=(c == CH - 1), skip_group_check=True))
                    p.sT.inc(t.matmul(pstat[32:33, n0:n0 + 512], p.c_ones_b[:],
                                      SQ[:, n0:n0 + 512], start=(c == 0),
                                      stop=(c == CH - 1), skip_group_check=True))
            p.wall(v, p.sT)
            mrow = p.SR[R_M:R_M + 1, :]
            vrow = p.SR[R_V:R_V + 1, :]
            arow = p.SR[R_A:R_A + 1, :]
            brow = p.SR[R_B:R_B + 1, :]
            p.sV.inc(v.tensor_scalar_mul(mrow, pstat[0:1, :], 1.0 / D))
            p.sV.inc(v.tensor_mul(vrow, mrow, mrow))
            p.sV.inc(v.scalar_tensor_tensor(vrow, pstat[32:33, :], 1.0 / D, vrow,
                                            ALU.mult, ALU.subtract))
            epsc = p.c_eps2 if inner else p.c_eps1
            p.wall(a, p.sV)
            p.sA.inc(a.activation(brow, vrow, AF.Ln, bias=epsc[0:1, 0:1], scale=1.0))
            a.drain()
            p.sA.inc(a.activation(arow, brow, AF.Exp, bias=p.c_zero[0:1, 0:1], scale=-0.5))
            p.wall(v, p.sA)
            p.sV.inc(v.scalar_tensor_tensor(brow, pstat[0:1, :], -1.0 / D, arow,
                                            ALU.mult, ALU.mult))
            pstat_cm.__exit__(None, None, None)
            pbca_cm = nc.psum_tensor(p.un("pbca"), [128, TOK], F32)
            pbcb_cm = nc.psum_tensor(p.un("pbcb"), [128, TOK], F32)
            pbca = pbca_cm.__enter__()
            pbcb = pbcb_cm.__enter__()
            p.wall(t, p.sV)
            bcast_row(p, pbca, arow, TOK)
            bcast_row(p, pbcb, brow, TOK)
            for c in range(CH):
                if inner:
                    xb = p.XN[:, c * SEQ + hs: c * SEQ + hs + TOK]
                    xout = xb
                else:
                    xb = p.XN[:, c * SEQ + TOK: c * SEQ + 2 * TOK]
                    xout = p.XN[:, c * SEQ: c * SEQ + TOK]
                p.wall(v, p.sT)
                p.sV.inc(v.tensor_tensor(p.TMP[:], xb, pbca[:], ALU.mult))
                if inner:
                    p.sV.inc(v.tensor_tensor(p.TMP[:], p.TMP[:], pbcb[:], ALU.add))
                    p.sV.inc(v.tensor_scalar(xout, p.TMP[:], p.gam[:, c:c + 1],
                                             p.bet[:, c:c + 1], ALU.mult, ALU.add))
                else:
                    p.sV.inc(v.tensor_tensor(xout, p.TMP[:], pbcb[:], ALU.add))
            p.wall(t, p.sV)
            pbcb_cm.__exit__(None, None, None)
            pbca_cm.__exit__(None, None, None)


def exchange_export(p, layer_id):
    nc = p.nc
    s, g = nc.sync, nc.gpsimd
    cc_in = p.dram[f"cc_in{layer_id}"]
    cc_out = p.dram[f"cc_out{layer_id}"]
    p.wall(s, p.sV)
    for c in range(CH):
        p.dS.inc(s.dma_start(out=cc_in[:, c * TOK:(c + 1) * TOK],
                             in_=p.XN[:, c * SEQ: c * SEQ + TOK]), 16)
    p.wall(g, p.dS)
    p.sC.inc(g.collective_compute(
        "AllGather", ALU.bypass, ins=[cc_in[:]], outs=[cc_out[:]],
        replica_groups=[[0, 1], [2, 3], [4, 5], [6, 7]]))


def exchange_import(p, layer_id):
    nc = p.nc
    s, v = nc.sync, nc.vector
    cc_out = p.dram[f"cc_out{layer_id}"]
    p.wall(s, p.sC)
    dma_fence(p)
    with nc.sbuf_tensor(p.un("imA"), [128, TOK], BF16) as imA, \
         nc.sbuf_tensor(p.un("imB"), [128, TOK], BF16) as imB:
        for c in range(CH):
            p.wall(s, p.sV)
            p.dS.inc(s.dma_start(out=imA[:], in_=cc_out[0:128, c * TOK:(c + 1) * TOK]), 16)
            p.dS.inc(s.dma_start(out=imB[:], in_=cc_out[128:256, c * TOK:(c + 1) * TOK]), 16)
            p.wall(v, p.dS, p.sT)
            xo = p.XN[:, c * SEQ + TOK: c * SEQ + 2 * TOK]
            p.sV.inc(v.tensor_scalar(p.TMP[:], imA[:], p.mlo[:, 0:1], None, ALU.mult))
            p.sV.inc(v.scalar_tensor_tensor(xo, imB[:], p.mhi[:, 0:1], p.TMP[:],
                                            ALU.mult, ALU.add))



def ffn0(p):
    nc = p.nc
    v, s, g, a, t = nc.vector, nc.sync, nc.gpsimd, nc.scalar, nc.tensor
    d = p.dram
    NTG = 8
    TW = 256

    with ExitStack() as ph:
        RTY = ph.enter_context(nc.sbuf_tensor(p.un("RTY"), [128, 32 * 128], BF16))
        RTZ = ph.enter_context(nc.sbuf_tensor(p.un("RTZ"), [128, 128 * 128], BF16))
        YB = ph.enter_context(nc.sbuf_tensor(p.un("YB"), [128, 32 * TW], BF16))
        H4 = ph.enter_context(nc.sbuf_tensor(p.un("H4"), [128, 4 * TW], BF16))
        WA = ph.enter_context(nc.sbuf_tensor(p.un("WA"), [128, 4 * D], BF16))
        def GTOK(tc):
            return p.XN[:, (tc // 2) * SEQ + TOK + (tc % 2) * D:
                        (tc // 2) * SEQ + TOK + (tc % 2 + 1) * D]

        dma_fence(p)
        p.dG.inc(g.dma_start(out=RTY[:], in_=d["railsY"][:]), 16)
        for q in range(4):
            p.dG.inc(g.dma_start(out=RTZ[:, q * 4096:(q + 1) * 4096],
                                 in_=d["railsZ"][:, q * 4096:(q + 1) * 4096]), 16)
        rails_rel = Rel(p.dG, p.dG.n)

        wa_rel = [None, None]
        wa_bounce = [None, None]
        wa_load = [None, None]
        ldsem = p.take_slot_sems(2)
        bnsem = p.take_slot_sems(2)

        def load_w2(j, tg):
            slot = j % 2
            dst = WA[:, slot * 2 * D:(slot + 1) * 2 * D]
            if NO_BOUNCE or tg == 0:
                if wa_rel[slot] is not None:
                    wa_rel[slot].wait(g)
                if wa_bounce[slot] is not None:
                    wa_bounce[slot].wait(g)
                ldsem[slot].inc(g.dma_start(out=dst,
                                            in_=d["cprojT"][:, 2 * j * D:(2 * j + 2) * D]), 16)
                wa_load[slot] = Rel(ldsem[slot], ldsem[slot].n)
                if not NO_BOUNCE:
                    wa_load[slot].wait(s)
                    bnsem[slot].inc(s.dma_start(
                        out=d["cprojT_bf"][:, 2 * j * D:(2 * j + 2) * D], in_=dst), 16)
                    wa_bounce[slot] = Rel(bnsem[slot], bnsem[slot].n)
            else:
                eng = s if (j % 2 == 0) else g
                if wa_rel[slot] is not None:
                    wa_rel[slot].wait(eng)
                ldsem[slot].inc(eng.dma_start(
                    out=dst, in_=d["cprojT_bf"][:, 2 * j * D:(2 * j + 2) * D]), 16)
                wa_load[slot] = Rel(ldsem[slot], ldsem[slot].n)

        with nc.psum_tensor(p.un("py"), [128, 512], F32) as py, \
             nc.psum_tensor(p.un("pz"), [128, 1024], F32) as pz, \
             nc.psum_tensor(p.un("pot"), [128, 2048], F32) as pot:
            for tg in range(NTG):
                t0 = tg * TW
                if tg == 0:
                    rails_rel.wait(t)
                    p.wall(t, p.sV)
                if tg == 1:
                    p.wall(s, bnsem[0], bnsem[1])
                    p.wall(g, bnsem[0], bnsem[1])
                load_w2(0, tg)
                load_w2(1, tg)
                for yt in range(32):
                    j = yt % 8
                    half = yt % 2
                    p.wall(t, p.sV)
                    p.sT.inc(t.matmul(py[:, half * TW:(half + 1) * TW],
                                      RTY[:, yt * 128:(yt + 1) * 128],
                                      p.XN[:, j * SEQ + t0: j * SEQ + t0 + TW],
                                      start=True, stop=True, skip_group_check=True))
                    if half == 1:
                        p.wall(v, p.sT)
                        p.sV.inc(v.tensor_copy(YB[:, (yt - 1) * TW:(yt + 1) * TW], py[:]))
                for ib in range(32):
                    for k in range(4):
                        i = 4 * ib + k
                        if PARANOID & 4:
                            p.wall(t, p.dS, p.dG)
                        if PARANOID & 8:
                            p.wall(t, p.sA, p.sV)
                        if k == 0:
                            p.wall(t, p.sA)
                            if ib == 0:
                                p.wall(t, p.sV)
                        p.sT.inc(t.matmul(pz[:, k * TW:(k + 1) * TW],
                                          RTZ[:, i * 128:(i + 1) * 128],
                                          YB[:, (i // 4) * TW:(i // 4 + 1) * TW],
                                          start=True, stop=True, skip_group_check=True))
                    p.wall(a, p.sT)
                    if PARANOID & 16:
                        p.wall(a, p.dS, p.dG, p.sV)
                    p.sA.inc(a.activation(H4[:], pz[:], AF.Gelu,
                                          bias=p.c_zero[:, 0:1], scale=1.0))
                    gelu_cnt = p.sA.n
                    for k in range(4):
                        i = 4 * ib + k
                        j = i // 2
                        slot = j % 2
                        woff = slot * 2 * D + (i % 2) * D
                        wa_load[slot].wait(t)
                        t.wait_ge(p.sA.h, gelu_cnt)
                        for tc in range(2):
                            hsl = H4[:, k * TW + tc * 128: k * TW + (tc + 1) * 128]
                            for ob in range(2):
                                p.sT.inc(t.matmul(
                                    pot[:, tc * 1024 + ob * 512: tc * 1024 + (ob + 1) * 512],
                                    hsl,
                                    WA[:, woff + ob * 512: woff + (ob + 1) * 512],
                                    start=(i == 0), stop=(i == 127),
                                    skip_group_check=True))
                        if i % 2 == 1:
                            wa_rel[slot] = Rel(p.sT, p.sT.n)
                            if j + 2 < 64:
                                load_w2(j + 2, tg)
                p.wall(v, p.sT)
                for tc in range(2):
                    p.sV.inc(v.tensor_copy(GTOK(2 * tg + tc),
                                           pot[:, tc * 1024:(tc + 1) * 1024]))
                p.wall(t, p.sV)

        tap(p, "yb", YB[:], (p.sV,))
        p.release_slot_sems(4)
        with nc.psum_tensor(p.un("pgt"), [128, 2048], BF16) as pgt:
            for fc in range(CH):
                p.wall(t, p.sA, p.sV)
                for tc in range(16):
                    p.sT.inc(t.transpose(pgt[:, tc * 128:(tc + 1) * 128],
                                         GTOK(tc)[:, fc * 128:(fc + 1) * 128],
                                         p.eyeb[:]))
                p.wall(a, p.sT)
                p.sA.inc(a.activation(p.G[:, fc * TOK:(fc + 1) * TOK], pgt[:],
                                      AF.Silu, bias=p.c_zero[:, 0:1], scale=1.0))



def attention(p, layer_id):
    nc = p.nc
    v, s, g, a, t = nc.vector, nc.sync, nc.gpsimd, nc.scalar, nc.tensor
    d = p.dram
    l0 = (layer_id == 0)
    wq = d["w0qT"] if l0 else d["w1qT"]
    wk = d["w0kT"] if l0 else d["w1kT"]
    wv = d["w0vT"] if l0 else d["w1vT"]
    wo = d["w0oT"] if l0 else d["w1postT"]

    with ExitStack() as ph:
        A_sb = ph.enter_context(nc.sbuf_tensor(p.un("A_sb"), [128, NPAIR * TOK], BF16))
        with ExitStack() as core:
            K_sb = core.enter_context(nc.sbuf_tensor(p.un("K_sb"), [128, SEQ], BF16))
            VT_sb = core.enter_context(nc.sbuf_tensor(p.un("VT_sb"), [128, SEQ], BF16))
            Vsb = core.enter_context(nc.sbuf_tensor(p.un("Vsb"), [128, 32 * 130], BF16))
            Q_sb = core.enter_context(nc.sbuf_tensor(p.un("Q_sb"), [128, TOK], BF16))
            U = core.enter_context(nc.sbuf_tensor(p.un("U"), [128, 2 * TOK], BF16))
            PB = core.enter_context(nc.sbuf_tensor(p.un("PB"), [128, 3 * 1024], BF16))
            WQ = core.enter_context(nc.sbuf_tensor(p.un("WQs"), [128, 2 * D], BF16))
            WK = core.enter_context(nc.sbuf_tensor(p.un("WKs"), [128, 2 * D], BF16))
            WV = core.enter_context(nc.sbuf_tensor(p.un("WVs"), [128, 2 * D], BF16))
            psSc = core.enter_context(nc.psum_tensor(p.un("psSc"), [128, TOK], F32))
            psB = core.enter_context(nc.psum_tensor(p.un("psB"), [128, TOK], F32))

            wsl_rel = [None, None]
            wsl_load = [None, None]
            wssem = p.take_slot_sems(2)

            def load_pair_w(hp):
                slot = hp % 2
                if wsl_rel[slot] is not None:
                    wsl_rel[slot].wait(g)
                for wdram, wbuf in ((wq, WQ), (wk, WK), (wv, WV)):
                    wssem[slot].inc(g.dma_start(out=wbuf[:, slot * D:(slot + 1) * D],
                                                in_=wdram[:, hp * D:(hp + 1) * D]), 16)
                wsl_load[slot] = Rel(wssem[slot], wssem[slot].n)

            act_read = [None, None]
            u_free = [None, None]
            vt_free = [None, None]
            pb_free = [None, None, None]

            def proj(wbuf, slot, out_sb, bias_col, ntok):
                evac_rel = [None, None, None, None]
                for nb in range(ntok // 512):
                    q4 = nb % 4
                    if evac_rel[q4] is not None:
                        evac_rel[q4].wait(t)
                    else:
                        p.wall(t, p.sV)
                    for ic in range(CH):
                        p.sT.inc(t.matmul(
                            psSc[:, q4 * 512:(q4 + 1) * 512],
                            wbuf[:, slot * D + ic * 128: slot * D + (ic + 1) * 128],
                            p.XN[:, ic * SEQ + nb * 512: ic * SEQ + (nb + 1) * 512],
                            start=(ic == 0), stop=(ic == CH - 1),
                            skip_group_check=True))
                    v.wait_ge(p.sT.h, p.sT.n)
                    if bias_col is not None:
                        p.sV.inc(v.tensor_scalar(out_sb[:, nb * 512:(nb + 1) * 512],
                                                 psSc[:, q4 * 512:(q4 + 1) * 512],
                                                 bias_col, None, ALU.add))
                    else:
                        p.sV.inc(v.tensor_copy(out_sb[:, nb * 512:(nb + 1) * 512],
                                               psSc[:, q4 * 512:(q4 + 1) * 512]))
                    evac_rel[q4] = Rel(p.sV, p.sV.n)

            def normalize_pair(hp_):
                p.wall(t, p.sV, p.sA)
                bcast_row(p, psSc, p.SR[R_A:R_A + 1, :], TOK)
                p.wall(v, p.sT)
                p.sV.inc(v.tensor_tensor(A_sb[0:64, hp_ * TOK:(hp_ + 1) * TOK],
                                         A_sb[0:64, hp_ * TOK:(hp_ + 1) * TOK],
                                         psSc[0:64, :], ALU.mult))
                p.wall(t, p.sV)
                bcast_row(p, psSc, p.SR[R_B:R_B + 1, :], TOK)
                p.wall(v, p.sT)
                p.sV.inc(v.tensor_tensor(A_sb[64:128, hp_ * TOK:(hp_ + 1) * TOK],
                                         A_sb[64:128, hp_ * TOK:(hp_ + 1) * TOK],
                                         psSc[64:128, :], ALU.mult))
                p.wall(t, p.sV)

            dma_fence(p)
            load_pair_w(0)
            for hp in range(NPAIR):
                slot = hp % 2
                wsl_load[slot].wait(t)
                if hp + 1 < NPAIR:
                    load_pair_w(hp + 1)
                p.wall(t, p.sA)
                proj(WK, slot, K_sb, p.b0k[:, hp:hp + 1] if l0 else None, SEQ)
                if hp > 0:
                    normalize_pair(hp - 1)
                proj(WV, slot, VT_sb, p.b0v[:, hp:hp + 1] if l0 else None, SEQ)
                proj(WQ, slot, Q_sb, p.b0q[:, hp:hp + 1] if l0 else None, TOK)
                wsl_rel[slot] = Rel(p.sT, p.sT.n)
                psBb = psB[:].bitcast(BF16)
                p.wall(t, p.sV)
                for tc in range(32):
                    p.sT.inc(t.transpose(psBb[:, tc * 128:(tc + 1) * 128],
                                         VT_sb[:, tc * 128:(tc + 1) * 128], p.eyeb[:]))
                p.wall(v, p.sT)
                v.memset(Vsb[:], 1.0)
                src3 = psBb[:, 0:4096].rearrange("q (tc h c) -> q tc h c",
                                                 tc=32, h=2, c=64)
                dst3 = Vsb[:].rearrange("q (tc h c) -> q tc h c",
                                        tc=32, h=2, c=65)[:, :, :, 0:64]
                p.sV.inc(v.tensor_copy(dst3, src3))
                vsb_ready = p.sV.n
                for hh in range(2):
                    rb = 64 * hh
                    pend = []
                    for kc in range(32):
                        ksl = K_sb[rb:rb + 64, kc * 128:(kc + 1) * 128]
                        for qh in range(2):
                            it = (kc * 2 + qh)
                            sb = it % 2
                            sb3 = it % 3
                            q0 = qh * 1024
                            if act_read[sb] is not None:
                                act_read[sb].wait(t)
                            if kc == 0 and qh == 0 and hh == 0:
                                t.wait_ge(p.sV.h, vsb_ready)
                            for nb in range(2):
                                p.sT.inc(t.matmul(
                                    psSc[:, sb * 1024 + nb * 512: sb * 1024 + (nb + 1) * 512],
                                    ksl,
                                    Q_sb[rb:rb + 64, q0 + nb * 512: q0 + (nb + 1) * 512],
                                    start=True, stop=True, skip_group_check=True))
                            sc_done = p.sT.n
                            a.wait_ge(p.sT.h, sc_done)
                            scsl = psSc[:, sb * 1024:(sb + 1) * 1024]
                            if l0:
                                if u_free[sb] is not None:
                                    u_free[sb].wait(a)
                                p.sA.inc(a.activation(U[:, sb * 1024:(sb + 1) * 1024],
                                                      scsl, AF.Exp,
                                                      bias=p.c_ln2[:, 0:1], scale=0.125))
                                if vt_free[sb] is not None:
                                    vt_free[sb].wait(a)
                                p.sA.inc(a.activation(p.TMP[:, sb * 1024:(sb + 1) * 1024],
                                                      scsl, AF.Tanh,
                                                      bias=p.c_mu2[:, 0:1], scale=0.0625))
                                act_read[sb] = Rel(p.sA, p.sA.n)
                                v.wait_ge(p.sA.h, p.sA.n)
                                if pb_free[sb3] is not None:
                                    pb_free[sb3].wait(v)
                                p.sV.inc(v.scalar_tensor_tensor(
                                    PB[:, sb3 * 1024:(sb3 + 1) * 1024],
                                    p.TMP[:, sb * 1024:(sb + 1) * 1024], 1.0,
                                    U[:, sb * 1024:(sb + 1) * 1024],
                                    ALU.add, ALU.mult))
                                u_free[sb] = Rel(p.sV, p.sV.n)
                                vt_free[sb] = Rel(p.sV, p.sV.n)
                                pready = Rel(p.sV, p.sV.n)
                            else:
                                if pb_free[sb3] is not None:
                                    pb_free[sb3].wait(a)
                                p.sA.inc(a.activation(PB[:, sb3 * 1024:(sb3 + 1) * 1024],
                                                      scsl, AF.Exp,
                                                      bias=p.c_zero[:, 0:1], scale=0.125))
                                act_read[sb] = Rel(p.sA, p.sA.n)
                                pready = Rel(p.sA, p.sA.n)
                            vsl = Vsb[:, kc * 130 + hh * 65: kc * 130 + hh * 65 + 65]
                            pend.append((vsl, sb3, q0, kc == 0, kc == 31, pready))
                            if len(pend) > 2:
                                _vsl, _sb3, _q0, _st, _sp, _pr = pend.pop(0)
                                _pr.wait(t)
                                for nb in range(2):
                                    p.sT.inc(t.matmul(
                                        psB[0:65, _q0 + nb * 512: _q0 + (nb + 1) * 512],
                                        _vsl,
                                        PB[:, _sb3 * 1024 + nb * 512: _sb3 * 1024 + (nb + 1) * 512],
                                        start=_st, stop=_sp, skip_group_check=True))
                                pb_free[_sb3] = Rel(p.sT, p.sT.n)
                    while pend:
                        _vsl, _sb3, _q0, _st, _sp, _pr = pend.pop(0)
                        _pr.wait(t)
                        for nb in range(2):
                            p.sT.inc(t.matmul(
                                psB[0:65, _q0 + nb * 512: _q0 + (nb + 1) * 512], _vsl,
                                PB[:, _sb3 * 1024 + nb * 512: _sb3 * 1024 + (nb + 1) * 512],
                                start=_st, stop=_sp, skip_group_check=True))
                        pb_free[_sb3] = Rel(p.sT, p.sT.n)
                    p.wall(v, p.sT)
                    p.sV.inc(v.tensor_copy(A_sb[rb:rb + 64, hp * TOK:(hp + 1) * TOK],
                                           psB[0:64, :]))
                    rsrow = p.SR[(R_V if hh == 0 else R_M):(R_V if hh == 0 else R_M) + 1, :]
                    p.sV.inc(v.tensor_copy(rsrow, psB[64:65, :]))
                    p.wall(t, p.sV)
                p.sV.inc(v.reciprocal(p.SR[R_A:R_A + 1, :], p.SR[R_V:R_V + 1, :]))
                p.sV.inc(v.reciprocal(p.SR[R_B:R_B + 1, :], p.SR[R_M:R_M + 1, :]))
            normalize_pair(NPAIR - 1)
            p.release_slot_sems(2)
            if not l0:
                p.wall(a, p.sV)
                for hp in range(NPAIR):
                    p.sA.inc(a.activation(A_sb[:, hp * TOK:(hp + 1) * TOK],
                                          A_sb[:, hp * TOK:(hp + 1) * TOK],
                                          AF.Gelu, bias=p.c_zero[:, 0:1], scale=1.0))

        out_dram = p.dram["x_dram"] if l0 else p.dram["outT"]
        xin_dram = p.dram["xT"] if l0 else p.dram["x_dram"]
        with ExitStack() as ep:
            WO = ep.enter_context(nc.sbuf_tensor(p.un("WOs"), [128, 2 * D], BF16))
            XB = ep.enter_context(nc.sbuf_tensor(p.un("epxb"), [128, 2 * TOK], F32))
            T2 = ep.enter_context(nc.sbuf_tensor(p.un("ept2"), [128, TOK], F32))
            psW = ep.enter_context(nc.psum_tensor(p.un("psW"), [128, TOK], F32))
            wo_rel = [None, None]
            wo_load = [None, None]
            wosem = p.take_slot_sems(2)

            def load_wo(oc):
                slot = oc % 2
                if wo_rel[slot] is not None:
                    wo_rel[slot].wait(g)
                wosem[slot].inc(g.dma_start(out=WO[:, slot * D:(slot + 1) * D],
                                            in_=wo[:, oc * D:(oc + 1) * D]), 16)
                wo_load[slot] = Rel(wosem[slot], wosem[slot].n)

            dma_fence(p)
            load_wo(0)
            for oc in range(CH):
                slot = oc % 2
                wo_load[slot].wait(t)
                if oc + 1 < CH:
                    load_wo(oc + 1)
                xsl = XB[:, slot * TOK:(slot + 1) * TOK]
                p.wall(s, p.sV)
                p.dS.inc(s.dma_start(out=xsl, in_=xin_dram[:, oc * TOK:(oc + 1) * TOK]), 16)
                p.wall(t, p.sV, p.sA)
                for nb in range(4):
                    for ic in range(CH):
                        p.sT.inc(t.matmul(
                            psW[:, nb * 512:(nb + 1) * 512],
                            WO[:, slot * D + ic * 128: slot * D + (ic + 1) * 128],
                            A_sb[:, ic * TOK + nb * 512: ic * TOK + (nb + 1) * 512],
                            start=(ic == 0), stop=(ic == CH - 1),
                            skip_group_check=True))
                wo_rel[slot] = Rel(p.sT, p.sT.n)
                p.wall(v, p.sT, p.dS)
                if l0:
                    p.sV.inc(v.scalar_tensor_tensor(p.TMP[:], psW[:], p.b0o[:, oc:oc + 1],
                                                    p.G[:, oc * TOK:(oc + 1) * TOK],
                                                    ALU.add, ALU.mult))
                    p.sV.inc(v.tensor_add(T2[:], xsl, p.TMP[:]))
                else:
                    p.sV.inc(v.tensor_tensor(p.TMP[:], psW[:],
                                             p.XN[:, oc * SEQ: oc * SEQ + TOK], ALU.add))
                    p.wall(a, p.sV)
                    p.sA.inc(a.activation(T2[:], p.G[:, oc * TOK:(oc + 1) * TOK],
                                          AF.Silu, bias=p.c_zero[:, 0:1], scale=1.0))
                    p.wall(v, p.sA)
                    p.sV.inc(v.tensor_mul(p.TMP[:], p.TMP[:], T2[:]))
                    p.sV.inc(v.tensor_add(T2[:], xsl, p.TMP[:]))
                p.wall(s, p.sV)
                p.dO.inc(s.dma_start(out=out_dram[:, oc * TOK:(oc + 1) * TOK], in_=T2[:]), 16)
                p.wall(v, p.dO)
            p.release_slot_sems(2)



def gating_moe(p):
    nc = p.nc
    v, s, g, a, t = nc.vector, nc.sync, nc.gpsimd, nc.scalar, nc.tensor
    d = p.dram

    with ExitStack() as ph:
        GT = ph.enter_context(nc.sbuf_tensor(p.un("GT"), [128, CH * NE], BF16))
        LG = ph.enter_context(nc.sbuf_tensor(p.un("LG"), [8, TOK], F32))
        LGt = ph.enter_context(nc.sbuf_tensor(p.un("LGt"), [128, 128], F32))
        MX = ph.enter_context(nc.sbuf_tensor(p.un("MX"), [128, 16], F32))
        RW = ph.enter_context(nc.sbuf_tensor(p.un("RW"), [128, 16], F32))
        TS = ph.enter_context(nc.sbuf_tensor(p.un("TS"), [128, 128], F32))
        dma_fence(p)
        p.dG.inc(g.dma_start(out=GT[:], in_=d["gateT"][:]), 16)
        gt_rel = Rel(p.dG, p.dG.n)

        with nc.psum_tensor(p.un("psG"), [32, TOK], F32) as psG:
            gt_rel.wait(t)
            p.wall(t, p.sV)
            for nb in range(4):
                for ic in range(CH):
                    p.sT.inc(t.matmul(psG[0:8, nb * 512:(nb + 1) * 512],
                                      GT[:, ic * 8:(ic + 1) * 8],
                                      p.XN[:, ic * SEQ + nb * 512: ic * SEQ + (nb + 1) * 512],
                                      start=(ic == 0), stop=(ic == CH - 1),
                                      skip_group_check=True))
            p.wall(v, p.sT)
            p.sV.inc(v.tensor_copy(LG[:], psG[0:8, :]))
        with nc.psum_tensor(p.un("psLt"), [128, 128], F32) as psLt:
            p.wall(t, p.sV)
            for tc in range(16):
                p.sT.inc(t.transpose(psLt[:, tc * 8:(tc + 1) * 8],
                                     LG[0:8, tc * 128:(tc + 1) * 128],
                                     p.eyef[0:8, 0:8]))
            p.wall(v, p.sT)
            p.sV.inc(v.tensor_copy(LGt[:], psLt[:]))
        for tc in range(16):
            p.sV.inc(v.reduce_max(MX[:, tc:tc + 1], LGt[:, tc * 8:(tc + 1) * 8], axis=AX.X))
        v.drain()
        p.wall(a, p.sV)
        p.sA.inc(a.activation(RW[:], MX[:], AF.Sigmoid, bias=p.c_zero[:, 0:1], scale=1.0))
        p.wall(v, p.sA)
        v.drain()
        for tc in range(16):
            p.sV.inc(v.tensor_scalar(TS[:, tc * 8:(tc + 1) * 8], LGt[:, tc * 8:(tc + 1) * 8],
                                     MX[:, tc:tc + 1], RW[:, tc:tc + 1],
                                     ALU.is_equal, ALU.mult))

        p.wall(v, p.sA)
        v.memset(p.G[:], 0.0)
        p.sV.inc(v.engine_nop())
        XS = ph.enter_context(nc.sbuf_tensor(p.un("XS"), [128, CH * TOK], BF16))
        PRD = ph.enter_context(nc.sbuf_tensor(p.un("PRD"), [128, TOK], BF16))
        WG = ph.enter_context(nc.sbuf_tensor(p.un("WGs"), [128, 2 * 4096], BF16))
        WU = ph.enter_context(nc.sbuf_tensor(p.un("WUs"), [128, 2 * 4096], BF16))
        WD = ph.enter_context(nc.sbuf_tensor(p.un("WDs"), [128, 2 * 4096], BF16))
        we_rel = [None, None]
        we_load = [None, None]
        wesem = p.take_slot_sems(2)

        def load_we(e):
            slot = e % 2
            if we_rel[slot] is not None:
                we_rel[slot].wait(g)
            for wdram, wbuf in ((d["wgT"], WG), (d["wuT"], WU), (d["wdT"], WD)):
                wesem[slot].inc(g.dma_start(out=wbuf[:, slot * 4096:(slot + 1) * 4096],
                                            in_=wdram[e]), 16)
            we_load[slot] = Rel(wesem[slot], wesem[slot].n)

        dma_fence(p)
        load_we(0)
        for e in range(NE):
            slot = e % 2
            we_load[slot].wait(t)
            if e + 1 < NE:
                load_we(e + 1)
            with nc.psum_tensor(p.un("pRow"), [1, TOK], F32) as pRow:
                p.wall(t, p.sV)
                for tc in range(16):
                    p.sT.inc(t.transpose(pRow[0:1, tc * 128:(tc + 1) * 128],
                                         TS[:, tc * 8 + e: tc * 8 + e + 1], p.eyef[:]))
                p.wall(v, p.sT)
                p.sV.inc(v.tensor_copy(p.SR[R_A:R_A + 1, :], pRow[0:1, :]))
            with nc.psum_tensor(p.un("pB2"), [128, TOK], F32) as pB2:
                p.wall(t, p.sV)
                bcast_row(p, pB2, p.SR[R_A:R_A + 1, :], TOK)
                p.wall(v, p.sT)
                for c in range(CH):
                    p.sV.inc(v.tensor_tensor(XS[:, c * TOK:(c + 1) * TOK],
                                             p.XN[:, c * SEQ: c * SEQ + TOK],
                                             pB2[:], ALU.mult))
            for tt in range(4):
                t0 = tt * 512
                with nc.psum_tensor(p.un("pGm"), [128, 2048], F32) as pG, \
                     nc.psum_tensor(p.un("pUm"), [128, 2048], F32) as pU:
                    p.wall(t, p.sV, p.sA)
                    for oc in range(4):
                        for ic in range(CH):
                            wof = slot * 4096 + ic * 512 + oc * 128
                            p.sT.inc(t.matmul(pG[:, oc * 512:(oc + 1) * 512],
                                              WG[:, wof:wof + 128],
                                              XS[:, ic * TOK + t0: ic * TOK + t0 + 512],
                                              start=(ic == 0), stop=(ic == CH - 1),
                                              skip_group_check=True))
                            p.sT.inc(t.matmul(pU[:, oc * 512:(oc + 1) * 512],
                                              WU[:, wof:wof + 128],
                                              XS[:, ic * TOK + t0: ic * TOK + t0 + 512],
                                              start=(ic == 0), stop=(ic == CH - 1),
                                              skip_group_check=True))
                    p.wall(a, p.sT)
                    p.sA.inc(a.activation(p.TMP[:], pG[:], AF.Silu,
                                          bias=p.c_zero[:, 0:1], scale=1.0))
                    p.wall(v, p.sA, p.sT)
                    p.sV.inc(v.scalar_tensor_tensor(PRD[:], pU[:], 1.0, p.TMP[:],
                                                    ALU.bypass, ALU.mult))
                with nc.psum_tensor(p.un("pDm"), [128, 4096], F32) as pD:
                    p.wall(t, p.sV)
                    for oc in range(CH):
                        for ic in range(4):
                            wof = slot * 4096 + ic * D + oc * 128
                            p.sT.inc(t.matmul(pD[:, oc * 512:(oc + 1) * 512],
                                              WD[:, wof:wof + 128],
                                              PRD[:, ic * 512:(ic + 1) * 512],
                                              start=(ic == 0), stop=(ic == 3),
                                              skip_group_check=True))
                    p.wall(v, p.sT)
                    gview = p.G[:].rearrange("q (c tk) -> q c tk", c=CH)[:, :, t0:t0 + 512]
                    dview = pD[:].rearrange("q (c tk) -> q c tk", c=CH)
                    p.sV.inc(v.tensor_tensor(gview, gview, dview, ALU.add))
            we_rel[slot] = Rel(p.sT, p.sT.n)


class StopEmission(Exception):
    pass


def _ck(p, name):
    if STOP_AFTER and name == STOP_AFTER:
        raise StopEmission()


def layer(p, lid):
    ln_phase(p, lid, inner=False)
    tap(p, f"xn{lid}", p.XN[:, 0:TOK], (p.sV,))
    _ck(p, f"ln{lid}")
    exchange_export(p, lid)
    _ck(p, f"exp{lid}")
    if lid == 0:
        ffn0(p)
    else:
        gating_moe(p)
    tap(p, f"g{lid}", p.G[:, 0:TOK], (p.sV, p.sA))
    _ck(p, f"ffn{lid}")
    exchange_import(p, lid)
    _ck(p, f"imp{lid}")
    if lid == 0:
        ln_phase(p, lid, inner=True)
        tap(p, "xni0", p.XN[:, 0:TOK], (p.sV,))
        _ck(p, "lni0")
    attention(p, lid)
    if lid == 0:
        tap(p, "x1", p.dram["x_dram"][:, 0:TOK], (p.dO,))
    _ck(p, f"att{lid}")


def finale(p):
    nc = p.nc
    p.wall(nc.sync, p.dO)
    p.wall(nc.sync, p.dS)



def _lay_cm(w):
    r, n = w.shape
    ch = r // 128
    return np.ascontiguousarray(w.reshape(ch, 128, n).transpose(1, 0, 2).reshape(128, ch * n))


def _lay_pm(w_io):
    b = w_io.reshape(8, 128, 8, 128)
    return np.ascontiguousarray(b.transpose(1, 2, 0, 3).reshape(128, 8192))


def _lay_col(vec):
    return np.ascontiguousarray(vec.reshape(8, 128).T)


def _build_rails(rm, rup):
    ytiles = np.zeros((32, 128, 128), np.float32)
    for h in range(4):
        for j in range(8):
            tl = ytiles[h * 8 + j]
            for a_ in range(8):
                r = 8 * j + a_
                tl[16 * a_:16 * a_ + 16, 16 * a_:16 * a_ + 16] = rm[h, r]
    ztiles = np.zeros((128, 128, 128), np.float32)
    for ci in range(128):
        h = ci // 32
        rr = ci % 32
        tl = ztiles[ci]
        for l_ in range(2):
            r = 2 * rr + l_
            tl[16 * (r % 8):16 * (r % 8) + 16, 64 * l_:64 * l_ + 64] = rup[h, r].T
    layY = np.ascontiguousarray(ytiles.transpose(1, 0, 2).reshape(128, 32 * 128))
    layZ = np.ascontiguousarray(ztiles.transpose(1, 0, 2).reshape(128, 128 * 128))
    return layY, layZ


def kernel(**inputs):
    global LAST_RESULTS
    f = lambda k: np.asarray(inputs[k], dtype=np.float32)

    x = f("L0_attn_sequence")
    shared = {
        "w0qT": _lay_pm(np.ascontiguousarray(f("L0_attn_Wq").T)),
        "w0kT": _lay_pm(np.ascontiguousarray(f("L0_attn_Wk").T)),
        "w0vT": _lay_pm(np.ascontiguousarray(f("L0_attn_Wv").T)),
        "w0oT": _lay_pm(np.ascontiguousarray(f("L0_attn_Wo").T)),
        "b0q": _lay_col(f("L0_attn_bq")),
        "b0k": _lay_col(f("L0_attn_bk")),
        "b0v": _lay_col(f("L0_attn_bv")),
        "b0o": _lay_col(f("L0_attn_bo")),
        "gam": _lay_col(f("L0_attn_ln_gamma")),
        "bet": _lay_col(f("L0_attn_ln_beta")),
        "mu": f("L0_attn_mu").reshape(1, 1),
        "cprojT": _lay_cm(np.ascontiguousarray(f("L0_ffn_w_cproj").T)),
        "w1qT": _lay_pm(f("L1_attn_Wq")),
        "w1kT": _lay_pm(f("L1_attn_Wk")),
        "w1vT": _lay_pm(f("L1_attn_Wv")),
        "w1postT": _lay_pm(f("L1_attn_Wpost")),
        "gateT": _lay_cm(np.ascontiguousarray(f("L1_ffn_gate_w").T)),
        "wgT": np.stack([_lay_cm(np.ascontiguousarray(f("L1_ffn_w_gate")[e].T))
                         for e in range(NE)]),
        "wuT": np.stack([_lay_cm(np.ascontiguousarray(f("L1_ffn_w_up")[e].T))
                         for e in range(NE)]),
        "wdT": np.stack([_lay_cm(np.ascontiguousarray(f("L1_ffn_w_down")[e].T))
                         for e in range(NE)]),
        "eye": np.eye(128, dtype=np.float32),
    }
    shared["railsY"], shared["railsZ"] = _build_rails(
        f("L0_ffn_rail_maker"), f("L0_ffn_rail_up_proj"))

    nc = build_program(taps=DEBUG_TAPS, emit_upto=EMIT_UPTO)

    in_maps = []
    for c in range(8):
        b, par = c // 2, c % 2
        xs = np.ascontiguousarray(x[b, par * TOK:(par + 1) * TOK, :].T)
        m = dict(shared)
        m["xT"] = _lay_cm(xs)
        m["mlo"] = np.full((128, 1), float(par), np.float32)
        m["mhi"] = np.full((128, 1), float(1 - par), np.float32)
        in_maps.append(m)

    res = run_bass_kernel_spmd(nc, in_maps, core_ids=list(range(8)))
    LAST_RESULTS = res.results

    out = np.zeros((4, SEQ, D), np.float32)
    for c in range(8):
        b, par = c // 2, c % 2
        o = res.results[c]["outT"]
        o = o.reshape(128, CH, TOK).transpose(1, 0, 2).reshape(D, TOK).T
        out[b, par * TOK:(par + 1) * TOK, :] = o
    return out


# version 06
# speedup vs baseline: 1.5874x; 1.0036x over previous
import numpy as np
from contextlib import ExitStack

import concourse.bass as bass
import concourse.mybir as mybir
from concourse.bass_utils import run_bass_kernel_spmd

F32 = mybir.dt.float32
BF16 = mybir.dt.bfloat16
AF = mybir.ActivationFunctionType
ALU = mybir.AluOpType
AX = mybir.AxisListType

D = 1024
CH = 8
TOK = 2048
SEQ = 4096
NH = 16
NPAIR = 8
NE = 8
FCH = 128

LOG2 = float(np.log(2.0))

DEBUG_TAPS = {}
EMIT_UPTO = 2
STOP_AFTER = ""
PARANOID = 0
NO_BOUNCE = False
LAST_RESULTS = None

R_A = 0
R_B = 32
R_V = 64
R_M = 96


class Sem:
    def __init__(self, ctx, nc, name):
        self.h = ctx.enter_context(nc.semaphore(name))
        self.n = 0

    def inc(self, instr, k=1):
        instr.then_inc(self.h, k)
        self.n += k
        return instr

    def wait(self, eng, n=None):
        eng.wait_ge(self.h, self.n if n is None else n)


class Rel:
    def __init__(self, sem, n):
        self.sem = sem
        self.n = n

    def wait(self, eng):
        eng.wait_ge(self.sem.h, self.n)


class P:
    def take_slot_sems(self, n):
        out = self.slot_sems[self._slot_next:self._slot_next + n]
        assert len(out) == n, "out of slot sems"
        self._slot_next += n
        return out

    def release_slot_sems(self, n):
        self._slot_next -= n

    def un(self, base):
        self._uc = getattr(self, "_uc", 0) + 1
        return f"{base}_{self._uc}"


def build_program(taps=None, emit_upto=2):
    nc = bass.Bass("TRN2", target_bir_lowering=False, debug=False)
    st = ExitStack()
    p = P()
    p.nc = nc
    p.st = st
    p.taps = dict(taps or {})

    dram = {}

    def din(name, shape, dtype=F32):
        dram[name] = nc.dram_tensor(name, shape, dtype, kind="ExternalInput").ap()

    def dint(name, shape, dtype=F32):
        dram[name] = nc.dram_tensor(name, shape, dtype, kind="Internal").ap()

    def dout(name, shape, dtype=F32):
        dram[name] = nc.dram_tensor(name, shape, dtype, kind="ExternalOutput").ap()

    din("xT", [128, CH * TOK])
    for nm in ("w0qT", "w0kT", "w0vT", "w0oT", "w1qT", "w1kT", "w1vT", "w1postT"):
        din(nm, [128, NPAIR * D])
    for nm in ("b0q", "b0k", "b0v", "b0o", "gam", "bet"):
        din(nm, [128, 8])
    din("mu", [1, 1])
    din("railsY", [128, 32 * 128])
    din("railsZ", [128, 128 * 128])
    din("cprojT", [128, FCH * D])
    din("gateT", [128, CH * NE])
    din("wgT", [NE, 128, CH * 512])
    din("wuT", [NE, 128, CH * 512])
    din("wdT", [NE, 128, 4 * D])
    din("mlo", [128, 1])
    din("mhi", [128, 1])
    din("eye", [128, 128])
    dint("x_dram", [128, CH * TOK])
    dint("cc_in0", [128, CH * TOK], BF16)
    dint("cc_out0", [256, CH * TOK], BF16)
    dint("cc_in1", [128, CH * TOK], BF16)
    dint("cc_out1", [256, CH * TOK], BF16)
    dint("cprojT_bf", [128, FCH * D], BF16)
    dout("outT", [128, CH * TOK])
    p.dram = dram

    p.tap_out = {}
    for tname, shape in p.taps.items():
        dout("tap_" + tname, list(shape), F32)
        p.tap_out[tname] = dram["tap_" + tname]

    p.dS = Sem(st, nc, "dS")
    p.dG = Sem(st, nc, "dG")
    p.sT = Sem(st, nc, "sT")
    p.sA = Sem(st, nc, "sA")
    p.sV = Sem(st, nc, "sV")
    p.sC = Sem(st, nc, "sC")
    p.dO = Sem(st, nc, "dO")
    p.slot_sems = [Sem(st, nc, f"sl{i}") for i in range(12)]
    p._slot_next = 0

    def wall(eng, *sems):
        for s_ in sems:
            s_.wait(eng)
    p.wall = wall

    sb = st.enter_context
    p.XN = sb(nc.sbuf_tensor("XN", [128, CH * SEQ], BF16))
    p.G = sb(nc.sbuf_tensor("G", [128, CH * TOK], BF16))
    p.TMP = sb(nc.sbuf_tensor("TMP", [128, TOK], F32))
    p.SR = sb(nc.sbuf_tensor("SR", [128, TOK], F32))
    p.c_ones_b = sb(nc.sbuf_tensor("c_ones_b", [128, 1], BF16))
    p.c_onesr_f = sb(nc.sbuf_tensor("c_onesr_f", [128, 128], F32))
    p.c_ln2 = sb(nc.sbuf_tensor("c_ln2", [128, 1], F32))
    p.c_mu2 = sb(nc.sbuf_tensor("c_mu2", [128, 1], F32))
    p.c_zero = sb(nc.sbuf_tensor("c_zero", [128, 1], F32))
    p.c_eps1 = sb(nc.sbuf_tensor("c_eps1", [128, 1], F32))
    p.c_eps2 = sb(nc.sbuf_tensor("c_eps2", [128, 1], F32))
    p.b0q = sb(nc.sbuf_tensor("b0q_s", [128, 8], F32))
    p.b0k = sb(nc.sbuf_tensor("b0k_s", [128, 8], F32))
    p.b0v = sb(nc.sbuf_tensor("b0v_s", [128, 8], F32))
    p.b0o = sb(nc.sbuf_tensor("b0o_s", [128, 8], F32))
    p.gam = sb(nc.sbuf_tensor("gam_s", [128, 8], F32))
    p.bet = sb(nc.sbuf_tensor("bet_s", [128, 8], F32))
    p.mlo = sb(nc.sbuf_tensor("mlo_s", [128, 1], F32))
    p.mhi = sb(nc.sbuf_tensor("mhi_s", [128, 1], F32))
    p.eyeb = sb(nc.sbuf_tensor("eyeb", [128, 128], BF16))
    p.eyef = sb(nc.sbuf_tensor("eyef", [128, 128], F32))

    prologue(p)
    try:
        for lid in range(min(2, emit_upto)):
            layer(p, lid)
    except StopEmission:
        pass
    finale(p)

    st.close()
    return nc


def tap(p, name, src_ap, sems):
    if name not in p.tap_out:
        return
    nc = p.nc
    eng = nc.gpsimd if src_ap.dtype != F32 else nc.sync
    p.wall(eng, *sems)
    p.dO.inc(eng.dma_start(out=p.tap_out[name][:], in_=src_ap), 16)


def prologue(p):
    nc = p.nc
    v, s, g, a, t = nc.vector, nc.sync, nc.gpsimd, nc.scalar, nc.tensor
    d = p.dram
    v.memset(p.c_ones_b[:], 1.0)
    v.memset(p.c_onesr_f[:], 1.0)
    v.memset(p.c_ln2[:], -LOG2)
    v.memset(p.c_zero[:], 0.0)
    v.memset(p.c_eps1[:], 1e-5)
    v.memset(p.c_eps2[:], 1e-6)
    p.sV.inc(v.engine_nop())
    for nm, t_sb in (("b0q", p.b0q), ("b0k", p.b0k), ("b0v", p.b0v), ("b0o", p.b0o),
                     ("gam", p.gam), ("bet", p.bet), ("mlo", p.mlo), ("mhi", p.mhi),
                     ("eye", p.eyef)):
        p.dS.inc(s.dma_start(out=t_sb[:], in_=d[nm][:]), 16)
    p.dG.inc(g.dma_start(out=p.eyeb[:], in_=d["eye"][:]), 16)
    mut = p.st.enter_context(nc.sbuf_tensor("mut", [1, 1], F32))
    with nc.psum_tensor("pmu", [128, 1], F32) as pmu:
        p.dS.inc(s.dma_start(out=mut[:], in_=d["mu"][:]), 16)
        p.wall(t, p.dS, p.sV)
        p.sT.inc(t.matmul(pmu[:], p.c_onesr_f[0:1, :], mut[:], start=True, stop=True))
        p.wall(a, p.sT, p.sV)
        p.sA.inc(a.activation(p.c_mu2[:], pmu[:], AF.Copy, bias=0.0, scale=-0.5))
        a.drain()


def dma_fence(p):
    nc = p.nc
    p.wall(nc.sync, p.sT, p.sA, p.sV)
    p.wall(nc.gpsimd, p.sT, p.sA, p.sV)


def bcast_row(p, psdst, row_ap, width):
    t = p.nc.tensor
    rb = row_ap.base_partition()
    for nb in range(width // 512):
        p.sT.inc(t.matmul(psdst[:, nb * 512:(nb + 1) * 512],
                          p.c_onesr_f[rb:rb + 1, :],
                          row_ap[0:1, nb * 512:(nb + 1) * 512],
                          start=True, stop=True, skip_group_check=True))


def ln_phase(p, layer_id, inner):
    nc = p.nc
    v, s, a, t = nc.vector, nc.sync, nc.scalar, nc.tensor
    nhalf = 2 if inner else 1
    src = p.dram["xT"] if layer_id == 0 else p.dram["x_dram"]

    p.wall(v, p.dS, p.sT)
    p.wall(s, p.dO)
    dma_fence(p)
    with ExitStack() as ph:
        SQ = ph.enter_context(nc.sbuf_tensor(p.un("lnsq"), [128, TOK], BF16))
        XB = None
        if not inner:
            XB = ph.enter_context(nc.sbuf_tensor(p.un("lnxb"), [128, 2 * TOK], F32))
        for half in range(nhalf):
            hs = half * TOK
            pstat_cm = nc.psum_tensor(p.un("pstat"), [64, TOK], F32)
            pstat = pstat_cm.__enter__()
            for c in range(CH):
                if not inner:
                    xsl = XB[:, (c % 2) * TOK:(c % 2 + 1) * TOK]
                    p.wall(s, p.sV)
                    p.dS.inc(s.dma_start(out=xsl, in_=src[:, c * TOK:(c + 1) * TOK]), 16)
                    p.wall(v, p.dS)
                    xb = p.XN[:, c * SEQ + TOK: c * SEQ + 2 * TOK]
                    p.sV.inc(v.tensor_copy(xb, xsl))
                else:
                    xb = p.XN[:, c * SEQ + hs: c * SEQ + hs + TOK]
                p.wall(v, p.sT)
                p.sV.inc(v.tensor_mul(SQ[:], xb, xb))
                p.wall(t, p.sV)
                for nb in range(4):
                    n0 = nb * 512
                    p.sT.inc(t.matmul(pstat[0:1, n0:n0 + 512], p.c_ones_b[:],
                                      xb[:, n0:n0 + 512], start=(c == 0),
                                      stop=(c == CH - 1), skip_group_check=True))
                    p.sT.inc(t.matmul(pstat[32:33, n0:n0 + 512], p.c_ones_b[:],
                                      SQ[:, n0:n0 + 512], start=(c == 0),
                                      stop=(c == CH - 1), skip_group_check=True))
            p.wall(v, p.sT)
            mrow = p.SR[R_M:R_M + 1, :]
            vrow = p.SR[R_V:R_V + 1, :]
            arow = p.SR[R_A:R_A + 1, :]
            brow = p.SR[R_B:R_B + 1, :]
            p.sV.inc(v.tensor_scalar_mul(mrow, pstat[0:1, :], 1.0 / D))
            p.sV.inc(v.tensor_mul(vrow, mrow, mrow))
            p.sV.inc(v.scalar_tensor_tensor(vrow, pstat[32:33, :], 1.0 / D, vrow,
                                            ALU.mult, ALU.subtract))
            epsc = p.c_eps2 if inner else p.c_eps1
            p.wall(a, p.sV)
            p.sA.inc(a.activation(brow, vrow, AF.Ln, bias=epsc[0:1, 0:1], scale=1.0))
            a.drain()
            p.sA.inc(a.activation(arow, brow, AF.Exp, bias=p.c_zero[0:1, 0:1], scale=-0.5))
            p.wall(v, p.sA)
            p.sV.inc(v.scalar_tensor_tensor(brow, pstat[0:1, :], -1.0 / D, arow,
                                            ALU.mult, ALU.mult))
            pstat_cm.__exit__(None, None, None)
            pbca_cm = nc.psum_tensor(p.un("pbca"), [128, TOK], F32)
            pbcb_cm = nc.psum_tensor(p.un("pbcb"), [128, TOK], F32)
            pbca = pbca_cm.__enter__()
            pbcb = pbcb_cm.__enter__()
            p.wall(t, p.sV)
            bcast_row(p, pbca, arow, TOK)
            bcast_row(p, pbcb, brow, TOK)
            for c in range(CH):
                if inner:
                    xb = p.XN[:, c * SEQ + hs: c * SEQ + hs + TOK]
                    xout = xb
                else:
                    xb = p.XN[:, c * SEQ + TOK: c * SEQ + 2 * TOK]
                    xout = p.XN[:, c * SEQ: c * SEQ + TOK]
                p.wall(v, p.sT)
                p.sV.inc(v.tensor_tensor(p.TMP[:], xb, pbca[:], ALU.mult))
                if inner:
                    p.sV.inc(v.tensor_tensor(p.TMP[:], p.TMP[:], pbcb[:], ALU.add))
                    p.sV.inc(v.tensor_scalar(xout, p.TMP[:], p.gam[:, c:c + 1],
                                             p.bet[:, c:c + 1], ALU.mult, ALU.add))
                else:
                    p.sV.inc(v.tensor_tensor(xout, p.TMP[:], pbcb[:], ALU.add))
            p.wall(t, p.sV)
            pbcb_cm.__exit__(None, None, None)
            pbca_cm.__exit__(None, None, None)


def exchange_export(p, layer_id):
    nc = p.nc
    s, g = nc.sync, nc.gpsimd
    cc_in = p.dram[f"cc_in{layer_id}"]
    cc_out = p.dram[f"cc_out{layer_id}"]
    p.wall(s, p.sV)
    for c in range(CH):
        p.dS.inc(s.dma_start(out=cc_in[:, c * TOK:(c + 1) * TOK],
                             in_=p.XN[:, c * SEQ: c * SEQ + TOK]), 16)
    p.wall(g, p.dS)
    p.sC.inc(g.collective_compute(
        "AllGather", ALU.bypass, ins=[cc_in[:]], outs=[cc_out[:]],
        replica_groups=[[0, 1], [2, 3], [4, 5], [6, 7]]))


def exchange_import(p, layer_id):
    nc = p.nc
    s, v = nc.sync, nc.vector
    cc_out = p.dram[f"cc_out{layer_id}"]
    p.wall(s, p.sC)
    dma_fence(p)
    with nc.sbuf_tensor(p.un("imA"), [128, TOK], BF16) as imA, \
         nc.sbuf_tensor(p.un("imB"), [128, TOK], BF16) as imB:
        for c in range(CH):
            p.wall(s, p.sV)
            p.dS.inc(s.dma_start(out=imA[:], in_=cc_out[0:128, c * TOK:(c + 1) * TOK]), 16)
            p.dS.inc(s.dma_start(out=imB[:], in_=cc_out[128:256, c * TOK:(c + 1) * TOK]), 16)
            p.wall(v, p.dS, p.sT)
            xo = p.XN[:, c * SEQ + TOK: c * SEQ + 2 * TOK]
            p.sV.inc(v.tensor_scalar(p.TMP[:], imA[:], p.mlo[:, 0:1], None, ALU.mult))
            p.sV.inc(v.scalar_tensor_tensor(xo, imB[:], p.mhi[:, 0:1], p.TMP[:],
                                            ALU.mult, ALU.add))



def ffn0(p):
    nc = p.nc
    v, s, g, a, t = nc.vector, nc.sync, nc.gpsimd, nc.scalar, nc.tensor
    d = p.dram
    NTG = 8
    TW = 256

    with ExitStack() as ph:
        RTY = ph.enter_context(nc.sbuf_tensor(p.un("RTY"), [128, 32 * 128], BF16))
        RTZ = ph.enter_context(nc.sbuf_tensor(p.un("RTZ"), [128, 128 * 128], BF16))
        YB = ph.enter_context(nc.sbuf_tensor(p.un("YB"), [128, 32 * TW], BF16))
        H4 = ph.enter_context(nc.sbuf_tensor(p.un("H4"), [128, 4 * TW], BF16))
        WA = ph.enter_context(nc.sbuf_tensor(p.un("WA"), [128, 4 * D], BF16))
        def GTOK(tc):
            return p.XN[:, (tc // 2) * SEQ + TOK + (tc % 2) * D:
                        (tc // 2) * SEQ + TOK + (tc % 2 + 1) * D]

        dma_fence(p)
        p.dG.inc(g.dma_start(out=RTY[:], in_=d["railsY"][:]), 16)
        for q in range(4):
            p.dG.inc(g.dma_start(out=RTZ[:, q * 4096:(q + 1) * 4096],
                                 in_=d["railsZ"][:, q * 4096:(q + 1) * 4096]), 16)
        rails_rel = Rel(p.dG, p.dG.n)

        wa_rel = [None, None]
        wa_bounce = [None, None]
        wa_load = [None, None]
        ldsem = p.take_slot_sems(2)
        bnsem = p.take_slot_sems(2)

        def load_w2(j, tg):
            slot = j % 2
            dst = WA[:, slot * 2 * D:(slot + 1) * 2 * D]
            if NO_BOUNCE or tg == 0:
                if wa_rel[slot] is not None:
                    wa_rel[slot].wait(g)
                if wa_bounce[slot] is not None:
                    wa_bounce[slot].wait(g)
                ldsem[slot].inc(g.dma_start(out=dst,
                                            in_=d["cprojT"][:, 2 * j * D:(2 * j + 2) * D]), 16)
                wa_load[slot] = Rel(ldsem[slot], ldsem[slot].n)
                if not NO_BOUNCE:
                    wa_load[slot].wait(s)
                    bnsem[slot].inc(s.dma_start(
                        out=d["cprojT_bf"][:, 2 * j * D:(2 * j + 2) * D], in_=dst), 16)
                    wa_bounce[slot] = Rel(bnsem[slot], bnsem[slot].n)
            else:
                eng = s if (j % 2 == 0) else g
                if wa_rel[slot] is not None:
                    wa_rel[slot].wait(eng)
                ldsem[slot].inc(eng.dma_start(
                    out=dst, in_=d["cprojT_bf"][:, 2 * j * D:(2 * j + 2) * D]), 16)
                wa_load[slot] = Rel(ldsem[slot], ldsem[slot].n)

        with nc.psum_tensor(p.un("py"), [128, 512], F32) as py, \
             nc.psum_tensor(p.un("pz"), [128, 1024], F32) as pz, \
             nc.psum_tensor(p.un("pot"), [128, 2048], F32) as pot:
            for tg in range(NTG):
                t0 = tg * TW
                if tg == 0:
                    rails_rel.wait(t)
                    p.wall(t, p.sV)
                if tg == 1:
                    p.wall(s, bnsem[0], bnsem[1])
                    p.wall(g, bnsem[0], bnsem[1])
                load_w2(0, tg)
                load_w2(1, tg)
                for yt in range(32):
                    j = yt % 8
                    half = yt % 2
                    p.wall(t, p.sV)
                    p.sT.inc(t.matmul(py[:, half * TW:(half + 1) * TW],
                                      RTY[:, yt * 128:(yt + 1) * 128],
                                      p.XN[:, j * SEQ + t0: j * SEQ + t0 + TW],
                                      start=True, stop=True, skip_group_check=True))
                    if half == 1:
                        p.wall(v, p.sT)
                        p.sV.inc(v.tensor_copy(YB[:, (yt - 1) * TW:(yt + 1) * TW], py[:]))
                for ib in range(32):
                    for k in range(4):
                        i = 4 * ib + k
                        if PARANOID & 4:
                            p.wall(t, p.dS, p.dG)
                        if PARANOID & 8:
                            p.wall(t, p.sA, p.sV)
                        if k == 0:
                            p.wall(t, p.sA)
                            if ib == 0:
                                p.wall(t, p.sV)
                        p.sT.inc(t.matmul(pz[:, k * TW:(k + 1) * TW],
                                          RTZ[:, i * 128:(i + 1) * 128],
                                          YB[:, (i // 4) * TW:(i // 4 + 1) * TW],
                                          start=True, stop=True, skip_group_check=True))
                    p.wall(a, p.sT)
                    if PARANOID & 16:
                        p.wall(a, p.dS, p.dG, p.sV)
                    p.sA.inc(a.activation(H4[:], pz[:], AF.Gelu,
                                          bias=p.c_zero[:, 0:1], scale=1.0))
                    gelu_cnt = p.sA.n
                    for k in range(4):
                        i = 4 * ib + k
                        j = i // 2
                        slot = j % 2
                        woff = slot * 2 * D + (i % 2) * D
                        wa_load[slot].wait(t)
                        t.wait_ge(p.sA.h, gelu_cnt)
                        for tc in range(2):
                            hsl = H4[:, k * TW + tc * 128: k * TW + (tc + 1) * 128]
                            for ob in range(2):
                                p.sT.inc(t.matmul(
                                    pot[:, tc * 1024 + ob * 512: tc * 1024 + (ob + 1) * 512],
                                    hsl,
                                    WA[:, woff + ob * 512: woff + (ob + 1) * 512],
                                    start=(i == 0), stop=(i == 127),
                                    skip_group_check=True))
                        if i % 2 == 1:
                            wa_rel[slot] = Rel(p.sT, p.sT.n)
                            if j + 2 < 64:
                                load_w2(j + 2, tg)
                p.wall(v, p.sT)
                for tc in range(2):
                    p.sV.inc(v.tensor_copy(GTOK(2 * tg + tc),
                                           pot[:, tc * 1024:(tc + 1) * 1024]))
                p.wall(t, p.sV)

        tap(p, "yb", YB[:], (p.sV,))
        p.release_slot_sems(4)
        with nc.psum_tensor(p.un("pgt"), [128, 2048], BF16) as pgt:
            for fc in range(CH):
                p.wall(t, p.sA, p.sV)
                for tc in range(16):
                    p.sT.inc(t.transpose(pgt[:, tc * 128:(tc + 1) * 128],
                                         GTOK(tc)[:, fc * 128:(fc + 1) * 128],
                                         p.eyeb[:]))
                p.wall(a, p.sT)
                p.sA.inc(a.activation(p.G[:, fc * TOK:(fc + 1) * TOK], pgt[:],
                                      AF.Silu, bias=p.c_zero[:, 0:1], scale=1.0))



def attention(p, layer_id):
    nc = p.nc
    v, s, g, a, t = nc.vector, nc.sync, nc.gpsimd, nc.scalar, nc.tensor
    d = p.dram
    l0 = (layer_id == 0)
    wq = d["w0qT"] if l0 else d["w1qT"]
    wk = d["w0kT"] if l0 else d["w1kT"]
    wv = d["w0vT"] if l0 else d["w1vT"]
    wo = d["w0oT"] if l0 else d["w1postT"]

    with ExitStack() as ph:
        A_sb = ph.enter_context(nc.sbuf_tensor(p.un("A_sb"), [128, NPAIR * TOK], BF16))
        with ExitStack() as core:
            K_sb = core.enter_context(nc.sbuf_tensor(p.un("K_sb"), [128, SEQ], BF16))
            VT_sb = core.enter_context(nc.sbuf_tensor(p.un("VT_sb"), [128, SEQ], BF16))
            Vsb = core.enter_context(nc.sbuf_tensor(p.un("Vsb"), [128, 32 * 130], BF16))
            Q_sb = core.enter_context(nc.sbuf_tensor(p.un("Q_sb"), [128, TOK], BF16))
            U = core.enter_context(nc.sbuf_tensor(p.un("U"), [128, 2 * TOK], BF16))
            PB = core.enter_context(nc.sbuf_tensor(p.un("PB"), [128, 4 * 1024], BF16))
            WQ = core.enter_context(nc.sbuf_tensor(p.un("WQs"), [128, 2 * D], BF16))
            WK = core.enter_context(nc.sbuf_tensor(p.un("WKs"), [128, 2 * D], BF16))
            WV = core.enter_context(nc.sbuf_tensor(p.un("WVs"), [128, 2 * D], BF16))
            psSc = core.enter_context(nc.psum_tensor(p.un("psSc"), [128, TOK], F32))
            psB = core.enter_context(nc.psum_tensor(p.un("psB"), [128, TOK], F32))

            wsl_rel = [None, None]
            wsl_load = [None, None]
            wssem = p.take_slot_sems(2)

            def load_pair_w(hp):
                slot = hp % 2
                if wsl_rel[slot] is not None:
                    wsl_rel[slot].wait(g)
                for wdram, wbuf in ((wq, WQ), (wk, WK), (wv, WV)):
                    wssem[slot].inc(g.dma_start(out=wbuf[:, slot * D:(slot + 1) * D],
                                                in_=wdram[:, hp * D:(hp + 1) * D]), 16)
                wsl_load[slot] = Rel(wssem[slot], wssem[slot].n)

            act_read = [None, None]
            u_free = [None, None]
            vt_free = [None, None]
            pb_free = [None, None, None, None]

            def proj(wbuf, slot, out_sb, bias_col, ntok):
                evac_rel = [None, None, None, None]
                for nb in range(ntok // 512):
                    q4 = nb % 4
                    if evac_rel[q4] is not None:
                        evac_rel[q4].wait(t)
                    else:
                        p.wall(t, p.sV)
                    for ic in range(CH):
                        p.sT.inc(t.matmul(
                            psSc[:, q4 * 512:(q4 + 1) * 512],
                            wbuf[:, slot * D + ic * 128: slot * D + (ic + 1) * 128],
                            p.XN[:, ic * SEQ + nb * 512: ic * SEQ + (nb + 1) * 512],
                            start=(ic == 0), stop=(ic == CH - 1),
                            skip_group_check=True))
                    v.wait_ge(p.sT.h, p.sT.n)
                    if bias_col is not None:
                        p.sV.inc(v.tensor_scalar(out_sb[:, nb * 512:(nb + 1) * 512],
                                                 psSc[:, q4 * 512:(q4 + 1) * 512],
                                                 bias_col, None, ALU.add))
                    else:
                        p.sV.inc(v.tensor_copy(out_sb[:, nb * 512:(nb + 1) * 512],
                                               psSc[:, q4 * 512:(q4 + 1) * 512]))
                    evac_rel[q4] = Rel(p.sV, p.sV.n)

            def normalize_pair(hp_):
                p.wall(t, p.sV, p.sA)
                bcast_row(p, psSc, p.SR[R_A:R_A + 1, :], TOK)
                p.wall(v, p.sT)
                p.sV.inc(v.tensor_tensor(A_sb[0:64, hp_ * TOK:(hp_ + 1) * TOK],
                                         A_sb[0:64, hp_ * TOK:(hp_ + 1) * TOK],
                                         psSc[0:64, :], ALU.mult))
                p.wall(t, p.sV)
                bcast_row(p, psSc, p.SR[R_B:R_B + 1, :], TOK)
                p.wall(v, p.sT)
                p.sV.inc(v.tensor_tensor(A_sb[64:128, hp_ * TOK:(hp_ + 1) * TOK],
                                         A_sb[64:128, hp_ * TOK:(hp_ + 1) * TOK],
                                         psSc[64:128, :], ALU.mult))
                p.wall(t, p.sV)

            dma_fence(p)
            load_pair_w(0)
            for hp in range(NPAIR):
                slot = hp % 2
                wsl_load[slot].wait(t)
                if hp + 1 < NPAIR:
                    load_pair_w(hp + 1)
                p.wall(t, p.sA)
                proj(WK, slot, K_sb, p.b0k[:, hp:hp + 1] if l0 else None, SEQ)
                if hp > 0:
                    normalize_pair(hp - 1)
                proj(WV, slot, VT_sb, p.b0v[:, hp:hp + 1] if l0 else None, SEQ)
                proj(WQ, slot, Q_sb, p.b0q[:, hp:hp + 1] if l0 else None, TOK)
                wsl_rel[slot] = Rel(p.sT, p.sT.n)
                psBb = psB[:].bitcast(BF16)
                p.wall(t, p.sV)
                for tc in range(32):
                    p.sT.inc(t.transpose(psBb[:, tc * 128:(tc + 1) * 128],
                                         VT_sb[:, tc * 128:(tc + 1) * 128], p.eyeb[:]))
                p.wall(v, p.sT)
                v.memset(Vsb[:], 1.0)
                src3 = psBb[:, 0:4096].rearrange("q (tc h c) -> q tc h c",
                                                 tc=32, h=2, c=64)
                dst3 = Vsb[:].rearrange("q (tc h c) -> q tc h c",
                                        tc=32, h=2, c=65)[:, :, :, 0:64]
                p.sV.inc(v.tensor_copy(dst3, src3))
                vsb_ready = p.sV.n
                for hh in range(2):
                    rb = 64 * hh
                    pend = []
                    for kc in range(32):
                        ksl = K_sb[rb:rb + 64, kc * 128:(kc + 1) * 128]
                        for qh in range(2):
                            it = (kc * 2 + qh)
                            sb = it % 2
                            sb3 = it % 4
                            q0 = qh * 1024
                            if act_read[sb] is not None:
                                act_read[sb].wait(t)
                            if kc == 0 and qh == 0 and hh == 0:
                                t.wait_ge(p.sV.h, vsb_ready)
                            for nb in range(2):
                                p.sT.inc(t.matmul(
                                    psSc[:, sb * 1024 + nb * 512: sb * 1024 + (nb + 1) * 512],
                                    ksl,
                                    Q_sb[rb:rb + 64, q0 + nb * 512: q0 + (nb + 1) * 512],
                                    start=True, stop=True, skip_group_check=True))
                            sc_done = p.sT.n
                            a.wait_ge(p.sT.h, sc_done)
                            scsl = psSc[:, sb * 1024:(sb + 1) * 1024]
                            if l0:
                                if u_free[sb] is not None:
                                    u_free[sb].wait(a)
                                p.sA.inc(a.activation(U[:, sb * 1024:(sb + 1) * 1024],
                                                      scsl, AF.Exp,
                                                      bias=p.c_ln2[:, 0:1], scale=0.125))
                                if vt_free[sb] is not None:
                                    vt_free[sb].wait(a)
                                p.sA.inc(a.activation(p.TMP[:, sb * 1024:(sb + 1) * 1024],
                                                      scsl, AF.Tanh,
                                                      bias=p.c_mu2[:, 0:1], scale=0.0625))
                                act_read[sb] = Rel(p.sA, p.sA.n)
                                v.wait_ge(p.sA.h, p.sA.n)
                                if pb_free[sb3] is not None:
                                    pb_free[sb3].wait(v)
                                p.sV.inc(v.scalar_tensor_tensor(
                                    PB[:, sb3 * 1024:(sb3 + 1) * 1024],
                                    p.TMP[:, sb * 1024:(sb + 1) * 1024], 1.0,
                                    U[:, sb * 1024:(sb + 1) * 1024],
                                    ALU.add, ALU.mult))
                                u_free[sb] = Rel(p.sV, p.sV.n)
                                vt_free[sb] = Rel(p.sV, p.sV.n)
                                pready = Rel(p.sV, p.sV.n)
                            else:
                                if pb_free[sb3] is not None:
                                    pb_free[sb3].wait(a)
                                p.sA.inc(a.activation(PB[:, sb3 * 1024:(sb3 + 1) * 1024],
                                                      scsl, AF.Exp,
                                                      bias=p.c_zero[:, 0:1], scale=0.125))
                                act_read[sb] = Rel(p.sA, p.sA.n)
                                pready = Rel(p.sA, p.sA.n)
                            vsl = Vsb[:, kc * 130 + hh * 65: kc * 130 + hh * 65 + 65]
                            pend.append((vsl, sb3, q0, kc == 0, kc == 31, pready))
                            if len(pend) > 3:
                                _vsl, _sb3, _q0, _st, _sp, _pr = pend.pop(0)
                                _pr.wait(t)
                                for nb in range(2):
                                    p.sT.inc(t.matmul(
                                        psB[0:65, _q0 + nb * 512: _q0 + (nb + 1) * 512],
                                        _vsl,
                                        PB[:, _sb3 * 1024 + nb * 512: _sb3 * 1024 + (nb + 1) * 512],
                                        start=_st, stop=_sp, skip_group_check=True))
                                pb_free[_sb3] = Rel(p.sT, p.sT.n)
                    while pend:
                        _vsl, _sb3, _q0, _st, _sp, _pr = pend.pop(0)
                        _pr.wait(t)
                        for nb in range(2):
                            p.sT.inc(t.matmul(
                                psB[0:65, _q0 + nb * 512: _q0 + (nb + 1) * 512], _vsl,
                                PB[:, _sb3 * 1024 + nb * 512: _sb3 * 1024 + (nb + 1) * 512],
                                start=_st, stop=_sp, skip_group_check=True))
                        pb_free[_sb3] = Rel(p.sT, p.sT.n)
                    p.wall(v, p.sT)
                    p.sV.inc(v.tensor_copy(A_sb[rb:rb + 64, hp * TOK:(hp + 1) * TOK],
                                           psB[0:64, :]))
                    rsrow = p.SR[(R_V if hh == 0 else R_M):(R_V if hh == 0 else R_M) + 1, :]
                    p.sV.inc(v.tensor_copy(rsrow, psB[64:65, :]))
                    p.wall(t, p.sV)
                p.sV.inc(v.reciprocal(p.SR[R_A:R_A + 1, :], p.SR[R_V:R_V + 1, :]))
                p.sV.inc(v.reciprocal(p.SR[R_B:R_B + 1, :], p.SR[R_M:R_M + 1, :]))
            normalize_pair(NPAIR - 1)
            p.release_slot_sems(2)
            if not l0:
                p.wall(a, p.sV)
                for hp in range(NPAIR):
                    p.sA.inc(a.activation(A_sb[:, hp * TOK:(hp + 1) * TOK],
                                          A_sb[:, hp * TOK:(hp + 1) * TOK],
                                          AF.Gelu, bias=p.c_zero[:, 0:1], scale=1.0))

        out_dram = p.dram["x_dram"] if l0 else p.dram["outT"]
        xin_dram = p.dram["xT"] if l0 else p.dram["x_dram"]
        with ExitStack() as ep:
            WO = ep.enter_context(nc.sbuf_tensor(p.un("WOs"), [128, 2 * D], BF16))
            XB = ep.enter_context(nc.sbuf_tensor(p.un("epxb"), [128, 2 * TOK], F32))
            T2 = ep.enter_context(nc.sbuf_tensor(p.un("ept2"), [128, TOK], F32))
            psW = ep.enter_context(nc.psum_tensor(p.un("psW"), [128, TOK], F32))
            wo_rel = [None, None]
            wo_load = [None, None]
            wosem = p.take_slot_sems(2)

            def load_wo(oc):
                slot = oc % 2
                if wo_rel[slot] is not None:
                    wo_rel[slot].wait(g)
                wosem[slot].inc(g.dma_start(out=WO[:, slot * D:(slot + 1) * D],
                                            in_=wo[:, oc * D:(oc + 1) * D]), 16)
                wo_load[slot] = Rel(wosem[slot], wosem[slot].n)

            dma_fence(p)
            load_wo(0)
            for oc in range(CH):
                slot = oc % 2
                wo_load[slot].wait(t)
                if oc + 1 < CH:
                    load_wo(oc + 1)
                xsl = XB[:, slot * TOK:(slot + 1) * TOK]
                p.wall(s, p.sV)
                p.dS.inc(s.dma_start(out=xsl, in_=xin_dram[:, oc * TOK:(oc + 1) * TOK]), 16)
                p.wall(t, p.sV, p.sA)
                for nb in range(4):
                    for ic in range(CH):
                        p.sT.inc(t.matmul(
                            psW[:, nb * 512:(nb + 1) * 512],
                            WO[:, slot * D + ic * 128: slot * D + (ic + 1) * 128],
                            A_sb[:, ic * TOK + nb * 512: ic * TOK + (nb + 1) * 512],
                            start=(ic == 0), stop=(ic == CH - 1),
                            skip_group_check=True))
                wo_rel[slot] = Rel(p.sT, p.sT.n)
                p.wall(v, p.sT, p.dS)
                if l0:
                    p.sV.inc(v.scalar_tensor_tensor(p.TMP[:], psW[:], p.b0o[:, oc:oc + 1],
                                                    p.G[:, oc * TOK:(oc + 1) * TOK],
                                                    ALU.add, ALU.mult))
                    p.sV.inc(v.tensor_add(T2[:], xsl, p.TMP[:]))
                else:
                    p.sV.inc(v.tensor_tensor(p.TMP[:], psW[:],
                                             p.XN[:, oc * SEQ: oc * SEQ + TOK], ALU.add))
                    p.wall(a, p.sV)
                    p.sA.inc(a.activation(T2[:], p.G[:, oc * TOK:(oc + 1) * TOK],
                                          AF.Silu, bias=p.c_zero[:, 0:1], scale=1.0))
                    p.wall(v, p.sA)
                    p.sV.inc(v.tensor_mul(p.TMP[:], p.TMP[:], T2[:]))
                    p.sV.inc(v.tensor_add(T2[:], xsl, p.TMP[:]))
                p.wall(s, p.sV)
                p.dO.inc(s.dma_start(out=out_dram[:, oc * TOK:(oc + 1) * TOK], in_=T2[:]), 16)
                p.wall(v, p.dO)
            p.release_slot_sems(2)



def gating_moe(p):
    nc = p.nc
    v, s, g, a, t = nc.vector, nc.sync, nc.gpsimd, nc.scalar, nc.tensor
    d = p.dram

    with ExitStack() as ph:
        GT = ph.enter_context(nc.sbuf_tensor(p.un("GT"), [128, CH * NE], BF16))
        LG = ph.enter_context(nc.sbuf_tensor(p.un("LG"), [8, TOK], F32))
        LGt = ph.enter_context(nc.sbuf_tensor(p.un("LGt"), [128, 128], F32))
        MX = ph.enter_context(nc.sbuf_tensor(p.un("MX"), [128, 16], F32))
        RW = ph.enter_context(nc.sbuf_tensor(p.un("RW"), [128, 16], F32))
        TS = ph.enter_context(nc.sbuf_tensor(p.un("TS"), [128, 128], F32))
        dma_fence(p)
        p.dG.inc(g.dma_start(out=GT[:], in_=d["gateT"][:]), 16)
        gt_rel = Rel(p.dG, p.dG.n)

        with nc.psum_tensor(p.un("psG"), [32, TOK], F32) as psG:
            gt_rel.wait(t)
            p.wall(t, p.sV)
            for nb in range(4):
                for ic in range(CH):
                    p.sT.inc(t.matmul(psG[0:8, nb * 512:(nb + 1) * 512],
                                      GT[:, ic * 8:(ic + 1) * 8],
                                      p.XN[:, ic * SEQ + nb * 512: ic * SEQ + (nb + 1) * 512],
                                      start=(ic == 0), stop=(ic == CH - 1),
                                      skip_group_check=True))
            p.wall(v, p.sT)
            p.sV.inc(v.tensor_copy(LG[:], psG[0:8, :]))
        with nc.psum_tensor(p.un("psLt"), [128, 128], F32) as psLt:
            p.wall(t, p.sV)
            for tc in range(16):
                p.sT.inc(t.transpose(psLt[:, tc * 8:(tc + 1) * 8],
                                     LG[0:8, tc * 128:(tc + 1) * 128],
                                     p.eyef[0:8, 0:8]))
            p.wall(v, p.sT)
            p.sV.inc(v.tensor_copy(LGt[:], psLt[:]))
        for tc in range(16):
            p.sV.inc(v.reduce_max(MX[:, tc:tc + 1], LGt[:, tc * 8:(tc + 1) * 8], axis=AX.X))
        v.drain()
        p.wall(a, p.sV)
        p.sA.inc(a.activation(RW[:], MX[:], AF.Sigmoid, bias=p.c_zero[:, 0:1], scale=1.0))
        p.wall(v, p.sA)
        v.drain()
        for tc in range(16):
            p.sV.inc(v.tensor_scalar(TS[:, tc * 8:(tc + 1) * 8], LGt[:, tc * 8:(tc + 1) * 8],
                                     MX[:, tc:tc + 1], RW[:, tc:tc + 1],
                                     ALU.is_equal, ALU.mult))

        p.wall(v, p.sA)
        v.memset(p.G[:], 0.0)
        p.sV.inc(v.engine_nop())
        XS = ph.enter_context(nc.sbuf_tensor(p.un("XS"), [128, CH * TOK], BF16))
        PRD = ph.enter_context(nc.sbuf_tensor(p.un("PRD"), [128, TOK], BF16))
        WG = ph.enter_context(nc.sbuf_tensor(p.un("WGs"), [128, 2 * 4096], BF16))
        WU = ph.enter_context(nc.sbuf_tensor(p.un("WUs"), [128, 2 * 4096], BF16))
        WD = ph.enter_context(nc.sbuf_tensor(p.un("WDs"), [128, 2 * 4096], BF16))
        we_rel = [None, None]
        we_load = [None, None]
        wesem = p.take_slot_sems(2)

        def load_we(e):
            slot = e % 2
            if we_rel[slot] is not None:
                we_rel[slot].wait(g)
            for wdram, wbuf in ((d["wgT"], WG), (d["wuT"], WU), (d["wdT"], WD)):
                wesem[slot].inc(g.dma_start(out=wbuf[:, slot * 4096:(slot + 1) * 4096],
                                            in_=wdram[e]), 16)
            we_load[slot] = Rel(wesem[slot], wesem[slot].n)

        dma_fence(p)
        load_we(0)
        for e in range(NE):
            slot = e % 2
            we_load[slot].wait(t)
            if e + 1 < NE:
                load_we(e + 1)
            with nc.psum_tensor(p.un("pRow"), [1, TOK], F32) as pRow:
                p.wall(t, p.sV)
                for tc in range(16):
                    p.sT.inc(t.transpose(pRow[0:1, tc * 128:(tc + 1) * 128],
                                         TS[:, tc * 8 + e: tc * 8 + e + 1], p.eyef[:]))
                p.wall(v, p.sT)
                p.sV.inc(v.tensor_copy(p.SR[R_A:R_A + 1, :], pRow[0:1, :]))
            with nc.psum_tensor(p.un("pB2"), [128, TOK], F32) as pB2:
                p.wall(t, p.sV)
                bcast_row(p, pB2, p.SR[R_A:R_A + 1, :], TOK)
                p.wall(v, p.sT)
                for c in range(CH):
                    p.sV.inc(v.tensor_tensor(XS[:, c * TOK:(c + 1) * TOK],
                                             p.XN[:, c * SEQ: c * SEQ + TOK],
                                             pB2[:], ALU.mult))
            for tt in range(4):
                t0 = tt * 512
                with nc.psum_tensor(p.un("pGm"), [128, 2048], F32) as pG, \
                     nc.psum_tensor(p.un("pUm"), [128, 2048], F32) as pU:
                    p.wall(t, p.sV, p.sA)
                    for oc in range(4):
                        for ic in range(CH):
                            wof = slot * 4096 + ic * 512 + oc * 128
                            p.sT.inc(t.matmul(pG[:, oc * 512:(oc + 1) * 512],
                                              WG[:, wof:wof + 128],
                                              XS[:, ic * TOK + t0: ic * TOK + t0 + 512],
                                              start=(ic == 0), stop=(ic == CH - 1),
                                              skip_group_check=True))
                            p.sT.inc(t.matmul(pU[:, oc * 512:(oc + 1) * 512],
                                              WU[:, wof:wof + 128],
                                              XS[:, ic * TOK + t0: ic * TOK + t0 + 512],
                                              start=(ic == 0), stop=(ic == CH - 1),
                                              skip_group_check=True))
                    p.wall(a, p.sT)
                    p.sA.inc(a.activation(p.TMP[:], pG[:], AF.Silu,
                                          bias=p.c_zero[:, 0:1], scale=1.0))
                    p.wall(v, p.sA, p.sT)
                    p.sV.inc(v.scalar_tensor_tensor(PRD[:], pU[:], 1.0, p.TMP[:],
                                                    ALU.bypass, ALU.mult))
                with nc.psum_tensor(p.un("pDm"), [128, 4096], F32) as pD:
                    p.wall(t, p.sV)
                    for oc in range(CH):
                        for ic in range(4):
                            wof = slot * 4096 + ic * D + oc * 128
                            p.sT.inc(t.matmul(pD[:, oc * 512:(oc + 1) * 512],
                                              WD[:, wof:wof + 128],
                                              PRD[:, ic * 512:(ic + 1) * 512],
                                              start=(ic == 0), stop=(ic == 3),
                                              skip_group_check=True))
                    p.wall(v, p.sT)
                    gview = p.G[:].rearrange("q (c tk) -> q c tk", c=CH)[:, :, t0:t0 + 512]
                    dview = pD[:].rearrange("q (c tk) -> q c tk", c=CH)
                    p.sV.inc(v.tensor_tensor(gview, gview, dview, ALU.add))
            we_rel[slot] = Rel(p.sT, p.sT.n)


class StopEmission(Exception):
    pass


def _ck(p, name):
    if STOP_AFTER and name == STOP_AFTER:
        raise StopEmission()


def layer(p, lid):
    ln_phase(p, lid, inner=False)
    tap(p, f"xn{lid}", p.XN[:, 0:TOK], (p.sV,))
    _ck(p, f"ln{lid}")
    exchange_export(p, lid)
    _ck(p, f"exp{lid}")
    if lid == 0:
        ffn0(p)
    else:
        gating_moe(p)
    tap(p, f"g{lid}", p.G[:, 0:TOK], (p.sV, p.sA))
    _ck(p, f"ffn{lid}")
    exchange_import(p, lid)
    _ck(p, f"imp{lid}")
    if lid == 0:
        ln_phase(p, lid, inner=True)
        tap(p, "xni0", p.XN[:, 0:TOK], (p.sV,))
        _ck(p, "lni0")
    attention(p, lid)
    if lid == 0:
        tap(p, "x1", p.dram["x_dram"][:, 0:TOK], (p.dO,))
    _ck(p, f"att{lid}")


def finale(p):
    nc = p.nc
    p.wall(nc.sync, p.dO)
    p.wall(nc.sync, p.dS)



def _lay_cm(w):
    r, n = w.shape
    ch = r // 128
    return np.ascontiguousarray(w.reshape(ch, 128, n).transpose(1, 0, 2).reshape(128, ch * n))


def _lay_pm(w_io):
    b = w_io.reshape(8, 128, 8, 128)
    return np.ascontiguousarray(b.transpose(1, 2, 0, 3).reshape(128, 8192))


def _lay_col(vec):
    return np.ascontiguousarray(vec.reshape(8, 128).T)


def _build_rails(rm, rup):
    ytiles = np.zeros((32, 128, 128), np.float32)
    for h in range(4):
        for j in range(8):
            tl = ytiles[h * 8 + j]
            for a_ in range(8):
                r = 8 * j + a_
                tl[16 * a_:16 * a_ + 16, 16 * a_:16 * a_ + 16] = rm[h, r]
    ztiles = np.zeros((128, 128, 128), np.float32)
    for ci in range(128):
        h = ci // 32
        rr = ci % 32
        tl = ztiles[ci]
        for l_ in range(2):
            r = 2 * rr + l_
            tl[16 * (r % 8):16 * (r % 8) + 16, 64 * l_:64 * l_ + 64] = rup[h, r].T
    layY = np.ascontiguousarray(ytiles.transpose(1, 0, 2).reshape(128, 32 * 128))
    layZ = np.ascontiguousarray(ztiles.transpose(1, 0, 2).reshape(128, 128 * 128))
    return layY, layZ


def kernel(**inputs):
    global LAST_RESULTS
    f = lambda k: np.asarray(inputs[k], dtype=np.float32)

    x = f("L0_attn_sequence")
    shared = {
        "w0qT": _lay_pm(np.ascontiguousarray(f("L0_attn_Wq").T)),
        "w0kT": _lay_pm(np.ascontiguousarray(f("L0_attn_Wk").T)),
        "w0vT": _lay_pm(np.ascontiguousarray(f("L0_attn_Wv").T)),
        "w0oT": _lay_pm(np.ascontiguousarray(f("L0_attn_Wo").T)),
        "b0q": _lay_col(f("L0_attn_bq")),
        "b0k": _lay_col(f("L0_attn_bk")),
        "b0v": _lay_col(f("L0_attn_bv")),
        "b0o": _lay_col(f("L0_attn_bo")),
        "gam": _lay_col(f("L0_attn_ln_gamma")),
        "bet": _lay_col(f("L0_attn_ln_beta")),
        "mu": f("L0_attn_mu").reshape(1, 1),
        "cprojT": _lay_cm(np.ascontiguousarray(f("L0_ffn_w_cproj").T)),
        "w1qT": _lay_pm(f("L1_attn_Wq")),
        "w1kT": _lay_pm(f("L1_attn_Wk")),
        "w1vT": _lay_pm(f("L1_attn_Wv")),
        "w1postT": _lay_pm(f("L1_attn_Wpost")),
        "gateT": _lay_cm(np.ascontiguousarray(f("L1_ffn_gate_w").T)),
        "wgT": np.stack([_lay_cm(np.ascontiguousarray(f("L1_ffn_w_gate")[e].T))
                         for e in range(NE)]),
        "wuT": np.stack([_lay_cm(np.ascontiguousarray(f("L1_ffn_w_up")[e].T))
                         for e in range(NE)]),
        "wdT": np.stack([_lay_cm(np.ascontiguousarray(f("L1_ffn_w_down")[e].T))
                         for e in range(NE)]),
        "eye": np.eye(128, dtype=np.float32),
    }
    shared["railsY"], shared["railsZ"] = _build_rails(
        f("L0_ffn_rail_maker"), f("L0_ffn_rail_up_proj"))

    nc = build_program(taps=DEBUG_TAPS, emit_upto=EMIT_UPTO)

    in_maps = []
    for c in range(8):
        b, par = c // 2, c % 2
        xs = np.ascontiguousarray(x[b, par * TOK:(par + 1) * TOK, :].T)
        m = dict(shared)
        m["xT"] = _lay_cm(xs)
        m["mlo"] = np.full((128, 1), float(par), np.float32)
        m["mhi"] = np.full((128, 1), float(1 - par), np.float32)
        in_maps.append(m)

    res = run_bass_kernel_spmd(nc, in_maps, core_ids=list(range(8)))
    LAST_RESULTS = res.results

    out = np.zeros((4, SEQ, D), np.float32)
    for c in range(8):
        b, par = c // 2, c % 2
        o = res.results[c]["outT"]
        o = o.reshape(128, CH, TOK).transpose(1, 0, 2).reshape(D, TOK).T
        out[b, par * TOK:(par + 1) * TOK, :] = o
    return out


# version 07
# speedup vs baseline: 1.5989x; 1.0073x over previous
import numpy as np
from contextlib import ExitStack

import concourse.bass as bass
import concourse.mybir as mybir
from concourse.bass_utils import run_bass_kernel_spmd

F32 = mybir.dt.float32
BF16 = mybir.dt.bfloat16
AF = mybir.ActivationFunctionType
ALU = mybir.AluOpType
AX = mybir.AxisListType

D = 1024
CH = 8
TOK = 2048
SEQ = 4096
NH = 16
NPAIR = 8
NE = 8
FCH = 128

LOG2 = float(np.log(2.0))

DEBUG_TAPS = {}
EMIT_UPTO = 2
STOP_AFTER = ""
PARANOID = 0
NO_BOUNCE = False
LAST_RESULTS = None

R_A = 0
R_B = 32
R_V = 64
R_M = 96


class Sem:
    def __init__(self, ctx, nc, name):
        self.h = ctx.enter_context(nc.semaphore(name))
        self.n = 0

    def inc(self, instr, k=1):
        instr.then_inc(self.h, k)
        self.n += k
        return instr

    def wait(self, eng, n=None):
        eng.wait_ge(self.h, self.n if n is None else n)


class Rel:
    def __init__(self, sem, n):
        self.sem = sem
        self.n = n

    def wait(self, eng):
        eng.wait_ge(self.sem.h, self.n)


class P:
    def take_slot_sems(self, n):
        out = self.slot_sems[self._slot_next:self._slot_next + n]
        assert len(out) == n, "out of slot sems"
        self._slot_next += n
        return out

    def release_slot_sems(self, n):
        self._slot_next -= n

    def un(self, base):
        self._uc = getattr(self, "_uc", 0) + 1
        return f"{base}_{self._uc}"


def build_program(taps=None, emit_upto=2):
    nc = bass.Bass("TRN2", target_bir_lowering=False, debug=False)
    st = ExitStack()
    p = P()
    p.nc = nc
    p.st = st
    p.taps = dict(taps or {})

    dram = {}

    def din(name, shape, dtype=F32):
        dram[name] = nc.dram_tensor(name, shape, dtype, kind="ExternalInput").ap()

    def dint(name, shape, dtype=F32):
        dram[name] = nc.dram_tensor(name, shape, dtype, kind="Internal").ap()

    def dout(name, shape, dtype=F32):
        dram[name] = nc.dram_tensor(name, shape, dtype, kind="ExternalOutput").ap()

    din("xT", [128, CH * TOK])
    for nm in ("w0qT", "w0kT", "w0vT", "w0oT", "w1qT", "w1kT", "w1vT", "w1postT"):
        din(nm, [128, NPAIR * D])
    for nm in ("b0q", "b0k", "b0v", "b0o", "gam", "bet"):
        din(nm, [128, 8])
    din("mu", [1, 1])
    din("railsY", [128, 32 * 128])
    din("railsZ", [128, 128 * 128])
    din("cprojT", [128, FCH * D])
    din("gateT", [128, CH * NE])
    din("wgT", [NE, 128, CH * 512])
    din("wuT", [NE, 128, CH * 512])
    din("wdT", [NE, 128, 4 * D])
    din("mlo", [128, 1])
    din("mhi", [128, 1])
    din("eye", [128, 128])
    dint("x_dram", [128, CH * TOK])
    dint("cc_in0", [128, CH * TOK], BF16)
    dint("cc_out0", [256, CH * TOK], BF16)
    dint("cc_in1", [128, CH * TOK], BF16)
    dint("cc_out1", [256, CH * TOK], BF16)
    dint("cprojT_bf", [128, FCH * D], BF16)
    dout("outT", [128, CH * TOK])
    p.dram = dram

    p.tap_out = {}
    for tname, shape in p.taps.items():
        dout("tap_" + tname, list(shape), F32)
        p.tap_out[tname] = dram["tap_" + tname]

    p.dS = Sem(st, nc, "dS")
    p.dG = Sem(st, nc, "dG")
    p.sT = Sem(st, nc, "sT")
    p.sA = Sem(st, nc, "sA")
    p.sV = Sem(st, nc, "sV")
    p.sC = Sem(st, nc, "sC")
    p.dO = Sem(st, nc, "dO")
    p.slot_sems = [Sem(st, nc, f"sl{i}") for i in range(12)]
    p._slot_next = 0

    def wall(eng, *sems):
        for s_ in sems:
            s_.wait(eng)
    p.wall = wall

    sb = st.enter_context
    p.XN = sb(nc.sbuf_tensor("XN", [128, CH * SEQ], BF16))
    p.G = sb(nc.sbuf_tensor("G", [128, CH * TOK], BF16))
    p.TMP = sb(nc.sbuf_tensor("TMP", [128, TOK], F32))
    p.SR = sb(nc.sbuf_tensor("SR", [128, TOK], F32))
    p.c_ones_b = sb(nc.sbuf_tensor("c_ones_b", [128, 1], BF16))
    p.c_onesr_f = sb(nc.sbuf_tensor("c_onesr_f", [128, 128], F32))
    p.c_ln2 = sb(nc.sbuf_tensor("c_ln2", [128, 1], F32))
    p.c_mu2 = sb(nc.sbuf_tensor("c_mu2", [128, 1], F32))
    p.c_zero = sb(nc.sbuf_tensor("c_zero", [128, 1], F32))
    p.c_eps1 = sb(nc.sbuf_tensor("c_eps1", [128, 1], F32))
    p.c_eps2 = sb(nc.sbuf_tensor("c_eps2", [128, 1], F32))
    p.b0q = sb(nc.sbuf_tensor("b0q_s", [128, 8], F32))
    p.b0k = sb(nc.sbuf_tensor("b0k_s", [128, 8], F32))
    p.b0v = sb(nc.sbuf_tensor("b0v_s", [128, 8], F32))
    p.b0o = sb(nc.sbuf_tensor("b0o_s", [128, 8], F32))
    p.gam = sb(nc.sbuf_tensor("gam_s", [128, 8], F32))
    p.bet = sb(nc.sbuf_tensor("bet_s", [128, 8], F32))
    p.mlo = sb(nc.sbuf_tensor("mlo_s", [128, 1], F32))
    p.mhi = sb(nc.sbuf_tensor("mhi_s", [128, 1], F32))
    p.eyeb = sb(nc.sbuf_tensor("eyeb", [128, 128], BF16))
    p.eyef = sb(nc.sbuf_tensor("eyef", [128, 128], F32))

    prologue(p)
    try:
        for lid in range(min(2, emit_upto)):
            layer(p, lid)
    except StopEmission:
        pass
    finale(p)

    st.close()
    return nc


def tap(p, name, src_ap, sems):
    if name not in p.tap_out:
        return
    nc = p.nc
    eng = nc.gpsimd if src_ap.dtype != F32 else nc.sync
    p.wall(eng, *sems)
    p.dO.inc(eng.dma_start(out=p.tap_out[name][:], in_=src_ap), 16)


def prologue(p):
    nc = p.nc
    v, s, g, a, t = nc.vector, nc.sync, nc.gpsimd, nc.scalar, nc.tensor
    d = p.dram
    v.memset(p.c_ones_b[:], 1.0)
    v.memset(p.c_onesr_f[:], 1.0)
    v.memset(p.c_ln2[:], -LOG2)
    v.memset(p.c_zero[:], 0.0)
    v.memset(p.c_eps1[:], 1e-5)
    v.memset(p.c_eps2[:], 1e-6)
    p.sV.inc(v.engine_nop())
    for nm, t_sb in (("b0q", p.b0q), ("b0k", p.b0k), ("b0v", p.b0v), ("b0o", p.b0o),
                     ("gam", p.gam), ("bet", p.bet), ("mlo", p.mlo), ("mhi", p.mhi),
                     ("eye", p.eyef)):
        p.dS.inc(s.dma_start(out=t_sb[:], in_=d[nm][:]), 16)
    p.dG.inc(g.dma_start(out=p.eyeb[:], in_=d["eye"][:]), 16)
    mut = p.st.enter_context(nc.sbuf_tensor("mut", [1, 1], F32))
    with nc.psum_tensor("pmu", [128, 1], F32) as pmu:
        p.dS.inc(s.dma_start(out=mut[:], in_=d["mu"][:]), 16)
        p.wall(t, p.dS, p.sV)
        p.sT.inc(t.matmul(pmu[:], p.c_onesr_f[0:1, :], mut[:], start=True, stop=True))
        p.wall(a, p.sT, p.sV)
        p.sA.inc(a.activation(p.c_mu2[:], pmu[:], AF.Copy, bias=0.0, scale=-0.5))
        a.drain()


def dma_fence(p):
    nc = p.nc
    p.wall(nc.sync, p.sT, p.sA, p.sV)
    p.wall(nc.gpsimd, p.sT, p.sA, p.sV)


def bcast_row(p, psdst, row_ap, width):
    t = p.nc.tensor
    rb = row_ap.base_partition()
    for nb in range(width // 512):
        p.sT.inc(t.matmul(psdst[:, nb * 512:(nb + 1) * 512],
                          p.c_onesr_f[rb:rb + 1, :],
                          row_ap[0:1, nb * 512:(nb + 1) * 512],
                          start=True, stop=True, skip_group_check=True))


def ln_phase(p, layer_id, inner):
    nc = p.nc
    v, s, a, t = nc.vector, nc.sync, nc.scalar, nc.tensor
    nhalf = 2 if inner else 1
    src = p.dram["xT"] if layer_id == 0 else p.dram["x_dram"]

    p.wall(v, p.dS, p.sT)
    p.wall(s, p.dO)
    dma_fence(p)
    with ExitStack() as ph:
        SQ = ph.enter_context(nc.sbuf_tensor(p.un("lnsq"), [128, TOK], BF16))
        XB = None
        if not inner:
            XB = ph.enter_context(nc.sbuf_tensor(p.un("lnxb"), [128, 2 * TOK], F32))
        for half in range(nhalf):
            hs = half * TOK
            pstat_cm = nc.psum_tensor(p.un("pstat"), [64, TOK], F32)
            pstat = pstat_cm.__enter__()
            for c in range(CH):
                if not inner:
                    xsl = XB[:, (c % 2) * TOK:(c % 2 + 1) * TOK]
                    p.wall(s, p.sV)
                    p.dS.inc(s.dma_start(out=xsl, in_=src[:, c * TOK:(c + 1) * TOK]), 16)
                    p.wall(v, p.dS)
                    xb = p.XN[:, c * SEQ + TOK: c * SEQ + 2 * TOK]
                    p.sV.inc(v.tensor_copy(xb, xsl))
                else:
                    xb = p.XN[:, c * SEQ + hs: c * SEQ + hs + TOK]
                p.wall(v, p.sT)
                p.sV.inc(v.tensor_mul(SQ[:], xb, xb))
                p.wall(t, p.sV)
                for nb in range(4):
                    n0 = nb * 512
                    p.sT.inc(t.matmul(pstat[0:1, n0:n0 + 512], p.c_ones_b[:],
                                      xb[:, n0:n0 + 512], start=(c == 0),
                                      stop=(c == CH - 1), skip_group_check=True))
                    p.sT.inc(t.matmul(pstat[32:33, n0:n0 + 512], p.c_ones_b[:],
                                      SQ[:, n0:n0 + 512], start=(c == 0),
                                      stop=(c == CH - 1), skip_group_check=True))
            p.wall(v, p.sT)
            mrow = p.SR[R_M:R_M + 1, :]
            vrow = p.SR[R_V:R_V + 1, :]
            arow = p.SR[R_A:R_A + 1, :]
            brow = p.SR[R_B:R_B + 1, :]
            p.sV.inc(v.tensor_scalar_mul(mrow, pstat[0:1, :], 1.0 / D))
            p.sV.inc(v.tensor_mul(vrow, mrow, mrow))
            p.sV.inc(v.scalar_tensor_tensor(vrow, pstat[32:33, :], 1.0 / D, vrow,
                                            ALU.mult, ALU.subtract))
            epsc = p.c_eps2 if inner else p.c_eps1
            p.wall(a, p.sV)
            p.sA.inc(a.activation(brow, vrow, AF.Ln, bias=epsc[0:1, 0:1], scale=1.0))
            a.drain()
            p.sA.inc(a.activation(arow, brow, AF.Exp, bias=p.c_zero[0:1, 0:1], scale=-0.5))
            p.wall(v, p.sA)
            p.sV.inc(v.scalar_tensor_tensor(brow, pstat[0:1, :], -1.0 / D, arow,
                                            ALU.mult, ALU.mult))
            pstat_cm.__exit__(None, None, None)
            pbca_cm = nc.psum_tensor(p.un("pbca"), [128, TOK], F32)
            pbcb_cm = nc.psum_tensor(p.un("pbcb"), [128, TOK], F32)
            pbca = pbca_cm.__enter__()
            pbcb = pbcb_cm.__enter__()
            p.wall(t, p.sV)
            bcast_row(p, pbca, arow, TOK)
            bcast_row(p, pbcb, brow, TOK)
            for c in range(CH):
                if inner:
                    xb = p.XN[:, c * SEQ + hs: c * SEQ + hs + TOK]
                    xout = xb
                else:
                    xb = p.XN[:, c * SEQ + TOK: c * SEQ + 2 * TOK]
                    xout = p.XN[:, c * SEQ: c * SEQ + TOK]
                p.wall(v, p.sT)
                p.sV.inc(v.tensor_tensor(p.TMP[:], xb, pbca[:], ALU.mult))
                if inner:
                    p.sV.inc(v.tensor_tensor(p.TMP[:], p.TMP[:], pbcb[:], ALU.add))
                    p.sV.inc(v.tensor_scalar(xout, p.TMP[:], p.gam[:, c:c + 1],
                                             p.bet[:, c:c + 1], ALU.mult, ALU.add))
                else:
                    p.sV.inc(v.tensor_tensor(xout, p.TMP[:], pbcb[:], ALU.add))
            p.wall(t, p.sV)
            pbcb_cm.__exit__(None, None, None)
            pbca_cm.__exit__(None, None, None)


def exchange_export(p, layer_id):
    nc = p.nc
    s, g = nc.sync, nc.gpsimd
    cc_in = p.dram[f"cc_in{layer_id}"]
    cc_out = p.dram[f"cc_out{layer_id}"]
    p.wall(s, p.sV)
    for c in range(CH):
        p.dS.inc(s.dma_start(out=cc_in[:, c * TOK:(c + 1) * TOK],
                             in_=p.XN[:, c * SEQ: c * SEQ + TOK]), 16)
    p.wall(g, p.dS)
    p.sC.inc(g.collective_compute(
        "AllGather", ALU.bypass, ins=[cc_in[:]], outs=[cc_out[:]],
        replica_groups=[[0, 1], [2, 3], [4, 5], [6, 7]]))


def exchange_import(p, layer_id):
    nc = p.nc
    s, v = nc.sync, nc.vector
    cc_out = p.dram[f"cc_out{layer_id}"]
    p.wall(s, p.sC)
    dma_fence(p)
    with nc.sbuf_tensor(p.un("imA"), [128, TOK], BF16) as imA, \
         nc.sbuf_tensor(p.un("imB"), [128, TOK], BF16) as imB:
        for c in range(CH):
            p.wall(s, p.sV)
            p.dS.inc(s.dma_start(out=imA[:], in_=cc_out[0:128, c * TOK:(c + 1) * TOK]), 16)
            p.dS.inc(s.dma_start(out=imB[:], in_=cc_out[128:256, c * TOK:(c + 1) * TOK]), 16)
            p.wall(v, p.dS, p.sT)
            xo = p.XN[:, c * SEQ + TOK: c * SEQ + 2 * TOK]
            p.sV.inc(v.tensor_scalar(p.TMP[:], imA[:], p.mlo[:, 0:1], None, ALU.mult))
            p.sV.inc(v.scalar_tensor_tensor(xo, imB[:], p.mhi[:, 0:1], p.TMP[:],
                                            ALU.mult, ALU.add))



def ffn0(p):
    nc = p.nc
    v, s, g, a, t = nc.vector, nc.sync, nc.gpsimd, nc.scalar, nc.tensor
    d = p.dram
    NTG = 8
    TW = 256

    with ExitStack() as ph:
        RTY = ph.enter_context(nc.sbuf_tensor(p.un("RTY"), [128, 32 * 128], BF16))
        RTZ = ph.enter_context(nc.sbuf_tensor(p.un("RTZ"), [128, 128 * 128], BF16))
        YB = ph.enter_context(nc.sbuf_tensor(p.un("YB"), [128, 32 * TW], BF16))
        H4 = ph.enter_context(nc.sbuf_tensor(p.un("H4"), [128, 2048], BF16))
        WA = ph.enter_context(nc.sbuf_tensor(p.un("WA"), [128, 4 * D], BF16))
        def GTOK(tc):
            return p.XN[:, (tc // 2) * SEQ + TOK + (tc % 2) * D:
                        (tc // 2) * SEQ + TOK + (tc % 2 + 1) * D]

        dma_fence(p)
        p.dG.inc(g.dma_start(out=RTY[:], in_=d["railsY"][:]), 16)
        for q in range(4):
            p.dG.inc(g.dma_start(out=RTZ[:, q * 4096:(q + 1) * 4096],
                                 in_=d["railsZ"][:, q * 4096:(q + 1) * 4096]), 16)
        rails_rel = Rel(p.dG, p.dG.n)

        wa_rel = [None, None]
        wa_bounce = [None, None]
        wa_load = [None, None]
        ldsem = p.take_slot_sems(2)
        bnsem = p.take_slot_sems(2)

        def load_w2(j, tg):
            slot = j % 2
            dst = WA[:, slot * 2 * D:(slot + 1) * 2 * D]
            if NO_BOUNCE or tg == 0:
                if wa_rel[slot] is not None:
                    wa_rel[slot].wait(g)
                if wa_bounce[slot] is not None:
                    wa_bounce[slot].wait(g)
                ldsem[slot].inc(g.dma_start(out=dst,
                                            in_=d["cprojT"][:, 2 * j * D:(2 * j + 2) * D]), 16)
                wa_load[slot] = Rel(ldsem[slot], ldsem[slot].n)
                if not NO_BOUNCE:
                    wa_load[slot].wait(s)
                    bnsem[slot].inc(s.dma_start(
                        out=d["cprojT_bf"][:, 2 * j * D:(2 * j + 2) * D], in_=dst), 16)
                    wa_bounce[slot] = Rel(bnsem[slot], bnsem[slot].n)
            else:
                eng = s if (j % 2 == 0) else g
                if wa_rel[slot] is not None:
                    wa_rel[slot].wait(eng)
                ldsem[slot].inc(eng.dma_start(
                    out=dst, in_=d["cprojT_bf"][:, 2 * j * D:(2 * j + 2) * D]), 16)
                wa_load[slot] = Rel(ldsem[slot], ldsem[slot].n)

        with nc.psum_tensor(p.un("pz"), [128, 2048], F32) as pz, \
             nc.psum_tensor(p.un("pot"), [128, 2048], F32) as pot:
            py = pz
            gelu_rel = [None, None]
            for tg in range(NTG):
                t0 = tg * TW
                if tg == 0:
                    rails_rel.wait(t)
                    p.wall(t, p.sV)
                if tg == 1:
                    p.wall(s, bnsem[0], bnsem[1])
                    p.wall(g, bnsem[0], bnsem[1])
                load_w2(0, tg)
                load_w2(1, tg)
                p.wall(t, p.sA)
                yev = [None, None]
                for yt in range(32):
                    j = yt % 8
                    half = yt % 2
                    if yev[half] is not None:
                        yev[half].wait(t)
                    elif yt < 2:
                        p.wall(t, p.sV)
                    p.sT.inc(t.matmul(py[:, half * 512: half * 512 + TW],
                                      RTY[:, yt * 128:(yt + 1) * 128],
                                      p.XN[:, j * SEQ + t0: j * SEQ + t0 + TW],
                                      start=True, stop=True, skip_group_check=True))
                    v.wait_ge(p.sT.h, p.sT.n)
                    p.sV.inc(v.tensor_copy(YB[:, yt * TW:(yt + 1) * TW],
                                           py[:, half * 512: half * 512 + TW]))
                    yev[half] = Rel(p.sV, p.sV.n)
                for ib in range(32):
                    zh = ib % 2
                    for k in range(4):
                        i = 4 * ib + k
                        if k == 0:
                            if gelu_rel[zh] is not None:
                                gelu_rel[zh].wait(t)
                            else:
                                p.wall(t, p.sA)
                            if ib == 0:
                                p.wall(t, p.sV)
                        p.sT.inc(t.matmul(pz[:, zh * 1024 + k * TW: zh * 1024 + (k + 1) * TW],
                                          RTZ[:, i * 128:(i + 1) * 128],
                                          YB[:, (i // 4) * TW:(i // 4 + 1) * TW],
                                          start=True, stop=True, skip_group_check=True))
                    zmm_done = p.sT.n
                    a.wait_ge(p.sT.h, zmm_done)
                    if ib > 1:
                        p.wall(a, p.sT)
                    p.sA.inc(a.activation(H4[:, zh * 1024:(zh + 1) * 1024],
                                          pz[:, zh * 1024:(zh + 1) * 1024], AF.Gelu,
                                          bias=p.c_zero[:, 0:1], scale=1.0))
                    gelu_rel[zh] = Rel(p.sA, p.sA.n)
                    gelu_cnt = p.sA.n
                    for k in range(4):
                        i = 4 * ib + k
                        j = i // 2
                        slot = j % 2
                        woff = slot * 2 * D + (i % 2) * D
                        wa_load[slot].wait(t)
                        t.wait_ge(p.sA.h, gelu_cnt)
                        for tc in range(2):
                            hsl = H4[:, zh * 1024 + k * TW + tc * 128:
                                     zh * 1024 + k * TW + (tc + 1) * 128]
                            for ob in range(2):
                                p.sT.inc(t.matmul(
                                    pot[:, tc * 1024 + ob * 512: tc * 1024 + (ob + 1) * 512],
                                    hsl,
                                    WA[:, woff + ob * 512: woff + (ob + 1) * 512],
                                    start=(i == 0), stop=(i == 127),
                                    skip_group_check=True))
                        if i % 2 == 1:
                            wa_rel[slot] = Rel(p.sT, p.sT.n)
                            if j + 2 < 64:
                                load_w2(j + 2, tg)
                p.wall(v, p.sT)
                for tc in range(2):
                    p.sV.inc(v.tensor_copy(GTOK(2 * tg + tc),
                                           pot[:, tc * 1024:(tc + 1) * 1024]))
                p.wall(t, p.sV)

        p.release_slot_sems(4)
        with nc.psum_tensor(p.un("pgt"), [128, 2048], BF16) as pgt:
            for fc in range(CH):
                p.wall(t, p.sA, p.sV)
                for tc in range(16):
                    p.sT.inc(t.transpose(pgt[:, tc * 128:(tc + 1) * 128],
                                         GTOK(tc)[:, fc * 128:(fc + 1) * 128],
                                         p.eyeb[:]))
                p.wall(a, p.sT)
                p.sA.inc(a.activation(p.G[:, fc * TOK:(fc + 1) * TOK], pgt[:],
                                      AF.Silu, bias=p.c_zero[:, 0:1], scale=1.0))



def attention(p, layer_id):
    nc = p.nc
    v, s, g, a, t = nc.vector, nc.sync, nc.gpsimd, nc.scalar, nc.tensor
    d = p.dram
    l0 = (layer_id == 0)
    wq = d["w0qT"] if l0 else d["w1qT"]
    wk = d["w0kT"] if l0 else d["w1kT"]
    wv = d["w0vT"] if l0 else d["w1vT"]
    wo = d["w0oT"] if l0 else d["w1postT"]

    with ExitStack() as ph:
        A_sb = ph.enter_context(nc.sbuf_tensor(p.un("A_sb"), [128, NPAIR * TOK], BF16))
        with ExitStack() as core:
            K_sb = core.enter_context(nc.sbuf_tensor(p.un("K_sb"), [128, SEQ], BF16))
            VT_sb = core.enter_context(nc.sbuf_tensor(p.un("VT_sb"), [128, SEQ], BF16))
            Vsb = core.enter_context(nc.sbuf_tensor(p.un("Vsb"), [128, 32 * 130], BF16))
            Q_sb = core.enter_context(nc.sbuf_tensor(p.un("Q_sb"), [128, TOK], BF16))
            U = core.enter_context(nc.sbuf_tensor(p.un("U"), [128, 2 * TOK], BF16))
            PB = core.enter_context(nc.sbuf_tensor(p.un("PB"), [128, 4 * 1024], BF16))
            WQ = core.enter_context(nc.sbuf_tensor(p.un("WQs"), [128, 2 * D], BF16))
            WK = core.enter_context(nc.sbuf_tensor(p.un("WKs"), [128, 2 * D], BF16))
            WV = core.enter_context(nc.sbuf_tensor(p.un("WVs"), [128, 2 * D], BF16))
            psSc = core.enter_context(nc.psum_tensor(p.un("psSc"), [128, TOK], F32))
            psB = core.enter_context(nc.psum_tensor(p.un("psB"), [128, TOK], F32))

            wsl_rel = [None, None]
            wsl_load = [None, None]
            wssem = p.take_slot_sems(2)

            def load_pair_w(hp):
                slot = hp % 2
                if wsl_rel[slot] is not None:
                    wsl_rel[slot].wait(g)
                for wdram, wbuf in ((wq, WQ), (wk, WK), (wv, WV)):
                    wssem[slot].inc(g.dma_start(out=wbuf[:, slot * D:(slot + 1) * D],
                                                in_=wdram[:, hp * D:(hp + 1) * D]), 16)
                wsl_load[slot] = Rel(wssem[slot], wssem[slot].n)

            act_read = [None, None]
            u_free = [None, None]
            vt_free = [None, None]
            pb_free = [None, None, None, None]

            def proj(wbuf, slot, out_sb, bias_col, ntok):
                evac_rel = [None, None, None, None]
                for nb in range(ntok // 512):
                    q4 = nb % 4
                    if evac_rel[q4] is not None:
                        evac_rel[q4].wait(t)
                    else:
                        p.wall(t, p.sV)
                    for ic in range(CH):
                        p.sT.inc(t.matmul(
                            psSc[:, q4 * 512:(q4 + 1) * 512],
                            wbuf[:, slot * D + ic * 128: slot * D + (ic + 1) * 128],
                            p.XN[:, ic * SEQ + nb * 512: ic * SEQ + (nb + 1) * 512],
                            start=(ic == 0), stop=(ic == CH - 1),
                            skip_group_check=True))
                    v.wait_ge(p.sT.h, p.sT.n)
                    if bias_col is not None:
                        p.sV.inc(v.tensor_scalar(out_sb[:, nb * 512:(nb + 1) * 512],
                                                 psSc[:, q4 * 512:(q4 + 1) * 512],
                                                 bias_col, None, ALU.add))
                    else:
                        p.sV.inc(v.tensor_copy(out_sb[:, nb * 512:(nb + 1) * 512],
                                               psSc[:, q4 * 512:(q4 + 1) * 512]))
                    evac_rel[q4] = Rel(p.sV, p.sV.n)

            def normalize_pair(hp_):
                p.wall(t, p.sV, p.sA)
                bcast_row(p, psSc, p.SR[R_A:R_A + 1, :], TOK)
                p.wall(v, p.sT)
                p.sV.inc(v.tensor_tensor(A_sb[0:64, hp_ * TOK:(hp_ + 1) * TOK],
                                         A_sb[0:64, hp_ * TOK:(hp_ + 1) * TOK],
                                         psSc[0:64, :], ALU.mult))
                p.wall(t, p.sV)
                bcast_row(p, psSc, p.SR[R_B:R_B + 1, :], TOK)
                p.wall(v, p.sT)
                p.sV.inc(v.tensor_tensor(A_sb[64:128, hp_ * TOK:(hp_ + 1) * TOK],
                                         A_sb[64:128, hp_ * TOK:(hp_ + 1) * TOK],
                                         psSc[64:128, :], ALU.mult))
                p.wall(t, p.sV)

            dma_fence(p)
            load_pair_w(0)
            for hp in range(NPAIR):
                slot = hp % 2
                wsl_load[slot].wait(t)
                if hp + 1 < NPAIR:
                    load_pair_w(hp + 1)
                p.wall(t, p.sA)
                proj(WK, slot, K_sb, p.b0k[:, hp:hp + 1] if l0 else None, SEQ)
                if hp > 0:
                    normalize_pair(hp - 1)
                proj(WV, slot, VT_sb, p.b0v[:, hp:hp + 1] if l0 else None, SEQ)
                proj(WQ, slot, Q_sb, p.b0q[:, hp:hp + 1] if l0 else None, TOK)
                wsl_rel[slot] = Rel(p.sT, p.sT.n)
                psBb = psB[:].bitcast(BF16)
                p.wall(t, p.sV)
                for tc in range(32):
                    p.sT.inc(t.transpose(psBb[:, tc * 128:(tc + 1) * 128],
                                         VT_sb[:, tc * 128:(tc + 1) * 128], p.eyeb[:]))
                p.wall(v, p.sT)
                v.memset(Vsb[:], 1.0)
                src3 = psBb[:, 0:4096].rearrange("q (tc h c) -> q tc h c",
                                                 tc=32, h=2, c=64)
                dst3 = Vsb[:].rearrange("q (tc h c) -> q tc h c",
                                        tc=32, h=2, c=65)[:, :, :, 0:64]
                p.sV.inc(v.tensor_copy(dst3, src3))
                vsb_ready = p.sV.n
                for hh in range(2):
                    rb = 64 * hh
                    pend = []
                    for kc in range(32):
                        ksl = K_sb[rb:rb + 64, kc * 128:(kc + 1) * 128]
                        for qh in range(2):
                            it = (kc * 2 + qh)
                            sb = it % 2
                            sb3 = it % 4
                            q0 = qh * 1024
                            if act_read[sb] is not None:
                                act_read[sb].wait(t)
                            if kc == 0 and qh == 0 and hh == 0:
                                t.wait_ge(p.sV.h, vsb_ready)
                            for nb in range(2):
                                p.sT.inc(t.matmul(
                                    psSc[:, sb * 1024 + nb * 512: sb * 1024 + (nb + 1) * 512],
                                    ksl,
                                    Q_sb[rb:rb + 64, q0 + nb * 512: q0 + (nb + 1) * 512],
                                    start=True, stop=True, skip_group_check=True))
                            sc_done = p.sT.n
                            a.wait_ge(p.sT.h, sc_done)
                            scsl = psSc[:, sb * 1024:(sb + 1) * 1024]
                            if l0:
                                if u_free[sb] is not None:
                                    u_free[sb].wait(a)
                                p.sA.inc(a.activation(U[:, sb * 1024:(sb + 1) * 1024],
                                                      scsl, AF.Exp,
                                                      bias=p.c_ln2[:, 0:1], scale=0.125))
                                if vt_free[sb] is not None:
                                    vt_free[sb].wait(a)
                                p.sA.inc(a.activation(p.TMP[:, sb * 1024:(sb + 1) * 1024],
                                                      scsl, AF.Tanh,
                                                      bias=p.c_mu2[:, 0:1], scale=0.0625))
                                act_read[sb] = Rel(p.sA, p.sA.n)
                                v.wait_ge(p.sA.h, p.sA.n)
                                if pb_free[sb3] is not None:
                                    pb_free[sb3].wait(v)
                                p.sV.inc(v.scalar_tensor_tensor(
                                    PB[:, sb3 * 1024:(sb3 + 1) * 1024],
                                    p.TMP[:, sb * 1024:(sb + 1) * 1024], 1.0,
                                    U[:, sb * 1024:(sb + 1) * 1024],
                                    ALU.add, ALU.mult))
                                u_free[sb] = Rel(p.sV, p.sV.n)
                                vt_free[sb] = Rel(p.sV, p.sV.n)
                                pready = Rel(p.sV, p.sV.n)
                            else:
                                if pb_free[sb3] is not None:
                                    pb_free[sb3].wait(a)
                                p.sA.inc(a.activation(PB[:, sb3 * 1024:(sb3 + 1) * 1024],
                                                      scsl, AF.Exp,
                                                      bias=p.c_zero[:, 0:1], scale=0.125))
                                act_read[sb] = Rel(p.sA, p.sA.n)
                                pready = Rel(p.sA, p.sA.n)
                            vsl = Vsb[:, kc * 130 + hh * 65: kc * 130 + hh * 65 + 65]
                            pend.append((vsl, sb3, q0, kc == 0, kc == 31, pready))
                            if len(pend) > 3:
                                _vsl, _sb3, _q0, _st, _sp, _pr = pend.pop(0)
                                _pr.wait(t)
                                for nb in range(2):
                                    p.sT.inc(t.matmul(
                                        psB[0:65, _q0 + nb * 512: _q0 + (nb + 1) * 512],
                                        _vsl,
                                        PB[:, _sb3 * 1024 + nb * 512: _sb3 * 1024 + (nb + 1) * 512],
                                        start=_st, stop=_sp, skip_group_check=True))
                                pb_free[_sb3] = Rel(p.sT, p.sT.n)
                    while pend:
                        _vsl, _sb3, _q0, _st, _sp, _pr = pend.pop(0)
                        _pr.wait(t)
                        for nb in range(2):
                            p.sT.inc(t.matmul(
                                psB[0:65, _q0 + nb * 512: _q0 + (nb + 1) * 512], _vsl,
                                PB[:, _sb3 * 1024 + nb * 512: _sb3 * 1024 + (nb + 1) * 512],
                                start=_st, stop=_sp, skip_group_check=True))
                        pb_free[_sb3] = Rel(p.sT, p.sT.n)
                    p.wall(v, p.sT)
                    p.sV.inc(v.tensor_copy(A_sb[rb:rb + 64, hp * TOK:(hp + 1) * TOK],
                                           psB[0:64, :]))
                    rsrow = p.SR[(R_V if hh == 0 else R_M):(R_V if hh == 0 else R_M) + 1, :]
                    p.sV.inc(v.tensor_copy(rsrow, psB[64:65, :]))
                    p.wall(t, p.sV)
                p.sV.inc(v.reciprocal(p.SR[R_A:R_A + 1, :], p.SR[R_V:R_V + 1, :]))
                p.sV.inc(v.reciprocal(p.SR[R_B:R_B + 1, :], p.SR[R_M:R_M + 1, :]))
            normalize_pair(NPAIR - 1)
            p.release_slot_sems(2)
            if not l0:
                p.wall(a, p.sV)
                for hp in range(NPAIR):
                    p.sA.inc(a.activation(A_sb[:, hp * TOK:(hp + 1) * TOK],
                                          A_sb[:, hp * TOK:(hp + 1) * TOK],
                                          AF.Gelu, bias=p.c_zero[:, 0:1], scale=1.0))

        out_dram = p.dram["x_dram"] if l0 else p.dram["outT"]
        xin_dram = p.dram["xT"] if l0 else p.dram["x_dram"]
        with ExitStack() as ep:
            WO = ep.enter_context(nc.sbuf_tensor(p.un("WOs"), [128, 2 * D], BF16))
            XB = ep.enter_context(nc.sbuf_tensor(p.un("epxb"), [128, 2 * TOK], F32))
            T2 = ep.enter_context(nc.sbuf_tensor(p.un("ept2"), [128, TOK], F32))
            psW = ep.enter_context(nc.psum_tensor(p.un("psW"), [128, TOK], F32))
            wo_rel = [None, None]
            wo_load = [None, None]
            wosem = p.take_slot_sems(2)

            def load_wo(oc):
                slot = oc % 2
                if wo_rel[slot] is not None:
                    wo_rel[slot].wait(g)
                wosem[slot].inc(g.dma_start(out=WO[:, slot * D:(slot + 1) * D],
                                            in_=wo[:, oc * D:(oc + 1) * D]), 16)
                wo_load[slot] = Rel(wosem[slot], wosem[slot].n)

            dma_fence(p)
            load_wo(0)
            for oc in range(CH):
                slot = oc % 2
                wo_load[slot].wait(t)
                if oc + 1 < CH:
                    load_wo(oc + 1)
                xsl = XB[:, slot * TOK:(slot + 1) * TOK]
                p.wall(s, p.sV)
                p.dS.inc(s.dma_start(out=xsl, in_=xin_dram[:, oc * TOK:(oc + 1) * TOK]), 16)
                p.wall(t, p.sV, p.sA)
                for nb in range(4):
                    for ic in range(CH):
                        p.sT.inc(t.matmul(
                            psW[:, nb * 512:(nb + 1) * 512],
                            WO[:, slot * D + ic * 128: slot * D + (ic + 1) * 128],
                            A_sb[:, ic * TOK + nb * 512: ic * TOK + (nb + 1) * 512],
                            start=(ic == 0), stop=(ic == CH - 1),
                            skip_group_check=True))
                wo_rel[slot] = Rel(p.sT, p.sT.n)
                p.wall(v, p.sT, p.dS)
                if l0:
                    p.sV.inc(v.scalar_tensor_tensor(p.TMP[:], psW[:], p.b0o[:, oc:oc + 1],
                                                    p.G[:, oc * TOK:(oc + 1) * TOK],
                                                    ALU.add, ALU.mult))
                    p.sV.inc(v.tensor_add(T2[:], xsl, p.TMP[:]))
                else:
                    p.sV.inc(v.tensor_tensor(p.TMP[:], psW[:],
                                             p.XN[:, oc * SEQ: oc * SEQ + TOK], ALU.add))
                    p.wall(a, p.sV)
                    p.sA.inc(a.activation(T2[:], p.G[:, oc * TOK:(oc + 1) * TOK],
                                          AF.Silu, bias=p.c_zero[:, 0:1], scale=1.0))
                    p.wall(v, p.sA)
                    p.sV.inc(v.tensor_mul(p.TMP[:], p.TMP[:], T2[:]))
                    p.sV.inc(v.tensor_add(T2[:], xsl, p.TMP[:]))
                p.wall(s, p.sV)
                p.dO.inc(s.dma_start(out=out_dram[:, oc * TOK:(oc + 1) * TOK], in_=T2[:]), 16)
                p.wall(v, p.dO)
            p.release_slot_sems(2)



def gating_moe(p):
    nc = p.nc
    v, s, g, a, t = nc.vector, nc.sync, nc.gpsimd, nc.scalar, nc.tensor
    d = p.dram

    with ExitStack() as ph:
        GT = ph.enter_context(nc.sbuf_tensor(p.un("GT"), [128, CH * NE], BF16))
        LG = ph.enter_context(nc.sbuf_tensor(p.un("LG"), [8, TOK], F32))
        LGt = ph.enter_context(nc.sbuf_tensor(p.un("LGt"), [128, 128], F32))
        MX = ph.enter_context(nc.sbuf_tensor(p.un("MX"), [128, 16], F32))
        RW = ph.enter_context(nc.sbuf_tensor(p.un("RW"), [128, 16], F32))
        TS = ph.enter_context(nc.sbuf_tensor(p.un("TS"), [128, 128], F32))
        dma_fence(p)
        p.dG.inc(g.dma_start(out=GT[:], in_=d["gateT"][:]), 16)
        gt_rel = Rel(p.dG, p.dG.n)

        with nc.psum_tensor(p.un("psG"), [32, TOK], F32) as psG:
            gt_rel.wait(t)
            p.wall(t, p.sV)
            for nb in range(4):
                for ic in range(CH):
                    p.sT.inc(t.matmul(psG[0:8, nb * 512:(nb + 1) * 512],
                                      GT[:, ic * 8:(ic + 1) * 8],
                                      p.XN[:, ic * SEQ + nb * 512: ic * SEQ + (nb + 1) * 512],
                                      start=(ic == 0), stop=(ic == CH - 1),
                                      skip_group_check=True))
            p.wall(v, p.sT)
            p.sV.inc(v.tensor_copy(LG[:], psG[0:8, :]))
        with nc.psum_tensor(p.un("psLt"), [128, 128], F32) as psLt:
            p.wall(t, p.sV)
            for tc in range(16):
                p.sT.inc(t.transpose(psLt[:, tc * 8:(tc + 1) * 8],
                                     LG[0:8, tc * 128:(tc + 1) * 128],
                                     p.eyef[0:8, 0:8]))
            p.wall(v, p.sT)
            p.sV.inc(v.tensor_copy(LGt[:], psLt[:]))
        for tc in range(16):
            p.sV.inc(v.reduce_max(MX[:, tc:tc + 1], LGt[:, tc * 8:(tc + 1) * 8], axis=AX.X))
        v.drain()
        p.wall(a, p.sV)
        p.sA.inc(a.activation(RW[:], MX[:], AF.Sigmoid, bias=p.c_zero[:, 0:1], scale=1.0))
        p.wall(v, p.sA)
        v.drain()
        for tc in range(16):
            p.sV.inc(v.tensor_scalar(TS[:, tc * 8:(tc + 1) * 8], LGt[:, tc * 8:(tc + 1) * 8],
                                     MX[:, tc:tc + 1], RW[:, tc:tc + 1],
                                     ALU.is_equal, ALU.mult))

        p.wall(v, p.sA)
        v.memset(p.G[:], 0.0)
        p.sV.inc(v.engine_nop())
        XS = ph.enter_context(nc.sbuf_tensor(p.un("XS"), [128, CH * TOK], BF16))
        PRD = ph.enter_context(nc.sbuf_tensor(p.un("PRD"), [128, TOK], BF16))
        WG = ph.enter_context(nc.sbuf_tensor(p.un("WGs"), [128, 2 * 4096], BF16))
        WU = ph.enter_context(nc.sbuf_tensor(p.un("WUs"), [128, 2 * 4096], BF16))
        WD = ph.enter_context(nc.sbuf_tensor(p.un("WDs"), [128, 2 * 4096], BF16))
        we_rel = [None, None]
        we_load = [None, None]
        wesem = p.take_slot_sems(2)

        def load_we(e):
            slot = e % 2
            if we_rel[slot] is not None:
                we_rel[slot].wait(g)
            for wdram, wbuf in ((d["wgT"], WG), (d["wuT"], WU), (d["wdT"], WD)):
                wesem[slot].inc(g.dma_start(out=wbuf[:, slot * 4096:(slot + 1) * 4096],
                                            in_=wdram[e]), 16)
            we_load[slot] = Rel(wesem[slot], wesem[slot].n)

        dma_fence(p)
        load_we(0)
        for e in range(NE):
            slot = e % 2
            we_load[slot].wait(t)
            if e + 1 < NE:
                load_we(e + 1)
            with nc.psum_tensor(p.un("pRow"), [1, TOK], F32) as pRow:
                p.wall(t, p.sV)
                for tc in range(16):
                    p.sT.inc(t.transpose(pRow[0:1, tc * 128:(tc + 1) * 128],
                                         TS[:, tc * 8 + e: tc * 8 + e + 1], p.eyef[:]))
                p.wall(v, p.sT)
                p.sV.inc(v.tensor_copy(p.SR[R_A:R_A + 1, :], pRow[0:1, :]))
            with nc.psum_tensor(p.un("pB2"), [128, TOK], F32) as pB2:
                p.wall(t, p.sV)
                bcast_row(p, pB2, p.SR[R_A:R_A + 1, :], TOK)
                p.wall(v, p.sT)
                for c in range(CH):
                    p.sV.inc(v.tensor_tensor(XS[:, c * TOK:(c + 1) * TOK],
                                             p.XN[:, c * SEQ: c * SEQ + TOK],
                                             pB2[:], ALU.mult))
            for tt in range(4):
                t0 = tt * 512
                with nc.psum_tensor(p.un("pGm"), [128, 2048], F32) as pG, \
                     nc.psum_tensor(p.un("pUm"), [128, 2048], F32) as pU:
                    p.wall(t, p.sV, p.sA)
                    for oc in range(4):
                        for ic in range(CH):
                            wof = slot * 4096 + ic * 512 + oc * 128
                            p.sT.inc(t.matmul(pG[:, oc * 512:(oc + 1) * 512],
                                              WG[:, wof:wof + 128],
                                              XS[:, ic * TOK + t0: ic * TOK + t0 + 512],
                                              start=(ic == 0), stop=(ic == CH - 1),
                                              skip_group_check=True))
                            p.sT.inc(t.matmul(pU[:, oc * 512:(oc + 1) * 512],
                                              WU[:, wof:wof + 128],
                                              XS[:, ic * TOK + t0: ic * TOK + t0 + 512],
                                              start=(ic == 0), stop=(ic == CH - 1),
                                              skip_group_check=True))
                    p.wall(a, p.sT)
                    p.sA.inc(a.activation(p.TMP[:], pG[:], AF.Silu,
                                          bias=p.c_zero[:, 0:1], scale=1.0))
                    p.wall(v, p.sA, p.sT)
                    p.sV.inc(v.scalar_tensor_tensor(PRD[:], pU[:], 1.0, p.TMP[:],
                                                    ALU.bypass, ALU.mult))
                with nc.psum_tensor(p.un("pDm"), [128, 4096], F32) as pD:
                    p.wall(t, p.sV)
                    for oc in range(CH):
                        for ic in range(4):
                            wof = slot * 4096 + ic * D + oc * 128
                            p.sT.inc(t.matmul(pD[:, oc * 512:(oc + 1) * 512],
                                              WD[:, wof:wof + 128],
                                              PRD[:, ic * 512:(ic + 1) * 512],
                                              start=(ic == 0), stop=(ic == 3),
                                              skip_group_check=True))
                    p.wall(v, p.sT)
                    gview = p.G[:].rearrange("q (c tk) -> q c tk", c=CH)[:, :, t0:t0 + 512]
                    dview = pD[:].rearrange("q (c tk) -> q c tk", c=CH)
                    p.sV.inc(v.tensor_tensor(gview, gview, dview, ALU.add))
            we_rel[slot] = Rel(p.sT, p.sT.n)


class StopEmission(Exception):
    pass


def _ck(p, name):
    if STOP_AFTER and name == STOP_AFTER:
        raise StopEmission()


def layer(p, lid):
    ln_phase(p, lid, inner=False)
    tap(p, f"xn{lid}", p.XN[:, 0:TOK], (p.sV,))
    _ck(p, f"ln{lid}")
    exchange_export(p, lid)
    _ck(p, f"exp{lid}")
    if lid == 0:
        ffn0(p)
    else:
        gating_moe(p)
    tap(p, f"g{lid}", p.G[:, 0:TOK], (p.sV, p.sA))
    _ck(p, f"ffn{lid}")
    exchange_import(p, lid)
    _ck(p, f"imp{lid}")
    if lid == 0:
        ln_phase(p, lid, inner=True)
        tap(p, "xni0", p.XN[:, 0:TOK], (p.sV,))
        _ck(p, "lni0")
    attention(p, lid)
    if lid == 0:
        tap(p, "x1", p.dram["x_dram"][:, 0:TOK], (p.dO,))
    _ck(p, f"att{lid}")


def finale(p):
    nc = p.nc
    p.wall(nc.sync, p.dO)
    p.wall(nc.sync, p.dS)



def _lay_cm(w):
    r, n = w.shape
    ch = r // 128
    return np.ascontiguousarray(w.reshape(ch, 128, n).transpose(1, 0, 2).reshape(128, ch * n))


def _lay_pm(w_io):
    b = w_io.reshape(8, 128, 8, 128)
    return np.ascontiguousarray(b.transpose(1, 2, 0, 3).reshape(128, 8192))


def _lay_col(vec):
    return np.ascontiguousarray(vec.reshape(8, 128).T)


def _build_rails(rm, rup):
    ytiles = np.zeros((32, 128, 128), np.float32)
    for h in range(4):
        for j in range(8):
            tl = ytiles[h * 8 + j]
            for a_ in range(8):
                r = 8 * j + a_
                tl[16 * a_:16 * a_ + 16, 16 * a_:16 * a_ + 16] = rm[h, r]
    ztiles = np.zeros((128, 128, 128), np.float32)
    for ci in range(128):
        h = ci // 32
        rr = ci % 32
        tl = ztiles[ci]
        for l_ in range(2):
            r = 2 * rr + l_
            tl[16 * (r % 8):16 * (r % 8) + 16, 64 * l_:64 * l_ + 64] = rup[h, r].T
    layY = np.ascontiguousarray(ytiles.transpose(1, 0, 2).reshape(128, 32 * 128))
    layZ = np.ascontiguousarray(ztiles.transpose(1, 0, 2).reshape(128, 128 * 128))
    return layY, layZ


def kernel(**inputs):
    global LAST_RESULTS
    f = lambda k: np.asarray(inputs[k], dtype=np.float32)

    x = f("L0_attn_sequence")
    shared = {
        "w0qT": _lay_pm(np.ascontiguousarray(f("L0_attn_Wq").T)),
        "w0kT": _lay_pm(np.ascontiguousarray(f("L0_attn_Wk").T)),
        "w0vT": _lay_pm(np.ascontiguousarray(f("L0_attn_Wv").T)),
        "w0oT": _lay_pm(np.ascontiguousarray(f("L0_attn_Wo").T)),
        "b0q": _lay_col(f("L0_attn_bq")),
        "b0k": _lay_col(f("L0_attn_bk")),
        "b0v": _lay_col(f("L0_attn_bv")),
        "b0o": _lay_col(f("L0_attn_bo")),
        "gam": _lay_col(f("L0_attn_ln_gamma")),
        "bet": _lay_col(f("L0_attn_ln_beta")),
        "mu": f("L0_attn_mu").reshape(1, 1),
        "cprojT": _lay_cm(np.ascontiguousarray(f("L0_ffn_w_cproj").T)),
        "w1qT": _lay_pm(f("L1_attn_Wq")),
        "w1kT": _lay_pm(f("L1_attn_Wk")),
        "w1vT": _lay_pm(f("L1_attn_Wv")),
        "w1postT": _lay_pm(f("L1_attn_Wpost")),
        "gateT": _lay_cm(np.ascontiguousarray(f("L1_ffn_gate_w").T)),
        "wgT": np.stack([_lay_cm(np.ascontiguousarray(f("L1_ffn_w_gate")[e].T))
                         for e in range(NE)]),
        "wuT": np.stack([_lay_cm(np.ascontiguousarray(f("L1_ffn_w_up")[e].T))
                         for e in range(NE)]),
        "wdT": np.stack([_lay_cm(np.ascontiguousarray(f("L1_ffn_w_down")[e].T))
                         for e in range(NE)]),
        "eye": np.eye(128, dtype=np.float32),
    }
    shared["railsY"], shared["railsZ"] = _build_rails(
        f("L0_ffn_rail_maker"), f("L0_ffn_rail_up_proj"))

    nc = build_program(taps=DEBUG_TAPS, emit_upto=EMIT_UPTO)

    in_maps = []
    for c in range(8):
        b, par = c // 2, c % 2
        xs = np.ascontiguousarray(x[b, par * TOK:(par + 1) * TOK, :].T)
        m = dict(shared)
        m["xT"] = _lay_cm(xs)
        m["mlo"] = np.full((128, 1), float(par), np.float32)
        m["mhi"] = np.full((128, 1), float(1 - par), np.float32)
        in_maps.append(m)

    res = run_bass_kernel_spmd(nc, in_maps, core_ids=list(range(8)))
    LAST_RESULTS = res.results

    out = np.zeros((4, SEQ, D), np.float32)
    for c in range(8):
        b, par = c // 2, c % 2
        o = res.results[c]["outT"]
        o = o.reshape(128, CH, TOK).transpose(1, 0, 2).reshape(D, TOK).T
        out[b, par * TOK:(par + 1) * TOK, :] = o
    return out


# version 09
# speedup vs baseline: 1.6101x; 1.0070x over previous
import numpy as np
from contextlib import ExitStack

import concourse.bass as bass
import concourse.mybir as mybir
from concourse.bass_utils import run_bass_kernel_spmd

F32 = mybir.dt.float32
BF16 = mybir.dt.bfloat16
AF = mybir.ActivationFunctionType
ALU = mybir.AluOpType
AX = mybir.AxisListType

D = 1024
CH = 8
TOK = 2048
SEQ = 4096
NH = 16
NPAIR = 8
NE = 8
FCH = 128

LOG2 = float(np.log(2.0))

DEBUG_TAPS = {}
EMIT_UPTO = 2
STOP_AFTER = ""
PARANOID = 0
NO_BOUNCE = False
LAST_RESULTS = None

R_A = 0
R_B = 32
R_V = 64
R_M = 96


class Sem:
    def __init__(self, ctx, nc, name):
        self.h = ctx.enter_context(nc.semaphore(name))
        self.n = 0

    def inc(self, instr, k=1):
        instr.then_inc(self.h, k)
        self.n += k
        return instr

    def wait(self, eng, n=None):
        eng.wait_ge(self.h, self.n if n is None else n)


class Rel:
    def __init__(self, sem, n):
        self.sem = sem
        self.n = n

    def wait(self, eng):
        eng.wait_ge(self.sem.h, self.n)


class P:
    def take_slot_sems(self, n):
        out = self.slot_sems[self._slot_next:self._slot_next + n]
        assert len(out) == n, "out of slot sems"
        self._slot_next += n
        return out

    def release_slot_sems(self, n):
        self._slot_next -= n

    def un(self, base):
        self._uc = getattr(self, "_uc", 0) + 1
        return f"{base}_{self._uc}"


def build_program(taps=None, emit_upto=2):
    nc = bass.Bass("TRN2", target_bir_lowering=False, debug=False)
    st = ExitStack()
    p = P()
    p.nc = nc
    p.st = st
    p.taps = dict(taps or {})

    dram = {}

    def din(name, shape, dtype=F32):
        dram[name] = nc.dram_tensor(name, shape, dtype, kind="ExternalInput").ap()

    def dint(name, shape, dtype=F32):
        dram[name] = nc.dram_tensor(name, shape, dtype, kind="Internal").ap()

    def dout(name, shape, dtype=F32):
        dram[name] = nc.dram_tensor(name, shape, dtype, kind="ExternalOutput").ap()

    din("xT", [128, CH * TOK])
    for nm in ("w0qT", "w0kT", "w0vT", "w0oT", "w1qT", "w1kT", "w1vT", "w1postT"):
        din(nm, [128, NPAIR * D])
    for nm in ("b0q", "b0k", "b0v", "b0o", "gam", "bet"):
        din(nm, [128, 8])
    din("mu", [1, 1])
    din("railsY", [128, 32 * 128])
    din("railsZ", [128, 128 * 128])
    din("cprojT", [128, FCH * D])
    din("gateT", [128, CH * NE])
    din("wgT", [NE, 128, CH * 512])
    din("wuT", [NE, 128, CH * 512])
    din("wdT", [NE, 128, 4 * D])
    din("mlo", [128, 1])
    din("mhi", [128, 1])
    din("eye", [128, 128])
    dint("x_dram", [128, CH * TOK])
    dint("cc_in0", [128, CH * TOK], BF16)
    dint("cc_out0", [256, CH * TOK], BF16)
    dint("cc_in1", [128, CH * TOK], BF16)
    dint("cc_out1", [256, CH * TOK], BF16)
    dint("cprojT_bf", [128, FCH * D], BF16)
    dout("outT", [128, CH * TOK])
    p.dram = dram

    p.tap_out = {}
    for tname, shape in p.taps.items():
        dout("tap_" + tname, list(shape), F32)
        p.tap_out[tname] = dram["tap_" + tname]

    p.dS = Sem(st, nc, "dS")
    p.dG = Sem(st, nc, "dG")
    p.sT = Sem(st, nc, "sT")
    p.sA = Sem(st, nc, "sA")
    p.sV = Sem(st, nc, "sV")
    p.sC = Sem(st, nc, "sC")
    p.dO = Sem(st, nc, "dO")
    p.slot_sems = [Sem(st, nc, f"sl{i}") for i in range(12)]
    p._slot_next = 0

    def wall(eng, *sems):
        for s_ in sems:
            s_.wait(eng)
    p.wall = wall

    sb = st.enter_context
    p.XN = sb(nc.sbuf_tensor("XN", [128, CH * SEQ], BF16))
    p.G = sb(nc.sbuf_tensor("G", [128, CH * TOK], BF16))
    p.TMP = sb(nc.sbuf_tensor("TMP", [128, TOK], F32))
    p.SR = sb(nc.sbuf_tensor("SR", [128, TOK], F32))
    p.c_ones_b = sb(nc.sbuf_tensor("c_ones_b", [128, 1], BF16))
    p.c_onesr_f = sb(nc.sbuf_tensor("c_onesr_f", [128, 128], F32))
    p.c_ln2 = sb(nc.sbuf_tensor("c_ln2", [128, 1], F32))
    p.c_mu2 = sb(nc.sbuf_tensor("c_mu2", [128, 1], F32))
    p.c_zero = sb(nc.sbuf_tensor("c_zero", [128, 1], F32))
    p.c_eps1 = sb(nc.sbuf_tensor("c_eps1", [128, 1], F32))
    p.c_eps2 = sb(nc.sbuf_tensor("c_eps2", [128, 1], F32))
    p.b0q = sb(nc.sbuf_tensor("b0q_s", [128, 8], F32))
    p.b0k = sb(nc.sbuf_tensor("b0k_s", [128, 8], F32))
    p.b0v = sb(nc.sbuf_tensor("b0v_s", [128, 8], F32))
    p.b0o = sb(nc.sbuf_tensor("b0o_s", [128, 8], F32))
    p.gam = sb(nc.sbuf_tensor("gam_s", [128, 8], F32))
    p.bet = sb(nc.sbuf_tensor("bet_s", [128, 8], F32))
    p.mlo = sb(nc.sbuf_tensor("mlo_s", [128, 1], F32))
    p.mhi = sb(nc.sbuf_tensor("mhi_s", [128, 1], F32))
    p.eyeb = sb(nc.sbuf_tensor("eyeb", [128, 128], BF16))
    p.eyef = sb(nc.sbuf_tensor("eyef", [128, 128], F32))

    prologue(p)
    try:
        for lid in range(min(2, emit_upto)):
            layer(p, lid)
    except StopEmission:
        pass
    finale(p)

    st.close()
    return nc


def tap(p, name, src_ap, sems):
    if name not in p.tap_out:
        return
    nc = p.nc
    eng = nc.gpsimd if src_ap.dtype != F32 else nc.sync
    p.wall(eng, *sems)
    p.dO.inc(eng.dma_start(out=p.tap_out[name][:], in_=src_ap), 16)


def prologue(p):
    nc = p.nc
    v, s, g, a, t = nc.vector, nc.sync, nc.gpsimd, nc.scalar, nc.tensor
    d = p.dram
    v.memset(p.c_ones_b[:], 1.0)
    v.memset(p.c_onesr_f[:], 1.0)
    v.memset(p.c_ln2[:], -LOG2)
    v.memset(p.c_zero[:], 0.0)
    v.memset(p.c_eps1[:], 1e-5)
    v.memset(p.c_eps2[:], 1e-6)
    p.sV.inc(v.engine_nop())
    for nm, t_sb in (("b0q", p.b0q), ("b0k", p.b0k), ("b0v", p.b0v), ("b0o", p.b0o),
                     ("gam", p.gam), ("bet", p.bet), ("mlo", p.mlo), ("mhi", p.mhi),
                     ("eye", p.eyef)):
        p.dS.inc(s.dma_start(out=t_sb[:], in_=d[nm][:]), 16)
    p.dG.inc(g.dma_start(out=p.eyeb[:], in_=d["eye"][:]), 16)
    mut = p.st.enter_context(nc.sbuf_tensor("mut", [1, 1], F32))
    with nc.psum_tensor("pmu", [128, 1], F32) as pmu:
        p.dS.inc(s.dma_start(out=mut[:], in_=d["mu"][:]), 16)
        p.wall(t, p.dS, p.sV)
        p.sT.inc(t.matmul(pmu[:], p.c_onesr_f[0:1, :], mut[:], start=True, stop=True))
        p.wall(a, p.sT, p.sV)
        p.sA.inc(a.activation(p.c_mu2[:], pmu[:], AF.Copy, bias=0.0, scale=-0.5))
        a.drain()


def dma_fence(p):
    nc = p.nc
    p.wall(nc.sync, p.sT, p.sA, p.sV)
    p.wall(nc.gpsimd, p.sT, p.sA, p.sV)


def bcast_row(p, psdst, row_ap, width):
    t = p.nc.tensor
    rb = row_ap.base_partition()
    for nb in range(width // 512):
        p.sT.inc(t.matmul(psdst[:, nb * 512:(nb + 1) * 512],
                          p.c_onesr_f[rb:rb + 1, :],
                          row_ap[0:1, nb * 512:(nb + 1) * 512],
                          start=True, stop=True, skip_group_check=True))


def ln_phase(p, layer_id, inner):
    nc = p.nc
    v, s, a, t = nc.vector, nc.sync, nc.scalar, nc.tensor
    nhalf = 2 if inner else 1
    src = p.dram["xT"] if layer_id == 0 else p.dram["x_dram"]

    p.wall(v, p.dS, p.sT)
    p.wall(s, p.dO)
    dma_fence(p)
    with ExitStack() as ph:
        SQ = ph.enter_context(nc.sbuf_tensor(p.un("lnsq"), [128, TOK], BF16))
        XB = None
        if not inner:
            XB = ph.enter_context(nc.sbuf_tensor(p.un("lnxb"), [128, 2 * TOK], F32))
        for half in range(nhalf):
            hs = half * TOK
            pstat_cm = nc.psum_tensor(p.un("pstat"), [64, TOK], F32)
            pstat = pstat_cm.__enter__()
            for c in range(CH):
                if not inner:
                    xsl = XB[:, (c % 2) * TOK:(c % 2 + 1) * TOK]
                    p.wall(s, p.sV)
                    p.dS.inc(s.dma_start(out=xsl, in_=src[:, c * TOK:(c + 1) * TOK]), 16)
                    p.wall(v, p.dS)
                    xb = p.XN[:, c * SEQ + TOK: c * SEQ + 2 * TOK]
                    p.sV.inc(v.tensor_copy(xb, xsl))
                else:
                    xb = p.XN[:, c * SEQ + hs: c * SEQ + hs + TOK]
                p.wall(v, p.sT)
                p.sV.inc(v.tensor_mul(SQ[:], xb, xb))
                p.wall(t, p.sV)
                for nb in range(4):
                    n0 = nb * 512
                    p.sT.inc(t.matmul(pstat[0:1, n0:n0 + 512], p.c_ones_b[:],
                                      xb[:, n0:n0 + 512], start=(c == 0),
                                      stop=(c == CH - 1), skip_group_check=True))
                    p.sT.inc(t.matmul(pstat[32:33, n0:n0 + 512], p.c_ones_b[:],
                                      SQ[:, n0:n0 + 512], start=(c == 0),
                                      stop=(c == CH - 1), skip_group_check=True))
            p.wall(v, p.sT)
            mrow = p.SR[R_M:R_M + 1, :]
            vrow = p.SR[R_V:R_V + 1, :]
            arow = p.SR[R_A:R_A + 1, :]
            brow = p.SR[R_B:R_B + 1, :]
            p.sV.inc(v.tensor_scalar_mul(mrow, pstat[0:1, :], 1.0 / D))
            p.sV.inc(v.tensor_mul(vrow, mrow, mrow))
            p.sV.inc(v.scalar_tensor_tensor(vrow, pstat[32:33, :], 1.0 / D, vrow,
                                            ALU.mult, ALU.subtract))
            epsc = p.c_eps2 if inner else p.c_eps1
            p.wall(a, p.sV)
            p.sA.inc(a.activation(brow, vrow, AF.Ln, bias=epsc[0:1, 0:1], scale=1.0))
            a.drain()
            p.sA.inc(a.activation(arow, brow, AF.Exp, bias=p.c_zero[0:1, 0:1], scale=-0.5))
            p.wall(v, p.sA)
            p.sV.inc(v.scalar_tensor_tensor(brow, pstat[0:1, :], -1.0 / D, arow,
                                            ALU.mult, ALU.mult))
            pstat_cm.__exit__(None, None, None)
            pbca_cm = nc.psum_tensor(p.un("pbca"), [128, TOK], F32)
            pbcb_cm = nc.psum_tensor(p.un("pbcb"), [128, TOK], F32)
            pbca = pbca_cm.__enter__()
            pbcb = pbcb_cm.__enter__()
            p.wall(t, p.sV)
            bcast_row(p, pbca, arow, TOK)
            bcast_row(p, pbcb, brow, TOK)
            for c in range(CH):
                if inner:
                    xb = p.XN[:, c * SEQ + hs: c * SEQ + hs + TOK]
                    xout = xb
                else:
                    xb = p.XN[:, c * SEQ + TOK: c * SEQ + 2 * TOK]
                    xout = p.XN[:, c * SEQ: c * SEQ + TOK]
                p.wall(v, p.sT)
                p.sV.inc(v.tensor_tensor(p.TMP[:], xb, pbca[:], ALU.mult))
                if inner:
                    p.sV.inc(v.tensor_tensor(p.TMP[:], p.TMP[:], pbcb[:], ALU.add))
                    p.sV.inc(v.tensor_scalar(xout, p.TMP[:], p.gam[:, c:c + 1],
                                             p.bet[:, c:c + 1], ALU.mult, ALU.add))
                else:
                    p.sV.inc(v.tensor_tensor(xout, p.TMP[:], pbcb[:], ALU.add))
            p.wall(t, p.sV)
            pbcb_cm.__exit__(None, None, None)
            pbca_cm.__exit__(None, None, None)


def exchange_export(p, layer_id):
    nc = p.nc
    s, g = nc.sync, nc.gpsimd
    cc_in = p.dram[f"cc_in{layer_id}"]
    cc_out = p.dram[f"cc_out{layer_id}"]
    p.wall(s, p.sV)
    for c in range(CH):
        p.dS.inc(s.dma_start(out=cc_in[:, c * TOK:(c + 1) * TOK],
                             in_=p.XN[:, c * SEQ: c * SEQ + TOK]), 16)
    exp_done = Rel(p.dS, p.dS.n)

    def trigger():
        exp_done.wait(g)
        p.sC.inc(g.collective_compute(
            "AllGather", ALU.bypass, ins=[cc_in[:]], outs=[cc_out[:]],
            replica_groups=[[0, 1], [2, 3], [4, 5], [6, 7]]))
    return trigger


def exchange_import(p, layer_id):
    nc = p.nc
    s, v = nc.sync, nc.vector
    cc_out = p.dram[f"cc_out{layer_id}"]
    p.wall(s, p.sC)
    dma_fence(p)
    with nc.sbuf_tensor(p.un("imA"), [128, TOK], BF16) as imA, \
         nc.sbuf_tensor(p.un("imB"), [128, TOK], BF16) as imB:
        for c in range(CH):
            p.wall(s, p.sV)
            p.dS.inc(s.dma_start(out=imA[:], in_=cc_out[0:128, c * TOK:(c + 1) * TOK]), 16)
            p.dS.inc(s.dma_start(out=imB[:], in_=cc_out[128:256, c * TOK:(c + 1) * TOK]), 16)
            p.wall(v, p.dS, p.sT)
            xo = p.XN[:, c * SEQ + TOK: c * SEQ + 2 * TOK]
            p.sV.inc(v.tensor_scalar(p.TMP[:], imA[:], p.mlo[:, 0:1], None, ALU.mult))
            p.sV.inc(v.scalar_tensor_tensor(xo, imB[:], p.mhi[:, 0:1], p.TMP[:],
                                            ALU.mult, ALU.add))



def ffn0(p, cc_trigger=None):
    nc = p.nc
    v, s, g, a, t = nc.vector, nc.sync, nc.gpsimd, nc.scalar, nc.tensor
    d = p.dram
    NTG = 8
    TW = 256

    with ExitStack() as ph:
        RTY = ph.enter_context(nc.sbuf_tensor(p.un("RTY"), [128, 32 * 128], BF16))
        RTZ = ph.enter_context(nc.sbuf_tensor(p.un("RTZ"), [128, 128 * 128], BF16))
        YB = ph.enter_context(nc.sbuf_tensor(p.un("YB"), [128, 32 * TW], BF16))
        H4 = ph.enter_context(nc.sbuf_tensor(p.un("H4"), [128, 2048], BF16))
        WA = ph.enter_context(nc.sbuf_tensor(p.un("WA"), [128, 4 * D], BF16))
        def GTOK(tc):
            return p.XN[:, (tc // 2) * SEQ + TOK + (tc % 2) * D:
                        (tc // 2) * SEQ + TOK + (tc % 2 + 1) * D]

        dma_fence(p)
        p.dG.inc(g.dma_start(out=RTY[:], in_=d["railsY"][:]), 16)
        for q in range(4):
            p.dG.inc(g.dma_start(out=RTZ[:, q * 4096:(q + 1) * 4096],
                                 in_=d["railsZ"][:, q * 4096:(q + 1) * 4096]), 16)
        rails_rel = Rel(p.dG, p.dG.n)

        wa_rel = [None, None]
        wa_bounce = [None, None]
        wa_load = [None, None]
        ldsem = p.take_slot_sems(2)
        bnsem = p.take_slot_sems(2)

        def load_w2(j, tg):
            slot = j % 2
            dst = WA[:, slot * 2 * D:(slot + 1) * 2 * D]
            if NO_BOUNCE or tg == 0:
                if wa_rel[slot] is not None:
                    wa_rel[slot].wait(g)
                if wa_bounce[slot] is not None:
                    wa_bounce[slot].wait(g)
                ldsem[slot].inc(g.dma_start(out=dst,
                                            in_=d["cprojT"][:, 2 * j * D:(2 * j + 2) * D]), 16)
                wa_load[slot] = Rel(ldsem[slot], ldsem[slot].n)
                if not NO_BOUNCE:
                    wa_load[slot].wait(s)
                    bnsem[slot].inc(s.dma_start(
                        out=d["cprojT_bf"][:, 2 * j * D:(2 * j + 2) * D], in_=dst), 16)
                    wa_bounce[slot] = Rel(bnsem[slot], bnsem[slot].n)
            else:
                eng = s if (j % 2 == 0) else g
                if wa_rel[slot] is not None:
                    wa_rel[slot].wait(eng)
                ldsem[slot].inc(eng.dma_start(
                    out=dst, in_=d["cprojT_bf"][:, 2 * j * D:(2 * j + 2) * D]), 16)
                wa_load[slot] = Rel(ldsem[slot], ldsem[slot].n)

        with nc.psum_tensor(p.un("pz"), [128, 2048], F32) as pz, \
             nc.psum_tensor(p.un("pot"), [128, 2048], F32) as pot:
            py = pz
            gelu_rel = [None, None]
            for tg in range(NTG):
                t0 = tg * TW
                if tg == 0:
                    rails_rel.wait(t)
                    p.wall(t, p.sV)
                if tg == 1:
                    if cc_trigger is not None:
                        cc_trigger()
                    p.wall(s, bnsem[0], bnsem[1])
                    p.wall(g, bnsem[0], bnsem[1])
                load_w2(0, tg)
                load_w2(1, tg)
                p.wall(t, p.sA)
                yev = [None, None]
                for yt in range(32):
                    j = yt % 8
                    half = yt % 2
                    if yev[half] is not None:
                        yev[half].wait(t)
                    elif yt < 2:
                        p.wall(t, p.sV)
                    p.sT.inc(t.matmul(py[:, half * 512: half * 512 + TW],
                                      RTY[:, yt * 128:(yt + 1) * 128],
                                      p.XN[:, j * SEQ + t0: j * SEQ + t0 + TW],
                                      start=True, stop=True, skip_group_check=True))
                    v.wait_ge(p.sT.h, p.sT.n)
                    p.sV.inc(v.tensor_copy(YB[:, yt * TW:(yt + 1) * TW],
                                           py[:, half * 512: half * 512 + TW]))
                    yev[half] = Rel(p.sV, p.sV.n)
                for ib in range(32):
                    zh = ib % 2
                    for k in range(4):
                        i = 4 * ib + k
                        if k == 0:
                            if gelu_rel[zh] is not None:
                                gelu_rel[zh].wait(t)
                            else:
                                p.wall(t, p.sA)
                            if ib == 0:
                                p.wall(t, p.sV)
                        p.sT.inc(t.matmul(pz[:, zh * 1024 + k * TW: zh * 1024 + (k + 1) * TW],
                                          RTZ[:, i * 128:(i + 1) * 128],
                                          YB[:, (i // 4) * TW:(i // 4 + 1) * TW],
                                          start=True, stop=True, skip_group_check=True))
                    zmm_done = p.sT.n
                    a.wait_ge(p.sT.h, zmm_done)
                    if ib > 1:
                        p.wall(a, p.sT)
                    p.sA.inc(a.activation(H4[:, zh * 1024:(zh + 1) * 1024],
                                          pz[:, zh * 1024:(zh + 1) * 1024], AF.Gelu,
                                          bias=p.c_zero[:, 0:1], scale=1.0))
                    gelu_rel[zh] = Rel(p.sA, p.sA.n)
                    gelu_cnt = p.sA.n
                    for k in range(4):
                        i = 4 * ib + k
                        j = i // 2
                        slot = j % 2
                        woff = slot * 2 * D + (i % 2) * D
                        wa_load[slot].wait(t)
                        t.wait_ge(p.sA.h, gelu_cnt)
                        for tc in range(2):
                            hsl = H4[:, zh * 1024 + k * TW + tc * 128:
                                     zh * 1024 + k * TW + (tc + 1) * 128]
                            for ob in range(2):
                                mm = t.matmul(
                                    pot[:, tc * 1024 + ob * 512: tc * 1024 + (ob + 1) * 512],
                                    hsl,
                                    WA[:, woff + ob * 512: woff + (ob + 1) * 512],
                                    start=(i == 0), stop=(i == 127),
                                    skip_group_check=True)
                                if tc == 1 and ob == 1:
                                    p.sT.inc(mm)
                        if i % 2 == 1:
                            wa_rel[slot] = Rel(p.sT, p.sT.n)
                            if j + 2 < 64:
                                load_w2(j + 2, tg)
                p.wall(v, p.sT)
                for tc in range(2):
                    p.sV.inc(v.tensor_copy(GTOK(2 * tg + tc),
                                           pot[:, tc * 1024:(tc + 1) * 1024]))
                p.wall(t, p.sV)

        p.release_slot_sems(4)
        with nc.psum_tensor(p.un("pgt"), [128, 2048], BF16) as pgt:
            for fc in range(CH):
                p.wall(t, p.sA, p.sV)
                for tc in range(16):
                    p.sT.inc(t.transpose(pgt[:, tc * 128:(tc + 1) * 128],
                                         GTOK(tc)[:, fc * 128:(fc + 1) * 128],
                                         p.eyeb[:]))
                p.wall(a, p.sT)
                p.sA.inc(a.activation(p.G[:, fc * TOK:(fc + 1) * TOK], pgt[:],
                                      AF.Silu, bias=p.c_zero[:, 0:1], scale=1.0))



def attention(p, layer_id):
    nc = p.nc
    v, s, g, a, t = nc.vector, nc.sync, nc.gpsimd, nc.scalar, nc.tensor
    d = p.dram
    l0 = (layer_id == 0)
    wq = d["w0qT"] if l0 else d["w1qT"]
    wk = d["w0kT"] if l0 else d["w1kT"]
    wv = d["w0vT"] if l0 else d["w1vT"]
    wo = d["w0oT"] if l0 else d["w1postT"]

    with ExitStack() as ph:
        A_sb = ph.enter_context(nc.sbuf_tensor(p.un("A_sb"), [128, NPAIR * TOK], BF16))
        with ExitStack() as core:
            K_sb = core.enter_context(nc.sbuf_tensor(p.un("K_sb"), [128, SEQ], BF16))
            VT_sb = core.enter_context(nc.sbuf_tensor(p.un("VT_sb"), [128, SEQ], BF16))
            Vsb = core.enter_context(nc.sbuf_tensor(p.un("Vsb"), [128, 32 * 130], BF16))
            Q_sb = core.enter_context(nc.sbuf_tensor(p.un("Q_sb"), [128, TOK], BF16))
            U = core.enter_context(nc.sbuf_tensor(p.un("U"), [128, 2 * TOK], BF16))
            PB = core.enter_context(nc.sbuf_tensor(p.un("PB"), [128, 4 * 1024], BF16))
            WQ = core.enter_context(nc.sbuf_tensor(p.un("WQs"), [128, 2 * D], BF16))
            WK = core.enter_context(nc.sbuf_tensor(p.un("WKs"), [128, 2 * D], BF16))
            WV = core.enter_context(nc.sbuf_tensor(p.un("WVs"), [128, 2 * D], BF16))
            psSc = core.enter_context(nc.psum_tensor(p.un("psSc"), [128, TOK], F32))
            psB = core.enter_context(nc.psum_tensor(p.un("psB"), [128, TOK], F32))

            wsl_rel = [None, None]
            wsl_load = [None, None]
            wssem = p.take_slot_sems(2)

            def load_pair_w(hp):
                slot = hp % 2
                if wsl_rel[slot] is not None:
                    wsl_rel[slot].wait(g)
                for wdram, wbuf in ((wq, WQ), (wk, WK), (wv, WV)):
                    wssem[slot].inc(g.dma_start(out=wbuf[:, slot * D:(slot + 1) * D],
                                                in_=wdram[:, hp * D:(hp + 1) * D]), 16)
                wsl_load[slot] = Rel(wssem[slot], wssem[slot].n)

            act_read = [None, None]
            u_free = [None, None]
            vt_free = [None, None]
            pb_free = [None, None, None, None]

            def proj(wbuf, slot, out_sb, bias_col, ntok):
                evac_rel = [None, None, None, None]
                for nb in range(ntok // 512):
                    q4 = nb % 4
                    if evac_rel[q4] is not None:
                        evac_rel[q4].wait(t)
                    else:
                        p.wall(t, p.sV)
                    for ic in range(CH):
                        mm = t.matmul(
                            psSc[:, q4 * 512:(q4 + 1) * 512],
                            wbuf[:, slot * D + ic * 128: slot * D + (ic + 1) * 128],
                            p.XN[:, ic * SEQ + nb * 512: ic * SEQ + (nb + 1) * 512],
                            start=(ic == 0), stop=(ic == CH - 1),
                            skip_group_check=True)
                        if ic == CH - 1:
                            p.sT.inc(mm)
                    v.wait_ge(p.sT.h, p.sT.n)
                    if bias_col is not None:
                        p.sV.inc(v.tensor_scalar(out_sb[:, nb * 512:(nb + 1) * 512],
                                                 psSc[:, q4 * 512:(q4 + 1) * 512],
                                                 bias_col, None, ALU.add))
                    else:
                        p.sV.inc(v.tensor_copy(out_sb[:, nb * 512:(nb + 1) * 512],
                                               psSc[:, q4 * 512:(q4 + 1) * 512]))
                    evac_rel[q4] = Rel(p.sV, p.sV.n)

            def normalize_pair(hp_):
                p.wall(t, p.sV, p.sA)
                bcast_row(p, psSc, p.SR[R_A:R_A + 1, :], TOK)
                p.wall(v, p.sT)
                p.sV.inc(v.tensor_tensor(A_sb[0:64, hp_ * TOK:(hp_ + 1) * TOK],
                                         A_sb[0:64, hp_ * TOK:(hp_ + 1) * TOK],
                                         psSc[0:64, :], ALU.mult))
                p.wall(t, p.sV)
                bcast_row(p, psSc, p.SR[R_B:R_B + 1, :], TOK)
                p.wall(v, p.sT)
                p.sV.inc(v.tensor_tensor(A_sb[64:128, hp_ * TOK:(hp_ + 1) * TOK],
                                         A_sb[64:128, hp_ * TOK:(hp_ + 1) * TOK],
                                         psSc[64:128, :], ALU.mult))
                p.wall(t, p.sV)

            dma_fence(p)
            load_pair_w(0)
            for hp in range(NPAIR):
                slot = hp % 2
                wsl_load[slot].wait(t)
                if hp + 1 < NPAIR:
                    load_pair_w(hp + 1)
                p.wall(t, p.sA)
                proj(WK, slot, K_sb, p.b0k[:, hp:hp + 1] if l0 else None, SEQ)
                if hp > 0:
                    normalize_pair(hp - 1)
                proj(WV, slot, VT_sb, p.b0v[:, hp:hp + 1] if l0 else None, SEQ)
                proj(WQ, slot, Q_sb, p.b0q[:, hp:hp + 1] if l0 else None, TOK)
                wsl_rel[slot] = Rel(p.sT, p.sT.n)
                psBb = psB[:].bitcast(BF16)
                p.wall(t, p.sV)
                for tc in range(32):
                    p.sT.inc(t.transpose(psBb[:, tc * 128:(tc + 1) * 128],
                                         VT_sb[:, tc * 128:(tc + 1) * 128], p.eyeb[:]))
                p.wall(v, p.sT)
                v.memset(Vsb[:], 1.0)
                src3 = psBb[:, 0:4096].rearrange("q (tc h c) -> q tc h c",
                                                 tc=32, h=2, c=64)
                dst3 = Vsb[:].rearrange("q (tc h c) -> q tc h c",
                                        tc=32, h=2, c=65)[:, :, :, 0:64]
                p.sV.inc(v.tensor_copy(dst3, src3))
                vsb_ready = p.sV.n
                for hh in range(2):
                    rb = 64 * hh
                    pend = []
                    for kc in range(32):
                        ksl = K_sb[rb:rb + 64, kc * 128:(kc + 1) * 128]
                        for qh in range(2):
                            it = (kc * 2 + qh)
                            sb = it % 2
                            sb3 = it % 4
                            q0 = qh * 1024
                            if act_read[sb] is not None:
                                act_read[sb].wait(t)
                            if kc == 0 and qh == 0 and hh == 0:
                                t.wait_ge(p.sV.h, vsb_ready)
                            for nb in range(2):
                                mm = t.matmul(
                                    psSc[:, sb * 1024 + nb * 512: sb * 1024 + (nb + 1) * 512],
                                    ksl,
                                    Q_sb[rb:rb + 64, q0 + nb * 512: q0 + (nb + 1) * 512],
                                    start=True, stop=True, skip_group_check=True)
                                if nb == 1:
                                    p.sT.inc(mm)
                            sc_done = p.sT.n
                            a.wait_ge(p.sT.h, sc_done)
                            scsl = psSc[:, sb * 1024:(sb + 1) * 1024]
                            if l0:
                                if u_free[sb] is not None:
                                    u_free[sb].wait(a)
                                p.sA.inc(a.activation(U[:, sb * 1024:(sb + 1) * 1024],
                                                      scsl, AF.Exp,
                                                      bias=p.c_ln2[:, 0:1], scale=0.125))
                                if vt_free[sb] is not None:
                                    vt_free[sb].wait(a)
                                p.sA.inc(a.activation(p.TMP[:, sb * 1024:(sb + 1) * 1024],
                                                      scsl, AF.Tanh,
                                                      bias=p.c_mu2[:, 0:1], scale=0.0625))
                                act_read[sb] = Rel(p.sA, p.sA.n)
                                v.wait_ge(p.sA.h, p.sA.n)
                                if pb_free[sb3] is not None:
                                    pb_free[sb3].wait(v)
                                p.sV.inc(v.scalar_tensor_tensor(
                                    PB[:, sb3 * 1024:(sb3 + 1) * 1024],
                                    p.TMP[:, sb * 1024:(sb + 1) * 1024], 1.0,
                                    U[:, sb * 1024:(sb + 1) * 1024],
                                    ALU.add, ALU.mult))
                                u_free[sb] = Rel(p.sV, p.sV.n)
                                vt_free[sb] = Rel(p.sV, p.sV.n)
                                pready = Rel(p.sV, p.sV.n)
                            else:
                                if pb_free[sb3] is not None:
                                    pb_free[sb3].wait(a)
                                p.sA.inc(a.activation(PB[:, sb3 * 1024:(sb3 + 1) * 1024],
                                                      scsl, AF.Exp,
                                                      bias=p.c_zero[:, 0:1], scale=0.125))
                                act_read[sb] = Rel(p.sA, p.sA.n)
                                pready = Rel(p.sA, p.sA.n)
                            vsl = Vsb[:, kc * 130 + hh * 65: kc * 130 + hh * 65 + 65]
                            pend.append((vsl, sb3, q0, kc == 0, kc == 31, pready))
                            if len(pend) > 3:
                                _vsl, _sb3, _q0, _st, _sp, _pr = pend.pop(0)
                                _pr.wait(t)
                                for nb in range(2):
                                    mm = t.matmul(
                                        psB[0:65, _q0 + nb * 512: _q0 + (nb + 1) * 512],
                                        _vsl,
                                        PB[:, _sb3 * 1024 + nb * 512: _sb3 * 1024 + (nb + 1) * 512],
                                        start=_st, stop=_sp, skip_group_check=True)
                                    if nb == 1:
                                        p.sT.inc(mm)
                                pb_free[_sb3] = Rel(p.sT, p.sT.n)
                    while pend:
                        _vsl, _sb3, _q0, _st, _sp, _pr = pend.pop(0)
                        _pr.wait(t)
                        for nb in range(2):
                            mm = t.matmul(
                                psB[0:65, _q0 + nb * 512: _q0 + (nb + 1) * 512], _vsl,
                                PB[:, _sb3 * 1024 + nb * 512: _sb3 * 1024 + (nb + 1) * 512],
                                start=_st, stop=_sp, skip_group_check=True)
                            if nb == 1:
                                p.sT.inc(mm)
                        pb_free[_sb3] = Rel(p.sT, p.sT.n)
                    p.wall(v, p.sT)
                    p.sV.inc(v.tensor_copy(A_sb[rb:rb + 64, hp * TOK:(hp + 1) * TOK],
                                           psB[0:64, :]))
                    rsrow = p.SR[(R_V if hh == 0 else R_M):(R_V if hh == 0 else R_M) + 1, :]
                    p.sV.inc(v.tensor_copy(rsrow, psB[64:65, :]))
                    p.wall(t, p.sV)
                p.sV.inc(v.reciprocal(p.SR[R_A:R_A + 1, :], p.SR[R_V:R_V + 1, :]))
                p.sV.inc(v.reciprocal(p.SR[R_B:R_B + 1, :], p.SR[R_M:R_M + 1, :]))
            normalize_pair(NPAIR - 1)
            p.release_slot_sems(2)
            if not l0:
                p.wall(a, p.sV)
                for hp in range(NPAIR):
                    p.sA.inc(a.activation(A_sb[:, hp * TOK:(hp + 1) * TOK],
                                          A_sb[:, hp * TOK:(hp + 1) * TOK],
                                          AF.Gelu, bias=p.c_zero[:, 0:1], scale=1.0))

        out_dram = p.dram["x_dram"] if l0 else p.dram["outT"]
        xin_dram = p.dram["xT"] if l0 else p.dram["x_dram"]
        with ExitStack() as ep:
            WO = ep.enter_context(nc.sbuf_tensor(p.un("WOs"), [128, 2 * D], BF16))
            XB = ep.enter_context(nc.sbuf_tensor(p.un("epxb"), [128, 2 * TOK], F32))
            T2 = ep.enter_context(nc.sbuf_tensor(p.un("ept2"), [128, TOK], F32))
            psW = ep.enter_context(nc.psum_tensor(p.un("psW"), [128, TOK], F32))
            wo_rel = [None, None]
            wo_load = [None, None]
            wosem = p.take_slot_sems(2)

            def load_wo(oc):
                slot = oc % 2
                if wo_rel[slot] is not None:
                    wo_rel[slot].wait(g)
                wosem[slot].inc(g.dma_start(out=WO[:, slot * D:(slot + 1) * D],
                                            in_=wo[:, oc * D:(oc + 1) * D]), 16)
                wo_load[slot] = Rel(wosem[slot], wosem[slot].n)

            dma_fence(p)
            load_wo(0)
            for oc in range(CH):
                slot = oc % 2
                wo_load[slot].wait(t)
                if oc + 1 < CH:
                    load_wo(oc + 1)
                xsl = XB[:, slot * TOK:(slot + 1) * TOK]
                p.wall(s, p.sV)
                p.dS.inc(s.dma_start(out=xsl, in_=xin_dram[:, oc * TOK:(oc + 1) * TOK]), 16)
                p.wall(t, p.sV, p.sA)
                for nb in range(4):
                    for ic in range(CH):
                        p.sT.inc(t.matmul(
                            psW[:, nb * 512:(nb + 1) * 512],
                            WO[:, slot * D + ic * 128: slot * D + (ic + 1) * 128],
                            A_sb[:, ic * TOK + nb * 512: ic * TOK + (nb + 1) * 512],
                            start=(ic == 0), stop=(ic == CH - 1),
                            skip_group_check=True))
                wo_rel[slot] = Rel(p.sT, p.sT.n)
                p.wall(v, p.sT, p.dS)
                if l0:
                    p.sV.inc(v.scalar_tensor_tensor(p.TMP[:], psW[:], p.b0o[:, oc:oc + 1],
                                                    p.G[:, oc * TOK:(oc + 1) * TOK],
                                                    ALU.add, ALU.mult))
                    p.sV.inc(v.tensor_add(T2[:], xsl, p.TMP[:]))
                else:
                    p.sV.inc(v.tensor_tensor(p.TMP[:], psW[:],
                                             p.XN[:, oc * SEQ: oc * SEQ + TOK], ALU.add))
                    p.wall(a, p.sV)
                    p.sA.inc(a.activation(T2[:], p.G[:, oc * TOK:(oc + 1) * TOK],
                                          AF.Silu, bias=p.c_zero[:, 0:1], scale=1.0))
                    p.wall(v, p.sA)
                    p.sV.inc(v.tensor_mul(p.TMP[:], p.TMP[:], T2[:]))
                    p.sV.inc(v.tensor_add(T2[:], xsl, p.TMP[:]))
                p.wall(s, p.sV)
                p.dO.inc(s.dma_start(out=out_dram[:, oc * TOK:(oc + 1) * TOK], in_=T2[:]), 16)
                p.wall(v, p.dO)
            p.release_slot_sems(2)



def gating_moe(p, cc_trigger=None):
    nc = p.nc
    v, s, g, a, t = nc.vector, nc.sync, nc.gpsimd, nc.scalar, nc.tensor
    d = p.dram

    with ExitStack() as ph:
        GT = ph.enter_context(nc.sbuf_tensor(p.un("GT"), [128, CH * NE], BF16))
        LG = ph.enter_context(nc.sbuf_tensor(p.un("LG"), [8, TOK], F32))
        LGt = ph.enter_context(nc.sbuf_tensor(p.un("LGt"), [128, 128], F32))
        MX = ph.enter_context(nc.sbuf_tensor(p.un("MX"), [128, 16], F32))
        RW = ph.enter_context(nc.sbuf_tensor(p.un("RW"), [128, 16], F32))
        TS = ph.enter_context(nc.sbuf_tensor(p.un("TS"), [128, 128], F32))
        dma_fence(p)
        p.dG.inc(g.dma_start(out=GT[:], in_=d["gateT"][:]), 16)
        gt_rel = Rel(p.dG, p.dG.n)

        with nc.psum_tensor(p.un("psG"), [32, TOK], F32) as psG:
            gt_rel.wait(t)
            p.wall(t, p.sV)
            for nb in range(4):
                for ic in range(CH):
                    p.sT.inc(t.matmul(psG[0:8, nb * 512:(nb + 1) * 512],
                                      GT[:, ic * 8:(ic + 1) * 8],
                                      p.XN[:, ic * SEQ + nb * 512: ic * SEQ + (nb + 1) * 512],
                                      start=(ic == 0), stop=(ic == CH - 1),
                                      skip_group_check=True))
            p.wall(v, p.sT)
            p.sV.inc(v.tensor_copy(LG[:], psG[0:8, :]))
        with nc.psum_tensor(p.un("psLt"), [128, 128], F32) as psLt:
            p.wall(t, p.sV)
            for tc in range(16):
                p.sT.inc(t.transpose(psLt[:, tc * 8:(tc + 1) * 8],
                                     LG[0:8, tc * 128:(tc + 1) * 128],
                                     p.eyef[0:8, 0:8]))
            p.wall(v, p.sT)
            p.sV.inc(v.tensor_copy(LGt[:], psLt[:]))
        for tc in range(16):
            p.sV.inc(v.reduce_max(MX[:, tc:tc + 1], LGt[:, tc * 8:(tc + 1) * 8], axis=AX.X))
        v.drain()
        p.wall(a, p.sV)
        p.sA.inc(a.activation(RW[:], MX[:], AF.Sigmoid, bias=p.c_zero[:, 0:1], scale=1.0))
        p.wall(v, p.sA)
        v.drain()
        for tc in range(16):
            p.sV.inc(v.tensor_scalar(TS[:, tc * 8:(tc + 1) * 8], LGt[:, tc * 8:(tc + 1) * 8],
                                     MX[:, tc:tc + 1], RW[:, tc:tc + 1],
                                     ALU.is_equal, ALU.mult))

        p.wall(v, p.sA)
        v.memset(p.G[:], 0.0)
        p.sV.inc(v.engine_nop())
        XS = ph.enter_context(nc.sbuf_tensor(p.un("XS"), [128, CH * TOK], BF16))
        PRD = ph.enter_context(nc.sbuf_tensor(p.un("PRD"), [128, TOK], BF16))
        WG = ph.enter_context(nc.sbuf_tensor(p.un("WGs"), [128, 2 * 4096], BF16))
        WU = ph.enter_context(nc.sbuf_tensor(p.un("WUs"), [128, 2 * 4096], BF16))
        WD = ph.enter_context(nc.sbuf_tensor(p.un("WDs"), [128, 2 * 4096], BF16))
        we_rel = [None, None]
        we_load = [None, None]
        wesem = p.take_slot_sems(2)

        def load_we(e):
            slot = e % 2
            if we_rel[slot] is not None:
                we_rel[slot].wait(g)
            for wdram, wbuf in ((d["wgT"], WG), (d["wuT"], WU), (d["wdT"], WD)):
                wesem[slot].inc(g.dma_start(out=wbuf[:, slot * 4096:(slot + 1) * 4096],
                                            in_=wdram[e]), 16)
            we_load[slot] = Rel(wesem[slot], wesem[slot].n)

        dma_fence(p)
        load_we(0)
        for e in range(NE):
            slot = e % 2
            we_load[slot].wait(t)
            if e + 1 < NE:
                load_we(e + 1)
            if e == 0 and cc_trigger is not None:
                cc_trigger()
            with nc.psum_tensor(p.un("pRow"), [1, TOK], F32) as pRow:
                p.wall(t, p.sV)
                for tc in range(16):
                    p.sT.inc(t.transpose(pRow[0:1, tc * 128:(tc + 1) * 128],
                                         TS[:, tc * 8 + e: tc * 8 + e + 1], p.eyef[:]))
                p.wall(v, p.sT)
                p.sV.inc(v.tensor_copy(p.SR[R_A:R_A + 1, :], pRow[0:1, :]))
            with nc.psum_tensor(p.un("pB2"), [128, TOK], F32) as pB2:
                p.wall(t, p.sV)
                bcast_row(p, pB2, p.SR[R_A:R_A + 1, :], TOK)
                p.wall(v, p.sT)
                for c in range(CH):
                    p.sV.inc(v.tensor_tensor(XS[:, c * TOK:(c + 1) * TOK],
                                             p.XN[:, c * SEQ: c * SEQ + TOK],
                                             pB2[:], ALU.mult))
            for tt in range(4):
                t0 = tt * 512
                with nc.psum_tensor(p.un("pGm"), [128, 2048], F32) as pG, \
                     nc.psum_tensor(p.un("pUm"), [128, 2048], F32) as pU:
                    p.wall(t, p.sV, p.sA)
                    for oc in range(4):
                        for ic in range(CH):
                            wof = slot * 4096 + ic * 512 + oc * 128
                            mg = t.matmul(pG[:, oc * 512:(oc + 1) * 512],
                                          WG[:, wof:wof + 128],
                                          XS[:, ic * TOK + t0: ic * TOK + t0 + 512],
                                          start=(ic == 0), stop=(ic == CH - 1),
                                          skip_group_check=True)
                            mu_ = t.matmul(pU[:, oc * 512:(oc + 1) * 512],
                                           WU[:, wof:wof + 128],
                                           XS[:, ic * TOK + t0: ic * TOK + t0 + 512],
                                           start=(ic == 0), stop=(ic == CH - 1),
                                           skip_group_check=True)
                            if ic == CH - 1:
                                p.sT.inc(mg)
                                p.sT.inc(mu_)
                    p.wall(a, p.sT)
                    p.sA.inc(a.activation(p.TMP[:], pG[:], AF.Silu,
                                          bias=p.c_zero[:, 0:1], scale=1.0))
                    p.wall(v, p.sA, p.sT)
                    p.sV.inc(v.scalar_tensor_tensor(PRD[:], pU[:], 1.0, p.TMP[:],
                                                    ALU.bypass, ALU.mult))
                with nc.psum_tensor(p.un("pDm"), [128, 4096], F32) as pD:
                    p.wall(t, p.sV)
                    for oc in range(CH):
                        for ic in range(4):
                            wof = slot * 4096 + ic * D + oc * 128
                            mm = t.matmul(pD[:, oc * 512:(oc + 1) * 512],
                                          WD[:, wof:wof + 128],
                                          PRD[:, ic * 512:(ic + 1) * 512],
                                          start=(ic == 0), stop=(ic == 3),
                                          skip_group_check=True)
                            if ic == 3:
                                p.sT.inc(mm)

                    p.wall(v, p.sT)
                    gview = p.G[:].rearrange("q (c tk) -> q c tk", c=CH)[:, :, t0:t0 + 512]
                    dview = pD[:].rearrange("q (c tk) -> q c tk", c=CH)
                    p.sV.inc(v.tensor_tensor(gview, gview, dview, ALU.add))
            we_rel[slot] = Rel(p.sT, p.sT.n)


class StopEmission(Exception):
    pass


def _ck(p, name):
    if STOP_AFTER and name == STOP_AFTER:
        raise StopEmission()


def layer(p, lid):
    ln_phase(p, lid, inner=False)
    tap(p, f"xn{lid}", p.XN[:, 0:TOK], (p.sV,))
    _ck(p, f"ln{lid}")
    cc_trigger = exchange_export(p, lid)
    _ck(p, f"exp{lid}")
    if lid == 0:
        ffn0(p, cc_trigger)
    else:
        gating_moe(p, cc_trigger)
    tap(p, f"g{lid}", p.G[:, 0:TOK], (p.sV, p.sA))
    _ck(p, f"ffn{lid}")
    exchange_import(p, lid)
    _ck(p, f"imp{lid}")
    if lid == 0:
        ln_phase(p, lid, inner=True)
        tap(p, "xni0", p.XN[:, 0:TOK], (p.sV,))
        _ck(p, "lni0")
    attention(p, lid)
    if lid == 0:
        tap(p, "x1", p.dram["x_dram"][:, 0:TOK], (p.dO,))
    _ck(p, f"att{lid}")


def finale(p):
    nc = p.nc
    p.wall(nc.sync, p.dO)
    p.wall(nc.sync, p.dS)



def _lay_cm(w):
    r, n = w.shape
    ch = r // 128
    return np.ascontiguousarray(w.reshape(ch, 128, n).transpose(1, 0, 2).reshape(128, ch * n))


def _lay_pm(w_io):
    b = w_io.reshape(8, 128, 8, 128)
    return np.ascontiguousarray(b.transpose(1, 2, 0, 3).reshape(128, 8192))


def _lay_col(vec):
    return np.ascontiguousarray(vec.reshape(8, 128).T)


def _build_rails(rm, rup):
    ytiles = np.zeros((32, 128, 128), np.float32)
    for h in range(4):
        for j in range(8):
            tl = ytiles[h * 8 + j]
            for a_ in range(8):
                r = 8 * j + a_
                tl[16 * a_:16 * a_ + 16, 16 * a_:16 * a_ + 16] = rm[h, r]
    ztiles = np.zeros((128, 128, 128), np.float32)
    for ci in range(128):
        h = ci // 32
        rr = ci % 32
        tl = ztiles[ci]
        for l_ in range(2):
            r = 2 * rr + l_
            tl[16 * (r % 8):16 * (r % 8) + 16, 64 * l_:64 * l_ + 64] = rup[h, r].T
    layY = np.ascontiguousarray(ytiles.transpose(1, 0, 2).reshape(128, 32 * 128))
    layZ = np.ascontiguousarray(ztiles.transpose(1, 0, 2).reshape(128, 128 * 128))
    return layY, layZ


def kernel(**inputs):
    global LAST_RESULTS
    f = lambda k: np.asarray(inputs[k], dtype=np.float32)

    x = f("L0_attn_sequence")
    shared = {
        "w0qT": _lay_pm(np.ascontiguousarray(f("L0_attn_Wq").T)),
        "w0kT": _lay_pm(np.ascontiguousarray(f("L0_attn_Wk").T)),
        "w0vT": _lay_pm(np.ascontiguousarray(f("L0_attn_Wv").T)),
        "w0oT": _lay_pm(np.ascontiguousarray(f("L0_attn_Wo").T)),
        "b0q": _lay_col(f("L0_attn_bq")),
        "b0k": _lay_col(f("L0_attn_bk")),
        "b0v": _lay_col(f("L0_attn_bv")),
        "b0o": _lay_col(f("L0_attn_bo")),
        "gam": _lay_col(f("L0_attn_ln_gamma")),
        "bet": _lay_col(f("L0_attn_ln_beta")),
        "mu": f("L0_attn_mu").reshape(1, 1),
        "cprojT": _lay_cm(np.ascontiguousarray(f("L0_ffn_w_cproj").T)),
        "w1qT": _lay_pm(f("L1_attn_Wq")),
        "w1kT": _lay_pm(f("L1_attn_Wk")),
        "w1vT": _lay_pm(f("L1_attn_Wv")),
        "w1postT": _lay_pm(f("L1_attn_Wpost")),
        "gateT": _lay_cm(np.ascontiguousarray(f("L1_ffn_gate_w").T)),
        "wgT": np.stack([_lay_cm(np.ascontiguousarray(f("L1_ffn_w_gate")[e].T))
                         for e in range(NE)]),
        "wuT": np.stack([_lay_cm(np.ascontiguousarray(f("L1_ffn_w_up")[e].T))
                         for e in range(NE)]),
        "wdT": np.stack([_lay_cm(np.ascontiguousarray(f("L1_ffn_w_down")[e].T))
                         for e in range(NE)]),
        "eye": np.eye(128, dtype=np.float32),
    }
    shared["railsY"], shared["railsZ"] = _build_rails(
        f("L0_ffn_rail_maker"), f("L0_ffn_rail_up_proj"))

    nc = build_program(taps=DEBUG_TAPS, emit_upto=EMIT_UPTO)

    in_maps = []
    for c in range(8):
        b, par = c // 2, c % 2
        xs = np.ascontiguousarray(x[b, par * TOK:(par + 1) * TOK, :].T)
        m = dict(shared)
        m["xT"] = _lay_cm(xs)
        m["mlo"] = np.full((128, 1), float(par), np.float32)
        m["mhi"] = np.full((128, 1), float(1 - par), np.float32)
        in_maps.append(m)

    res = run_bass_kernel_spmd(nc, in_maps, core_ids=list(range(8)))
    LAST_RESULTS = res.results

    out = np.zeros((4, SEQ, D), np.float32)
    for c in range(8):
        b, par = c // 2, c % 2
        o = res.results[c]["outT"]
        o = o.reshape(128, CH, TOK).transpose(1, 0, 2).reshape(D, TOK).T
        out[b, par * TOK:(par + 1) * TOK, :] = o
    return out
